# version 03
# baseline (speedup 1.0000x reference)
import numpy as np
import ml_dtypes
import concourse.bass as bass
import concourse.mybir as mybir
from concourse.bass_utils import run_bass_kernel_spmd

F32 = mybir.dt.float32
BF16 = mybir.dt.bfloat16
AF = mybir.ActivationFunctionType
ALU = mybir.AluOpType
AX = mybir.AxisListType
BF = ml_dtypes.bfloat16

NCORES = 8
B, TG, D = 2, 4096, 2048
TL = 1024
NCH = 16
HP = 16
CK = 128
NCK = TL // CK
SL = 512
NTC = 8
EPS_GN = 0.00064
EPS_LN = 1e-5
EPS_RMS = float(np.finfo(np.float32).eps)
KB = 1024
ENGS = ("T", "V", "A", "S", "G")



class Op:
    __slots__ = ("fn", "reads", "writes", "dma", "tag", "idx", "needed", "cnt",
                 "sync", "sync_before")

    def __init__(self, fn, reads, writes, dma, tag, sync=False, sync_before=False):
        self.fn = fn
        self.reads = reads
        self.writes = writes
        self.dma = dma
        self.tag = tag
        self.idx = -1
        self.needed = False
        self.cnt = 0
        self.sync = sync
        self.sync_before = sync_before


def _overlap(a, b):
    return a[0] < b[2] and b[0] < a[2] and a[1] < b[3] and b[1] < a[3]


class Sched:

    def __init__(self):
        self.ops = {e: [] for e in ENGS}
        self.writers = {}
        self.readers = {}
        self.deps = {}
        self.arena = {}
        self.psum = set()

    def reg_arena(self, handle, byte_lo, byte_hi, esz):
        self.arena[id(handle)] = (byte_lo, byte_hi, esz)

    def reg_psum(self, handle):
        self.psum.add(id(handle))

    def _norm(self, acc):
        h, p_lo, c_lo, p_hi, c_hi = acc
        hid = id(h)
        if hid in self.psum:
            return ("PSUM", hid, 0, 0, 128, 1 << 30)
        if hid in self.arena:
            lo, hi, esz = self.arena[hid]
            b_lo = min(lo + c_lo * esz, hi)
            b_hi = min(lo + (c_hi * esz if c_hi < (1 << 29) else hi), hi)
            return ("ARENA", "ARENA", p_lo, b_lo, p_hi, b_hi)
        return ("H", hid, p_lo, c_lo, p_hi, c_hi)

    def add(self, eng, fn, reads=(), writes=(), dma=False, tag="", sync=False,
            sync_before=False):
        op = Op(fn, [self._norm(a) for a in reads],
                [self._norm(a) for a in writes], dma, tag, sync, sync_before)
        op.idx = len(self.ops[eng])
        self.ops[eng].append(op)
        key = (eng, op.idx)
        dep = set()
        for acc in op.reads:
            for (a2, e2, i2) in self.writers.get(acc[1], ()):
                if _overlap(acc[2:], a2[2:]):
                    dep.add((e2, i2))
        for acc in op.writes:
            for (a2, e2, i2) in self.writers.get(acc[1], ()):
                if _overlap(acc[2:], a2[2:]):
                    dep.add((e2, i2))
            for (a2, e2, i2) in self.readers.get(acc[1], ()):
                if _overlap(acc[2:], a2[2:]):
                    dep.add((e2, i2))
        for acc in op.reads:
            self.readers.setdefault(acc[1], []).append((acc, eng, op.idx))
        for acc in op.writes:
            self.writers.setdefault(acc[1], []).append((acc, eng, op.idx))
        self.deps[key] = dep
        return key

    def finalize(self, blk, sems, engines_of):
        for e in ENGS:
            for op in self.ops[e]:
                if op.dma:
                    op.needed = True
                for (e2, i2) in self.deps[(e, op.idx)]:
                    p = self.ops[e2][i2]
                    if e2 != e or p.dma or p.sync or op.sync_before:
                        p.needed = True
        for e in ENGS:
            c = 0
            for op in self.ops[e]:
                if op.needed:
                    c += 16 if op.dma else 1
                op.cnt = c
        def emitter(e):
            def run(eng):
                hw = {e2: 0 for e2 in ENGS}
                prev_cnt = 0
                for op in self.ops[e]:
                    waits = {}
                    for (e2, i2) in self.deps[(e, op.idx)]:
                        p = self.ops[e2][i2]
                        if e2 == e and not (p.dma or p.sync or op.sync_before):
                            continue
                        waits[e2] = max(waits.get(e2, 0), p.cnt)
                    if op.dma:
                        waits[e] = max(waits.get(e, 0), prev_cnt)
                    for e2, v in sorted(waits.items()):
                        if v > hw[e2]:
                            eng.wait_ge(sems[e2], v)
                            hw[e2] = v
                    instr = op.fn(eng)
                    if op.needed:
                        instr.then_inc(sems[e], 16 if op.dma else 1)
                    prev_cnt = op.cnt
            return run

        blk.tensor(emitter("T"))
        blk.vector(emitter("V"))
        blk.scalar(emitter("A"))
        blk.sync(emitter("S"))
        blk.gpsimd(emitter("G"))
        return {e: (self.ops[e][-1].cnt if self.ops[e] else 0) for e in ENGS}


def A(h, p_lo=0, c_lo=0, p_hi=128, c_hi=1 << 30):
    return (h, p_lo, c_lo, p_hi, c_hi)



def build_program(debug_taps=()):
    nc = bass.Bass("TRN2", target_bir_lowering=False)

    def din(name, shape, dt=F32):
        return nc.dram_tensor(name, list(shape), dt, kind="ExternalInput").ap()

    x_blk = din("x_blk", [NCH, 128, TL + 1])
    w_blk = {m: din(f"w{m}_blk", [NCH, 128, NCH, 128]) for m in "krvgo"}
    guq_in = din("guq_in", [16, 128, D])
    pjq_in = din("pjq_in", [16, 128, 1024])
    tm_d = {m: din(f"tm{m}_pp", [128, NCH]) for m in "kvrg"}
    decay_d = din("decay_pp", [128, NCH])
    u_d = din("u_pp", [128, NCH])
    wkvbd_d = din("wkvbd", [128, HP * 128])
    gam_d = din("gam_rep", [128, D])
    bet_d = din("bet_rep", [128, D])
    iota_f_d = din("iota_fwd", [128, TL])
    iota_r_d = din("iota_rev", [128, TL])
    mask2_d = din("mask2", [128, 256], BF16)
    id_bf_d = din("id_bf", [128, 128], BF16)
    id_f32_d = din("id_f32", [128, 128])
    onesbd2_d = din("ones_bd2", [128, 2], BF16)
    onescol_d = din("ones_col", [128, 1], BF16)
    onesrow_d = din("ones_row", [1, 128])
    pmask_d = din("pmask_pp", [128, 8])
    x_tok_d = din("x_tok", [TL, D])

    out_d = nc.dram_tensor("out", [TL, D], F32, kind="ExternalOutput").ap()

    guq_c = nc.dram_tensor("guq_c", [16, 128, D], BF16, kind="Internal")
    pjq_c = nc.dram_tensor("pjq_c", [16, 128, 1024], BF16, kind="Internal")
    guq_g = nc.dram_tensor("guq_g", [NCORES, 16, 128, D], BF16, kind="Internal")
    pjq_g = nc.dram_tensor("pjq_g", [NCORES, 16, 128, 1024], BF16, kind="Internal")
    sig_c = nc.dram_tensor("sig_c", [128, HP * 64], F32, kind="Internal")
    sig_g = nc.dram_tensor("sig_g", [NCORES, 128, HP * 64], F32, kind="Internal")

    taps = {}

    def tapd(name, shape, dt=F32):
        if name in debug_taps:
            taps[name] = nc.dram_tensor("tap_" + name, list(shape), dt,
                                        kind="ExternalOutput").ap()
        return taps.get(name)

    Sal = nc.alloc_sbuf_tensor
    iota = Sal("iota", [128, TL], F32)
    wkvbd = Sal("wkvbd_sb", [128, HP * 128], BF16)
    lnw = Sal("lnw", [128, NCH], F32)
    dectmp = Sal("dectmp", [128, NCH], F32)
    u_sb = Sal("u_sb", [128, NCH], F32)
    tm_sb = {m: Sal(f"tm_{m}", [128, NCH], F32) for m in "kvrg"}
    pmask = Sal("pmask_sb", [128, 8], F32)
    id_bf = Sal("id_bf_sb", [128, 128], BF16)
    id_f32 = Sal("id_f32_sb", [128, 128], F32)
    mask2 = Sal("mask2_sb", [128, 256], BF16)
    onesbd2 = Sal("onesbd2_sb", [128, 2], BF16)
    onescol = Sal("onescol_sb", [128, 1], BF16)
    onesrow = Sal("onesrow_sb", [1, 128], F32)
    S_f32 = Sal("S_f32", [128, HP * 128], F32)
    S_bf = [Sal(f"S_bf{i}", [128, 128], BF16) for i in range(2)]
    bon = Sal("bon", [128, 256], F32)
    qsc = Sal("qsc", [128, 32], F32)
    qscb = Sal("qscb", [128, 32], BF16)
    qrc = Sal("qrc", [128, 32], F32)
    gstat = Sal("gstat", [128, 96], F32)
    statln = Sal("statln", [128, 8], F32)
    rw_t = [Sal(f"rw_t{i}", [128, 128], BF16) for i in range(2)]
    d_t = Sal("d_t", [128, SL], F32)
    vgT_t = Sal("vgT_t", [128, TL], BF16)
    wfor_t = [Sal(f"wfor_t{i}", [128, 128], BF16) for i in range(2)]

    ARENA = 182 * KB
    arena = Sal("arena", [128, ARENA], mybir.dt.uint8)
    abase = nc.lookup_mloc(arena).addr
    sc = Sched()

    def at(name, off_kb, shape, dt):
        off = int(round(off_kb * KB))
        h = nc.alloc_sbuf_tensor_at(name, list(shape), dt, offset=abase + off)
        esz = mybir.dt.size(dt)
        sc.reg_arena(h, off, off + shape[1] * esz, esz)
        return h

    kwT = at("kwT", 0, [128, HP * 1024], BF16)
    rT = at("rT", 32, [128, HP * 1024], BF16)
    x2tok = at("x2tok", 0, [128, NTC * D], F32)
    lerpA = at("lerpA", 64, [128, NCH * 1024], BF16)
    v_tok = at("v_tok", 64, [128, NTC * D], BF16)
    ygT = at("ygT", 64, [128, NCH * 1024], BF16)
    sqd = at("sqd", 64, [128, D], F32)
    x2btok = at("x2btok", 72, [128, D], BF16)
    h_lo = at("h_lo", 64, [128, 32 * SL], BF16)
    g_tok = at("g_tok", 96, [128, NTC * D], BF16)
    qw = [at(f"qw{i}", 96 + 4 * i, [128, D], BF16) for i in range(2)]
    qx = at("qx", 104, [128, D], BF16)
    qa = at("qa", 108, [128, D], BF16)
    qs = at("qs", 112, [128, D], BF16)
    wback = [at(f"wback{i}", 116 + 4 * i, [128, TL], F32) for i in range(2)]
    ku_t = at("ku_t", 124, [128, TL], BF16)
    kr_t = at("kr_t", 126, [128, TL], BF16)
    h_hi = at("h_hi", 96, [128, 32 * SL], BF16)
    lerpB = at("lerpB", 128, [128, NCH * 1024], BF16)
    sig_sb = at("sig_sb", 128, [128, HP * 64], F32)
    wkv_t = [at(f"wkv_t{i}", 128 + 4 * i, [128, D], BF16) for i in range(2)]
    yc_t = at("yc_t", 136, [128, D], F32)
    slab_t = at("slab_t", 144, [128, HP * 64], F32)

    sqc_t = at("sqc_t", 152, [128, D], BF16)
    rbd = [at(f"rbd{i}", 156 + 0.5 * i, [128, 256], BF16) for i in range(2)]
    attT = [at(f"attT{i}", 157 + 0.5 * i, [128, 256], BF16) for i in range(2)]
    kwtok = [at(f"kwtok{i}", 158 + 0.25 * i, [128, 128], BF16) for i in range(2)]
    x2bf = at("x2bf", 128, [128, NCH * 1024], BF16)
    gamb = at("gamb", 160, [128, D], F32)
    betb = at("betb", 168, [128, D], F32)
    ao_tmp = [at(f"ao_tmp{i}", 160 + 2 * i, [128, SL], F32) for i in range(2)]
    sgl_t = at("sgl_t", 160, [128, SL], F32)
    h2_t = at("h2_t", 162, [128, SL], BF16)
    fo_t = at("fo_t", 160, [128, SL], F32)
    inv_row = at("inv_row", 162, [1, SL], F32)
    inv_bc = at("inv_bc", 164, [128, SL], F32)
    xst = [at(f"xst{i}", 160 + 2.25 * i, [128, 513], F32) for i in range(2)]
    wst = [at(f"wst{i}", 165 + 4 * i, [128, D], BF16) for i in range(2)]
    egu = [at(f"egu{i}", 166 + 4 * i, [128, D], BF16) for i in range(4)]
    eph = [at(f"eph{i}", 166 + 8 * i, [128, 4096], BF16) for i in range(2)]

    ps = {
        "a0": nc.alloc_psum_tensor("ps_a0", [128, SL], F32),
        "a1": nc.alloc_psum_tensor("ps_a1", [128, SL], F32),
        "b0": nc.alloc_psum_tensor("ps_b0", [128, SL], F32),
        "b1": nc.alloc_psum_tensor("ps_b1", [128, SL], F32),
        "t0": nc.alloc_psum_tensor("ps_t0", [128, 1024], BF16),
        "t1": nc.alloc_psum_tensor("ps_t1", [128, 1024], BF16),
        "m0": nc.alloc_psum_tensor("ps_m0", [128, SL], F32),
        "m1": nc.alloc_psum_tensor("ps_m1", [128, SL], F32),
    }
    for h in ps.values():
        sc.reg_psum(h)

    def dma(eng, out_ap, in_ap, out_acc=None, in_acc=None, tag=""):
        def f(e, out_ap=out_ap, in_ap=in_ap):
            return e.dma_start(out=out_ap, in_=in_ap)
        sc.add(eng, f, reads=(in_acc or []), writes=(out_acc or []),
               dma=True, tag=tag)

    dma("S", iota[:, :], iota_r_d, out_acc=[A(iota)], tag="ld_iota_r")
    dma("S", lnw[:, :], decay_d, out_acc=[A(lnw)], tag="ld_decay")
    dma("S", u_sb[:, :], u_d, out_acc=[A(u_sb)])
    for m in "kvrg":
        dma("S", tm_sb[m][:, :], tm_d[m], out_acc=[A(tm_sb[m])])
    dma("S", pmask[:, :], pmask_d, out_acc=[A(pmask)])
    dma("S", id_bf[:, :], id_bf_d, out_acc=[A(id_bf)])
    dma("S", id_f32[:, :], id_f32_d, out_acc=[A(id_f32)])
    dma("S", mask2[:, :], mask2_d, out_acc=[A(mask2)])
    dma("S", onesbd2[:, :], onesbd2_d, out_acc=[A(onesbd2)])
    dma("S", onescol[:, :], onescol_d, out_acc=[A(onescol)])
    dma("S", onesrow[:, :], onesrow_d, out_acc=[A(onesrow, 0, 0, 1, 128)])
    dma("G", wkvbd[:, :], wkvbd_d,
        out_acc=[A(wkvbd)], tag="ld_wkvbd")

    sc.add("A", lambda e: e.activation(out=dectmp[:, :], in_=lnw[:, :], func=AF.Exp),
           reads=[A(lnw)], writes=[A(dectmp)])
    sc.add("V", lambda e: e.tensor_scalar_mul(lnw[:, :], dectmp[:, :], -1.0),
           reads=[A(dectmp)], writes=[A(lnw)])

    def emit_lerp(mkey, dest):
        tm = tm_sb[mkey]
        for h2 in range(2):
            for ic in range(NCH):
                i = (h2 * NCH + ic) % 2
                xs = xst[i]
                dma("S", xs[:, :], x_blk[ic, :, h2 * SL:h2 * SL + SL + 1],
                    out_acc=[A(xs)], tag=f"x_{mkey}_{h2}_{ic}")
                sc.add("V", lambda e, xs=xs: e.tensor_sub(
                    d_t[:, :], xs[:, 1:SL + 1], xs[:, 0:SL]),
                    reads=[A(xs)], writes=[A(d_t)])
                c0 = ic * 1024 + h2 * SL
                sc.add("V", lambda e, xs=xs, ic=ic, c0=c0: e.scalar_tensor_tensor(
                    out=dest[:, c0:c0 + SL], in0=d_t[:, :],
                    scalar=tm[:, ic:ic + 1], in1=xs[:, 0:SL],
                    op0=ALU.mult, op1=ALU.add),
                    reads=[A(d_t), A(tm), A(xs)],
                    writes=[A(dest, 0, c0, 128, c0 + SL)])

    def emit_proj(mat, src, evac, accs=("a0", "a1", "b0", "b1")):
        for oc in range(NCH):
            wb = wst[oc % 2]
            dma("G", wb[:, :], w_blk[mat][oc].rearrange("p a b -> p (a b)"),
                out_acc=[A(wb)], tag=f"w{mat}{oc}")
            pp = [ps[accs[2 * (oc % 2)]], ps[accs[2 * (oc % 2) + 1]]]
            for ic in range(NCH):
                st, sp = ic == 0, ic == NCH - 1
                for n in range(2):
                    sc.add("T", lambda e, wb=wb, ic=ic, n=n, st=st, sp=sp, pp=pp: e.matmul(
                        pp[n][:, :], wb[:, ic * 128:(ic + 1) * 128],
                        src[:, ic * 1024 + n * SL: ic * 1024 + (n + 1) * SL],
                        start=st, stop=sp),
                        reads=[A(wb, 0, ic * 128, 128, (ic + 1) * 128),
                               A(src, 0, ic * 1024 + n * SL, 128, ic * 1024 + (n + 1) * SL)],
                        writes=[A(pp[n])], tag=f"{mat}mm{oc}_{ic}_{n}")
            evac(oc, pp)

    emit_lerp("r", lerpA)

    def r_evac(oc, pp):
        for n in range(2):
            c0 = oc * 1024 + n * SL
            sc.add("V", lambda e, pp=pp, n=n, c0=c0: e.tensor_copy(
                out=rT[:, c0:c0 + SL], in_=pp[n][:, :]),
                reads=[A(pp[n])], writes=[A(rT, 0, c0, 128, c0 + SL)])

    emit_proj("r", lerpA, r_evac)

    emit_lerp("k", lerpA)

    def k_evac(oc, pp):
        wbk = wback[oc % 2]
        sc.add("A", lambda e, wbk=wbk, oc=oc: e.activation(
            out=wbk[:, :], in_=iota[:, :], func=AF.Exp, scale=lnw[:, oc:oc + 1]),
            reads=[A(iota), A(lnw, 0, oc, 128, oc + 1)], writes=[A(wbk)])
        for n in range(2):
            c0 = oc * 1024 + n * SL
            sc.add("V", lambda e, pp=pp, n=n, c0=c0, wbk=wbk: e.tensor_tensor(
                out=kwT[:, c0:c0 + SL], in0=pp[n][:, :],
                in1=wbk[:, n * SL:(n + 1) * SL], op=ALU.mult),
                reads=[A(pp[n]), A(wbk, 0, n * SL, 128, (n + 1) * SL)],
                writes=[A(kwT, 0, c0, 128, c0 + SL)])
            sc.add("V", lambda e, pp=pp, n=n, oc=oc: e.tensor_scalar_mul(
                ku_t[:, n * SL:(n + 1) * SL], pp[n][:, :], u_sb[:, oc:oc + 1]),
                reads=[A(pp[n]), A(u_sb, 0, oc, 128, oc + 1)],
                writes=[A(ku_t, 0, n * SL, 128, (n + 1) * SL)])
        sc.add("V", lambda e, oc=oc: e.tensor_mul(
            kr_t[:, :], ku_t[:, :], rT[:, oc * 1024:(oc + 1) * 1024]),
            reads=[A(ku_t), A(rT, 0, oc * 1024, 128, (oc + 1) * 1024)],
            writes=[A(kr_t)])
        for ck in range(NCK):
            sc.add("T", lambda e, ck=ck: e.matmul(
                ps["m0"][:, ck * 2:(ck + 1) * 2], kr_t[:, ck * 128:(ck + 1) * 128],
                onesbd2[:, :], start=True, stop=True),
                reads=[A(kr_t, 0, ck * 128, 128, (ck + 1) * 128), A(onesbd2)],
                writes=[A(ps["m0"], 0, ck * 2, 128, (ck + 1) * 2)],
                tag=f"bon{oc}_{ck}")
        sc.add("V", lambda e, oc=oc: e.tensor_copy(
            out=bon[:, oc * 16:(oc + 1) * 16], in_=ps["m0"][:, 0:16]),
            reads=[A(ps["m0"], 0, 0, 128, 16)],
            writes=[A(bon, 0, oc * 16, 128, (oc + 1) * 16)])

    emit_proj("k", lerpA, k_evac)

    for t in range(32):
        isgu = t < 16
        W = D if isgu else 1024
        ngrp = W // 64
        ntr = W // 128
        qwb = qw[t % 2]
        dma("G", qwb[:, 0:W], (guq_in[t] if isgu else pjq_in[t - 16]),
            out_acc=[A(qwb, 0, 0, 128, W)], tag=f"q_in{t}")
        a3 = qwb.ap()[:, 0:W].rearrange("p (g e) -> p g e", e=64)
        sc.add("V", lambda e, a3=a3, ngrp=ngrp: e.tensor_reduce(
            qsc[:, 0:ngrp], a3, AX.X, ALU.add, apply_absolute_value=True),
            reads=[A(qwb, 0, 0, 128, W)], writes=[A(qsc)], sync=True, sync_before=True)
        sc.add("G", lambda e, ngrp=ngrp: e.tensor_scalar(
            out=qscb[:, 0:ngrp], in0=qsc[:, 0:ngrp], scalar1=1.0 / 64,
            scalar2=1e-8, op0=ALU.mult, op1=ALU.max),
            reads=[A(qsc)], writes=[A(qscb)])
        sc.add("V", lambda e, ngrp=ngrp: e.reciprocal(qrc[:, 0:ngrp], qscb[:, 0:ngrp]),
            reads=[A(qscb)], writes=[A(qrc)], sync=True, sync_before=True)
        rbc = qrc.ap()[:, 0:ngrp].unsqueeze(2).broadcast_to([128, ngrp, 64])
        sbc = qscb.ap()[:, 0:ngrp].unsqueeze(2).broadcast_to([128, ngrp, 64])
        qx3 = qx.ap()[:, 0:W].rearrange("p (g e) -> p g e", e=64)
        qa3 = qa.ap()[:, 0:W].rearrange("p (g e) -> p g e", e=64)
        qw3 = qwb.ap()[:, 0:W].rearrange("p (g e) -> p g e", e=64)
        sc.add("G", lambda e, qx3=qx3, qw3=qw3, rbc=rbc: e.tensor_tensor(
            out=qx3, in0=qw3, in1=rbc, op=ALU.mult),
            reads=[A(qwb, 0, 0, 128, W), A(qrc)], writes=[A(qx, 0, 0, 128, W)])
        sc.add("V", lambda e, qa3=qa3, qx3=qx3, sbc=sbc: e.scalar_tensor_tensor(
            out=qa3, in0=qx3, scalar=0.5, in1=sbc, op0=ALU.is_gt, op1=ALU.mult),
            reads=[A(qx, 0, 0, 128, W), A(qscb)], writes=[A(qa, 0, 0, 128, W)])
        sc.add("V", lambda e, qx3=qx3, sbc=sbc: e.scalar_tensor_tensor(
            out=qx3, in0=qx3, scalar=-0.5, in1=sbc, op0=ALU.is_lt, op1=ALU.mult),
            reads=[A(qx, 0, 0, 128, W), A(qscb)], writes=[A(qx, 0, 0, 128, W)])
        sc.add("G", lambda e, W=W: e.tensor_sub(
            qa.ap()[:, 0:W], qa.ap()[:, 0:W], qx.ap()[:, 0:W]),
            reads=[A(qa, 0, 0, 128, W), A(qx, 0, 0, 128, W)],
            writes=[A(qa, 0, 0, 128, W)])
        for k in range(ntr):
            pt = ps["t0"] if k % 2 == 0 else ps["t1"]
            sc.add("T", lambda e, k=k, pt=pt: e.transpose(
                pt[:, 0:128], qa[:, k * 128:(k + 1) * 128], id_bf[:, :]),
                reads=[A(qa, 0, k * 128, 128, (k + 1) * 128), A(id_bf)],
                writes=[A(pt, 0, 0, 128, 128)], tag=f"qtr{t}_{k}")
            sc.add("A", lambda e, k=k, pt=pt: e.activation(
                out=qs[:, k * 128:(k + 1) * 128], in_=pt[:, 0:128], func=AF.Copy),
                reads=[A(pt, 0, 0, 128, 128)],
                writes=[A(qs, 0, k * 128, 128, (k + 1) * 128)])
        dma("G", (guq_c.ap()[t] if isgu else pjq_c.ap()[t - 16]), qs[:, 0:W],
            in_acc=[A(qs, 0, 0, 128, W)],
            out_acc=[A(guq_c if isgu else pjq_c)], tag=f"q_st{t}")

    def coll1a(e):
        return e.collective_compute(
            "AllGather", ALU.bypass, replica_groups=[list(range(NCORES))],
            ins=[guq_c.ap().opt()], outs=[guq_g.ap().opt()])

    def coll1b(e):
        return e.collective_compute(
            "AllGather", ALU.bypass, replica_groups=[list(range(NCORES))],
            ins=[pjq_c.ap().opt()], outs=[pjq_g.ap().opt()])
    sc.add("G", coll1a, reads=[A(guq_c)], writes=[A(guq_g)], tag="coll1a")
    sc.add("G", coll1b, reads=[A(pjq_c)], writes=[A(pjq_g)], tag="coll1b")

    emit_lerp("v", lerpB)
    if tapd("xv", [128, NCH * 1024], BF16) is not None:
        dma("S", taps["xv"], lerpB[:, :], in_acc=[A(lerpB)])

    def vg_evac(kind):
        def evac(oc, pp):
            for n in range(2):
                if kind == "v":
                    sc.add("A", lambda e, pp=pp, n=n: e.activation(
                        out=vgT_t[:, n * SL:(n + 1) * SL], in_=pp[n][:, :],
                        func=AF.Copy),
                        reads=[A(pp[n])],
                        writes=[A(vgT_t, 0, n * SL, 128, (n + 1) * SL)])
                else:
                    sc.add("A", lambda e, pp=pp, n=n: e.activation(
                        out=vgT_t[:, n * SL:(n + 1) * SL], in_=pp[n][:, :],
                        func=AF.Silu),
                        reads=[A(pp[n])],
                        writes=[A(vgT_t, 0, n * SL, 128, (n + 1) * SL)])
            dst = v_tok if kind == "v" else g_tok
            for tc in range(NTC):
                pt = ps["t0"] if tc % 2 == 0 else ps["t1"]
                sc.add("T", lambda e, tc=tc, pt=pt: e.transpose(
                    pt[:, 0:128], vgT_t[:, tc * 128:(tc + 1) * 128], id_bf[:, :]),
                    reads=[A(vgT_t, 0, tc * 128, 128, (tc + 1) * 128), A(id_bf)],
                    writes=[A(pt, 0, 0, 128, 128)], tag=f"{kind}tr{oc}_{tc}")
                c0 = tc * D + oc * 128
                sc.add("V", lambda e, pt=pt, c0=c0: e.tensor_copy(
                    out=dst[:, c0:c0 + 128], in_=pt[:, 0:128]),
                    reads=[A(pt, 0, 0, 128, 128)],
                    writes=[A(dst, 0, c0, 128, c0 + 128)])
        return evac

    emit_proj("v", lerpB, vg_evac("v"))
    if tapd("vgT", [128, TL], BF16) is not None:
        dma("S", taps["vgT"], vgT_t[:, :], in_acc=[A(vgT_t)])
    emit_lerp("g", lerpB)
    emit_proj("g", lerpB, vg_evac("g"))

    dma("S", gamb[:, :], gam_d, out_acc=[A(gamb)], tag="ld_gam")
    dma("S", betb[:, :], bet_d, out_acc=[A(betb)], tag="ld_bet")

    for hp in range(HP):
        pA_ = ps["m0"]
        pB_ = ps["m1"]
        for ck in range(NCK):
            pt = ps["t0"] if ck % 2 == 0 else ps["t1"]
            kt = kwtok[ck % 2]
            c0 = hp * 1024 + ck * 128
            sc.add("T", lambda e, pt=pt, c0=c0: e.transpose(
                pt[:, 0:128], kwT[:, c0:c0 + 128], id_bf[:, :]),
                reads=[A(kwT, 0, c0, 128, c0 + 128), A(id_bf)],
                writes=[A(pt, 0, 0, 128, 128)], tag=f"sigtr{hp}_{ck}")
            sc.add("A", lambda e, pt=pt, kt=kt: e.activation(
                out=kt[:, :], in_=pt[:, 0:128], func=AF.Copy),
                reads=[A(pt, 0, 0, 128, 128)], writes=[A(kt)])
            vc0 = ck * D + hp * 128
            st, sp = ck == 0, ck == NCK - 1
            sc.add("T", lambda e, kt=kt, vc0=vc0, st=st, sp=sp, pA_=pA_: e.matmul(
                pA_[0:64, 0:64], kt[:, 0:64], v_tok[:, vc0:vc0 + 64],
                start=st, stop=sp),
                reads=[A(kt, 0, 0, 128, 64), A(v_tok, 0, vc0, 128, vc0 + 64)],
                writes=[A(pA_, 0, 0, 64, 64)], tag=f"sigA{hp}_{ck}")
            sc.add("T", lambda e, kt=kt, vc0=vc0, st=st, sp=sp, pB_=pB_: e.matmul(
                pB_[64:128, 64:128], kt[:, 64:128], v_tok[:, vc0 + 64:vc0 + 128],
                start=st, stop=sp),
                reads=[A(kt, 0, 64, 128, 128), A(v_tok, 0, vc0 + 64, 128, vc0 + 128)],
                writes=[A(pB_, 64, 64, 128, 128)], tag=f"sigB{hp}_{ck}")
        sc.add("V", lambda e, hp=hp, pA_=pA_: e.tensor_copy(
            out=sig_sb[0:64, hp * 64:(hp + 1) * 64], in_=pA_[0:64, 0:64]),
            reads=[A(pA_, 0, 0, 64, 64)],
            writes=[A(sig_sb, 0, hp * 64, 64, (hp + 1) * 64)])
        sc.add("V", lambda e, hp=hp, pB_=pB_: e.tensor_copy(
            out=sig_sb[64:128, hp * 64:(hp + 1) * 64], in_=pB_[64:128, 64:128]),
            reads=[A(pB_, 64, 64, 128, 128)],
            writes=[A(sig_sb, 64, hp * 64, 128, (hp + 1) * 64)])
    dma("S", sig_c.ap(), sig_sb[:, :], in_acc=[A(sig_sb)], out_acc=[A(sig_c)],
        tag="sig_store")

    def coll2(e):
        return e.collective_compute(
            "AllGather", ALU.bypass, replica_groups=[list(range(NCORES))],
            ins=[sig_c.ap().opt()], outs=[sig_g.ap().opt()])
    sc.add("G", coll2, reads=[A(sig_c)], writes=[A(sig_g)], tag="coll2")

    if tapd("rT", [128, HP * 1024], BF16) is not None:
        dma("S", taps["rT"], rT[:, :], in_acc=[A(rT)])
    if tapd("kwT", [128, HP * 1024], BF16) is not None:
        dma("S", taps["kwT"], kwT[:, :], in_acc=[A(kwT)])
    if tapd("v_tok", [128, NTC * D], BF16) is not None:
        dma("S", taps["v_tok"], v_tok[:, :], in_acc=[A(v_tok)])
    if tapd("g_tok", [128, NTC * D], BF16) is not None:
        dma("S", taps["g_tok"], g_tok[:, :], in_acc=[A(g_tok)])
    if tapd("bon", [128, 256]) is not None:
        dma("S", taps["bon"], bon[:, :], in_acc=[A(bon)])
    if tapd("sig", [128, HP * 64]) is not None:
        dma("S", taps["sig"], sig_sb[:, :], in_acc=[A(sig_sb)])

    dma("S", iota[:, :], iota_f_d, out_acc=[A(iota)], tag="ld_iota_f")

    sc.add("V", lambda e: e.memset(S_f32[:, :], 0.0), writes=[A(S_f32)])
    sc.add("V", lambda e: e.memset(sig_sb[:, :], 0.0), writes=[A(sig_sb)])
    for r in range(NCORES):
        dma("S", slab_t[:, :], sig_g.ap()[r], in_acc=[A(sig_g)],
            out_acc=[A(slab_t)], tag=f"slab{r}")
        sc.add("V", lambda e, r=r: e.scalar_tensor_tensor(
            out=sig_sb[:, :], in0=slab_t[:, :], scalar=pmask[:, r:r + 1],
            in1=sig_sb[:, :], op0=ALU.mult, op1=ALU.add),
            reads=[A(slab_t), A(pmask), A(sig_sb)], writes=[A(sig_sb)])
    for hp in range(HP):
        sc.add("V", lambda e, hp=hp: e.tensor_copy(
            out=S_f32[0:64, hp * 128:hp * 128 + 64],
            in_=sig_sb[0:64, hp * 64:(hp + 1) * 64]),
            reads=[A(sig_sb, 0, hp * 64, 64, (hp + 1) * 64)],
            writes=[A(S_f32, 0, hp * 128, 64, hp * 128 + 64)])
        sc.add("V", lambda e, hp=hp: e.tensor_copy(
            out=S_f32[64:128, hp * 128 + 64:(hp + 1) * 128],
            in_=sig_sb[64:128, hp * 64:(hp + 1) * 64]),
            reads=[A(sig_sb, 64, hp * 64, 128, (hp + 1) * 64)],
            writes=[A(S_f32, 64, hp * 128 + 64, 128, (hp + 1) * 128)])
    sc.add("V", lambda e: e.memset(rbd[0][:, :], 0.0), writes=[A(rbd[0])])
    sc.add("V", lambda e: e.memset(rbd[1][:, :], 0.0), writes=[A(rbd[1])])
    if tapd("cpref", [128, HP * 128]) is not None:
        dma("S", taps["cpref"], S_f32[:, :], in_acc=[A(S_f32)])

    def scan_front(it):
        ck, hp = it
        p = (ck * HP + hp) % 2
        c0 = hp * 1024 + ck * 128
        sc.add("A", lambda e, p=p, ck=ck, hp=hp: e.activation(
            out=wfor_t[p][:, :], in_=iota[:, ck * 128:(ck + 1) * 128],
            func=AF.Exp, scale=lnw[:, hp:hp + 1]),
            reads=[A(iota, 0, ck * 128, 128, (ck + 1) * 128),
                   A(lnw, 0, hp, 128, hp + 1)],
            writes=[A(wfor_t[p])])
        sc.add("V", lambda e, p=p, c0=c0: e.tensor_mul(
            rw_t[p][:, :], rT[:, c0:c0 + 128], wfor_t[p][:, :]),
            reads=[A(rT, 0, c0, 128, c0 + 128), A(wfor_t[p])],
            writes=[A(rw_t[p])])
        sc.add("V", lambda e, p=p, c0=c0: e.tensor_copy(
            out=rbd[p][0:64, 0:128], in_=rT[0:64, c0:c0 + 128]),
            reads=[A(rT, 0, c0, 64, c0 + 128)], writes=[A(rbd[p], 0, 0, 64, 128)])
        sc.add("V", lambda e, p=p, c0=c0: e.tensor_copy(
            out=rbd[p][64:128, 128:256], in_=rT[64:128, c0:c0 + 128]),
            reads=[A(rT, 64, c0, 128, c0 + 128)],
            writes=[A(rbd[p], 64, 128, 128, 256)])
        sc.add("V", lambda e, p=p, hp=hp: e.tensor_copy(
            out=S_bf[p][:, :], in_=S_f32[:, hp * 128:(hp + 1) * 128]),
            reads=[A(S_f32, 0, hp * 128, 128, (hp + 1) * 128)],
            writes=[A(S_bf[p])])
        pt = ps["t0"] if p == 0 else ps["t1"]
        sc.add("T", lambda e, pt=pt, c0=c0: e.transpose(
            pt[:, 0:128], kwT[:, c0:c0 + 128], id_bf[:, :]),
            reads=[A(kwT, 0, c0, 128, c0 + 128), A(id_bf)],
            writes=[A(pt, 0, 0, 128, 128)], tag=f"ctr{ck}_{hp}")
        pa = ps["a0"] if p == 0 else ps["a1"]
        sc.add("T", lambda e, pa=pa, c0=c0, p=p: e.matmul(
            pa[:, 0:256], kwT[:, c0:c0 + 128], rbd[p][:, :],
            start=True, stop=True),
            reads=[A(kwT, 0, c0, 128, c0 + 128), A(rbd[p])],
            writes=[A(pa, 0, 0, 128, 256)], tag=f"catt{ck}_{hp}")
        sc.add("V", lambda e, pa=pa, p=p: e.tensor_tensor(
            out=attT[p][:, :], in0=pa[:, 0:256], in1=mask2[:, :], op=ALU.mult),
            reads=[A(pa, 0, 0, 128, 256), A(mask2)], writes=[A(attT[p])])
        sc.add("V", lambda e, pt=pt, p=p: e.tensor_copy(
            out=kwtok[p][:, :], in_=pt[:, 0:128]),
            reads=[A(pt, 0, 0, 128, 128)], writes=[A(kwtok[p])])

    def scan_back(it):
        ck, hp = it
        p = (ck * HP + hp) % 2
        c0 = hp * 1024 + ck * 128
        vc0 = ck * D + hp * 128
        po = ps["b0"] if p == 0 else ps["b1"]
        sc.add("T", lambda e, po=po, p=p, hp=hp: e.matmul(
            po[:, 0:128], rw_t[p][:, :], wkvbd[:, hp * 128:(hp + 1) * 128],
            start=True, stop=False),
            reads=[A(rw_t[p]), A(wkvbd, 0, hp * 128, 128, (hp + 1) * 128)],
            writes=[A(po, 0, 0, 128, 128)], tag=f"cbase{ck}_{hp}")
        sc.add("T", lambda e, po=po, p=p, c0=c0: e.matmul(
            po[:, 0:128], rT[:, c0:c0 + 128], S_bf[p][:, :],
            start=False, stop=False),
            reads=[A(rT, 0, c0, 128, c0 + 128), A(S_bf[p])],
            writes=[A(po, 0, 0, 128, 128)])
        sc.add("T", lambda e, po=po, p=p, vc0=vc0: e.matmul(
            po[:, 0:64], attT[p][:, 0:128], v_tok[:, vc0:vc0 + 64],
            start=False, stop=False),
            reads=[A(attT[p], 0, 0, 128, 128), A(v_tok, 0, vc0, 128, vc0 + 64)],
            writes=[A(po, 0, 0, 128, 64)])
        sc.add("T", lambda e, po=po, p=p, vc0=vc0: e.matmul(
            po[:, 64:128], attT[p][:, 128:256], v_tok[:, vc0 + 64:vc0 + 128],
            start=False, stop=True),
            reads=[A(attT[p], 0, 128, 128, 256),
                   A(v_tok, 0, vc0 + 64, 128, vc0 + 128)],
            writes=[A(po, 0, 64, 128, 128)])
        pm = ps["m0"] if p == 0 else ps["m1"]
        sc.add("T", lambda e, pm=pm, p=p, vc0=vc0: e.matmul(
            pm[0:64, 0:64], kwtok[p][:, 0:64], v_tok[:, vc0:vc0 + 64],
            start=True, stop=True),
            reads=[A(kwtok[p], 0, 0, 128, 64), A(v_tok, 0, vc0, 128, vc0 + 64)],
            writes=[A(pm, 0, 0, 64, 64)], tag=f"csup{ck}_{hp}")
        sc.add("T", lambda e, pm=pm, p=p, vc0=vc0: e.matmul(
            pm[64:128, 64:128], kwtok[p][:, 64:128],
            v_tok[:, vc0 + 64:vc0 + 128], start=True, stop=True),
            reads=[A(kwtok[p], 0, 64, 128, 128),
                   A(v_tok, 0, vc0 + 64, 128, vc0 + 128)],
            writes=[A(pm, 64, 64, 128, 128)])
        wt = wkv_t[ck % 2]
        for ee in range(2):
            bc = hp * 16 + ck * 2 + ee
            wc0 = hp * 128 + ee * 64
            sc.add("V", lambda e, po=po, vc0=vc0, ee=ee, bc=bc, wt=wt, wc0=wc0:
                   e.scalar_tensor_tensor(
                       out=wt[:, wc0:wc0 + 64],
                       in0=v_tok[:, vc0 + ee * 64:vc0 + (ee + 1) * 64],
                       scalar=bon[:, bc:bc + 1], in1=po[:, ee * 64:(ee + 1) * 64],
                       op0=ALU.mult, op1=ALU.add),
                   reads=[A(v_tok, 0, vc0 + ee * 64, 128, vc0 + (ee + 1) * 64),
                          A(bon, 0, bc, 128, bc + 1),
                          A(po, 0, ee * 64, 128, (ee + 1) * 64)],
                   writes=[A(wt, 0, wc0, 128, wc0 + 64)])
        sc.add("V", lambda e, pm=pm, hp=hp: e.tensor_add(
            S_f32[0:64, hp * 128:hp * 128 + 64],
            S_f32[0:64, hp * 128:hp * 128 + 64], pm[0:64, 0:64]),
            reads=[A(pm, 0, 0, 64, 64),
                   A(S_f32, 0, hp * 128, 64, hp * 128 + 64)],
            writes=[A(S_f32, 0, hp * 128, 64, hp * 128 + 64)])
        sc.add("V", lambda e, pm=pm, hp=hp: e.tensor_add(
            S_f32[64:128, hp * 128 + 64:(hp + 1) * 128],
            S_f32[64:128, hp * 128 + 64:(hp + 1) * 128], pm[64:128, 64:128]),
            reads=[A(pm, 64, 64, 128, 128),
                   A(S_f32, 64, hp * 128 + 64, 128, (hp + 1) * 128)],
            writes=[A(S_f32, 64, hp * 128 + 64, 128, (hp + 1) * 128)])

    def groupnorm_gate(ck):
        wt = wkv_t[ck % 2]
        g3 = lambda x: x.ap().rearrange("p (g e) -> p g e", e=64)
        bc = lambda x, n: x.ap()[:, 0:n].unsqueeze(2).broadcast_to([128, n, 64])
        sc.add("V", lambda e, wt=wt: e.tensor_reduce(
            gstat[:, 0:32], g3(wt), AX.X, ALU.add),
            reads=[A(wt)], writes=[A(gstat, 0, 0, 128, 32)], sync=True, sync_before=True)
        sc.add("V", lambda e: e.tensor_scalar_mul(
            gstat[:, 32:64], gstat[:, 0:32], 1.0 / 64),
            reads=[A(gstat, 0, 0, 128, 32)], writes=[A(gstat, 0, 32, 128, 64)], sync=True, sync_before=True)
        sc.add("V", lambda e, wt=wt: e.tensor_tensor(
            out=yc_t.ap().rearrange("p (g e) -> p g e", e=64), in0=g3(wt),
            in1=gstat.ap()[:, 32:64].unsqueeze(2).broadcast_to([128, 32, 64]),
            op=ALU.subtract),
            reads=[A(wt), A(gstat, 0, 32, 128, 64)], writes=[A(yc_t)], sync=True, sync_before=True)
        sc.add("A", lambda e: e.activation(out=sqc_t[:, :], in_=yc_t[:, :],
                                           func=AF.Square),
            reads=[A(yc_t)], writes=[A(sqc_t)], sync=True, sync_before=True)
        sc.add("V", lambda e: e.tensor_reduce(
            gstat[:, 0:32], g3(sqc_t), AX.X, ALU.add),
            reads=[A(sqc_t)], writes=[A(gstat, 0, 0, 128, 32)], sync=True, sync_before=True)
        sc.add("V", lambda e: e.tensor_scalar(
            out=gstat[:, 0:32], in0=gstat[:, 0:32], scalar1=1.0 / 64,
            scalar2=EPS_GN, op0=ALU.mult, op1=ALU.add),
            reads=[A(gstat, 0, 0, 128, 32)], writes=[A(gstat, 0, 0, 128, 32)], sync=True, sync_before=True)
        sc.add("V", lambda e: e.reciprocal(gstat[:, 64:96], gstat[:, 0:32]),
            reads=[A(gstat, 0, 0, 128, 32)], writes=[A(gstat, 0, 64, 128, 96)],
            sync=True, sync_before=True)
        sc.add("A", lambda e: e.activation(out=gstat[:, 0:32], in_=gstat[:, 64:96],
                                           func=AF.Sqrt),
            reads=[A(gstat, 0, 64, 128, 96)], writes=[A(gstat, 0, 0, 128, 32)], sync=True, sync_before=True)
        sc.add("V", lambda e: e.tensor_tensor(
            out=yc_t.ap().rearrange("p (g e) -> p g e", e=64),
            in0=yc_t.ap().rearrange("p (g e) -> p g e", e=64),
            in1=gstat.ap()[:, 0:32].unsqueeze(2).broadcast_to([128, 32, 64]),
            op=ALU.mult),
            reads=[A(yc_t), A(gstat, 0, 0, 128, 32)], writes=[A(yc_t)], sync=True, sync_before=True)
        sc.add("V", lambda e: e.tensor_mul(yc_t[:, :], yc_t[:, :], gamb[:, :]),
            reads=[A(yc_t), A(gamb)], writes=[A(yc_t)], sync=True, sync_before=True)
        sc.add("V", lambda e: e.tensor_add(yc_t[:, :], yc_t[:, :], betb[:, :]),
            reads=[A(yc_t), A(betb)], writes=[A(yc_t)], sync=True, sync_before=True)
        c0 = ck * D
        sc.add("V", lambda e, c0=c0: e.tensor_mul(
            g_tok[:, c0:c0 + D], yc_t[:, :], g_tok[:, c0:c0 + D]),
            reads=[A(yc_t), A(g_tok, 0, c0, 128, c0 + D)],
            writes=[A(g_tok, 0, c0, 128, c0 + D)], sync=True, sync_before=True)

    tw = tapd("wkv8", [NCK, 128, D], BF16)
    def maybe_tap_wkv(ck):
        if tw is not None:
            dma("S", tw[ck], wkv_t[ck % 2][:, :], in_acc=[A(wkv_t[ck % 2])])

    its = [(ck, hp) for ck in range(NCK) for hp in range(HP)]
    prev = None
    for it in its:
        scan_front(it)
        if prev is not None:
            scan_back(prev)
            if prev[1] == HP - 1:
                maybe_tap_wkv(prev[0])
                groupnorm_gate(prev[0])
        prev = it
    scan_back(prev)
    maybe_tap_wkv(prev[0])
    groupnorm_gate(prev[0])
    if tapd("sfin", [128, HP * 128]) is not None:
        dma("S", taps["sfin"], S_f32[:, :], in_acc=[A(S_f32)])
    if tapd("gnyc", [128, D]) is not None:
        dma("S", taps["gnyc"], yc_t[:, :], in_acc=[A(yc_t)])
    if tapd("gnstat", [128, 96]) is not None:
        dma("S", taps["gnstat"], gstat[:, :], in_acc=[A(gstat)])

    if tapd("yg", [128, NTC * D], BF16) is not None:
        dma("S", taps["yg"], g_tok[:, :], in_acc=[A(g_tok)])

    for tc in range(NTC):
        for cc in range(NCH):
            pt = ps["t0"] if (tc * NCH + cc) % 2 == 0 else ps["t1"]
            c0 = tc * D + cc * 128
            d0 = cc * 1024 + tc * 128
            sc.add("T", lambda e, pt=pt, c0=c0: e.transpose(
                pt[:, 0:128], g_tok[:, c0:c0 + 128], id_bf[:, :]),
                reads=[A(g_tok, 0, c0, 128, c0 + 128), A(id_bf)],
                writes=[A(pt, 0, 0, 128, 128)], tag=f"ygtr{tc}_{cc}")
            sc.add("V", lambda e, pt=pt, d0=d0: e.tensor_copy(
                out=ygT[:, d0:d0 + 128], in_=pt[:, 0:128]),
                reads=[A(pt, 0, 0, 128, 128)],
                writes=[A(ygT, 0, d0, 128, d0 + 128)])

    for tc in range(NTC):
        dma("S", x2tok[:, tc * D:(tc + 1) * D],
            x_tok_d[tc * 128:(tc + 1) * 128, :],
            out_acc=[A(x2tok, 0, tc * D, 128, (tc + 1) * D)], tag=f"xtok{tc}")

    for oc in range(NCH):
        wb = wst[oc % 2]
        dma("G", wb[:, :], w_blk["o"][oc].rearrange("p a b -> p (a b)"),
            out_acc=[A(wb)], tag=f"wo{oc}")
        pp = [ps["a0"], ps["a1"]] if oc % 2 == 0 else [ps["b0"], ps["b1"]]
        for cc in range(NCH):
            st, sp = cc == 0, cc == NCH - 1
            for n in range(2):
                sc.add("T", lambda e, wb=wb, cc=cc, n=n, st=st, sp=sp, pp=pp: e.matmul(
                    pp[n][:, :], wb[:, cc * 128:(cc + 1) * 128],
                    ygT[:, cc * 1024 + n * SL: cc * 1024 + (n + 1) * SL],
                    start=st, stop=sp),
                    reads=[A(wb, 0, cc * 128, 128, (cc + 1) * 128),
                           A(ygT, 0, cc * 1024 + n * SL, 128, cc * 1024 + (n + 1) * SL)],
                    writes=[A(pp[n])], tag=f"womm{oc}_{cc}_{n}")
        for n in range(2):
            ao = ao_tmp[n]
            sc.add("V", lambda e, pp=pp, n=n, ao=ao: e.tensor_copy(
                out=ao[:, :], in_=pp[n][:, :]),
                reads=[A(pp[n])], writes=[A(ao)])
            for ts in range(4):
                tc = n * 4 + ts
                pm = ps["m0"] if tc % 2 == 0 else ps["m1"]
                sc.add("T", lambda e, ao=ao, ts=ts, pm=pm: e.transpose(
                    pm[:, 0:128], ao[:, ts * 128:(ts + 1) * 128], id_f32[:, :]),
                    reads=[A(ao, 0, ts * 128, 128, (ts + 1) * 128), A(id_f32)],
                    writes=[A(pm, 0, 0, 128, 128)], tag=f"aotr{oc}_{tc}")
                xc0 = tc * D + oc * 128
                sc.add("V", lambda e, pm=pm, xc0=xc0: e.tensor_add(
                    x2tok[:, xc0:xc0 + 128], x2tok[:, xc0:xc0 + 128],
                    pm[:, 0:128]),
                    reads=[A(pm, 0, 0, 128, 128),
                           A(x2tok, 0, xc0, 128, xc0 + 128)],
                    writes=[A(x2tok, 0, xc0, 128, xc0 + 128)])

    if tapd("x1", [128, NTC * D]) is not None:
        dma("S", taps["x1"], x2tok[:, :], in_acc=[A(x2tok)])

    def emit_ln(tc, src_col0, write_bf16):
        c0 = src_col0
        sc.add("V", lambda e, c0=c0: e.tensor_reduce(
            statln[:, 0:1], x2tok[:, c0:c0 + D], AX.X, ALU.add),
            reads=[A(x2tok, 0, c0, 128, c0 + D)],
            writes=[A(statln, 0, 0, 128, 1)], sync=True, sync_before=True)
        sc.add("V", lambda e: e.tensor_scalar_mul(
            statln[:, 1:2], statln[:, 0:1], 1.0 / D),
            reads=[A(statln, 0, 0, 128, 1)], writes=[A(statln, 0, 1, 128, 2)])
        sc.add("A", lambda e, c0=c0: e.activation(
            out=sqd[:, :], in_=x2tok[:, c0:c0 + D], func=AF.Square),
            reads=[A(x2tok, 0, c0, 128, c0 + D)], writes=[A(sqd)])
        sc.add("V", lambda e: e.tensor_reduce(
            statln[:, 2:3], sqd[:, :], AX.X, ALU.add),
            reads=[A(sqd)], writes=[A(statln, 0, 2, 128, 3)], sync=True, sync_before=True)
        sc.add("V", lambda e: e.tensor_scalar_mul(
            statln[:, 2:3], statln[:, 2:3], 1.0 / D),
            reads=[A(statln, 0, 2, 128, 3)], writes=[A(statln, 0, 2, 128, 3)])
        sc.add("V", lambda e: e.tensor_mul(
            statln[:, 3:4], statln[:, 1:2], statln[:, 1:2]),
            reads=[A(statln, 0, 1, 128, 2)], writes=[A(statln, 0, 3, 128, 4)])
        sc.add("V", lambda e: e.tensor_sub(
            statln[:, 2:3], statln[:, 2:3], statln[:, 3:4]),
            reads=[A(statln, 0, 2, 128, 4)], writes=[A(statln, 0, 2, 128, 3)])
        sc.add("V", lambda e: e.tensor_scalar_add(
            statln[:, 2:3], statln[:, 2:3], EPS_LN),
            reads=[A(statln, 0, 2, 128, 3)], writes=[A(statln, 0, 2, 128, 3)])
        sc.add("V", lambda e: e.reciprocal(statln[:, 4:5], statln[:, 2:3]),
            reads=[A(statln, 0, 2, 128, 3)], writes=[A(statln, 0, 4, 128, 5)],
            sync=True, sync_before=True)
        sc.add("A", lambda e: e.activation(
            out=statln[:, 5:6], in_=statln[:, 4:5], func=AF.Sqrt),
            reads=[A(statln, 0, 4, 128, 5)], writes=[A(statln, 0, 5, 128, 6)])
        sc.add("V", lambda e, c0=c0: e.tensor_scalar(
            out=x2tok[:, c0:c0 + D], in0=x2tok[:, c0:c0 + D],
            scalar1=statln[:, 1:2], scalar2=statln[:, 5:6],
            op0=ALU.subtract, op1=ALU.mult),
            reads=[A(x2tok, 0, c0, 128, c0 + D), A(statln, 0, 1, 128, 6)],
            writes=[A(x2tok, 0, c0, 128, c0 + D)])
        if write_bf16:
            sc.add("V", lambda e, c0=c0: e.tensor_copy(
                out=x2btok[:, :], in_=x2tok[:, c0:c0 + D]),
                reads=[A(x2tok, 0, c0, 128, c0 + D)], writes=[A(x2btok)])

    for tc in range(NTC):
        emit_ln(tc, tc * D, True)
        for ic in range(NCH):
            pt = ps["t0"] if (tc * NCH + ic) % 2 == 0 else ps["t1"]
            sc.add("T", lambda e, pt=pt, ic=ic: e.transpose(
                pt[:, 0:128], x2btok[:, ic * 128:(ic + 1) * 128], id_bf[:, :]),
                reads=[A(x2btok, 0, ic * 128, 128, (ic + 1) * 128), A(id_bf)],
                writes=[A(pt, 0, 0, 128, 128)], tag=f"x2tr{tc}_{ic}")
            d0 = ic * 1024 + tc * 128
            sc.add("V", lambda e, pt=pt, d0=d0: e.tensor_copy(
                out=x2bf[:, d0:d0 + 128], in_=pt[:, 0:128]),
                reads=[A(pt, 0, 0, 128, 128)],
                writes=[A(x2bf, 0, d0, 128, d0 + 128)])

    if tapd("x2bf", [128, NCH * 1024], BF16) is not None:
        dma("S", taps["x2bf"], x2bf[:, :], in_acc=[A(x2bf)])

    for sl in range(2):
        for hcp in range(64):
            gblk = egu[(2 * hcp) % 4]
            ublk = egu[(2 * hcp + 1) % 4]
            gh, uh = hcp, 64 + hcp
            dma("S", gblk[:, :], guq_g.ap()[gh // 16, gh % 16],
                in_acc=[A(guq_g)], out_acc=[A(gblk)], tag=f"gu_g{sl}_{hcp}")
            dma("S", ublk[:, :], guq_g.ap()[uh // 16, uh % 16],
                in_acc=[A(guq_g)], out_acc=[A(ublk)], tag=f"gu_u{sl}_{hcp}")
            pg = ps["a0"] if hcp % 2 == 0 else ps["a1"]
            pu = ps["b0"] if hcp % 2 == 0 else ps["b1"]
            for ic in range(NCH):
                st, sp = ic == 0, ic == NCH - 1
                xs0 = ic * 1024 + sl * SL
                sc.add("T", lambda e, gblk=gblk, ic=ic, st=st, sp=sp, pg=pg, xs0=xs0:
                       e.matmul(pg[:, :], gblk[:, ic * 128:(ic + 1) * 128],
                                x2bf[:, xs0:xs0 + SL], start=st, stop=sp),
                       reads=[A(gblk, 0, ic * 128, 128, (ic + 1) * 128),
                              A(x2bf, 0, xs0, 128, xs0 + SL)],
                       writes=[A(pg)], tag=f"gmm{sl}_{hcp}_{ic}")
                sc.add("T", lambda e, ublk=ublk, ic=ic, st=st, sp=sp, pu=pu, xs0=xs0:
                       e.matmul(pu[:, :], ublk[:, ic * 128:(ic + 1) * 128],
                                x2bf[:, xs0:xs0 + SL], start=st, stop=sp),
                       reads=[A(ublk, 0, ic * 128, 128, (ic + 1) * 128),
                              A(x2bf, 0, xs0, 128, xs0 + SL)],
                       writes=[A(pu)], tag=f"umm{sl}_{hcp}_{ic}")
            sc.add("A", lambda e, pg=pg: e.activation(
                out=sgl_t[:, :], in_=pg[:, :], func=AF.Silu),
                reads=[A(pg)], writes=[A(sgl_t)])
            hdst = h_lo if hcp < 32 else h_hi
            hc0 = (hcp % 32) * SL
            sc.add("V", lambda e, pu=pu, hdst=hdst, hc0=hc0: e.tensor_tensor(
                out=hdst[:, hc0:hc0 + SL], in0=sgl_t[:, :], in1=pu[:, :],
                op=ALU.mult),
                reads=[A(sgl_t), A(pu)],
                writes=[A(hdst, 0, hc0, 128, hc0 + SL)])
            sc.add("V", lambda e, hdst=hdst, hc0=hc0: e.tensor_mul(
                h2_t[:, :], hdst[:, hc0:hc0 + SL], hdst[:, hc0:hc0 + SL]),
                reads=[A(hdst, 0, hc0, 128, hc0 + SL)], writes=[A(h2_t)])
            st, sp = hcp == 0, hcp == 63
            sc.add("T", lambda e, st=st, sp=sp: e.matmul(
                ps["m0"][0:1, 0:SL], onescol[:, :], h2_t[:, :],
                start=st, stop=sp),
                reads=[A(onescol), A(h2_t)],
                writes=[A(ps["m0"], 0, 0, 1, SL)], tag=f"hsq{sl}_{hcp}")
        sc.add("V", lambda e: e.tensor_copy(out=inv_row[:, :], in_=ps["m0"][0:1, 0:SL]),
            reads=[A(ps["m0"], 0, 0, 1, SL)], writes=[A(inv_row, 0, 0, 1, SL)])
        sc.add("V", lambda e: e.tensor_scalar(
            out=inv_row[:, :], in0=inv_row[:, :], scalar1=1.0 / 8192,
            scalar2=EPS_RMS, op0=ALU.mult, op1=ALU.add),
            reads=[A(inv_row, 0, 0, 1, SL)], writes=[A(inv_row, 0, 0, 1, SL)])
        sc.add("V", lambda e: e.reciprocal(inv_row[:, :], inv_row[:, :]),
            reads=[A(inv_row, 0, 0, 1, SL)], writes=[A(inv_row, 0, 0, 1, SL)],
            sync=True, sync_before=True)
        sc.add("A", lambda e: e.activation(
            out=inv_row[:, :], in_=inv_row[:, :], func=AF.Sqrt),
            reads=[A(inv_row, 0, 0, 1, SL)], writes=[A(inv_row, 0, 0, 1, SL)])
        sc.add("T", lambda e: e.matmul(
            ps["m1"][:, 0:SL], onesrow[:, :], inv_row[:, :],
            start=True, stop=True),
            reads=[A(onesrow, 0, 0, 1, 128), A(inv_row, 0, 0, 1, SL)],
            writes=[A(ps["m1"])], tag=f"invb{sl}")
        sc.add("V", lambda e: e.tensor_copy(out=inv_bc[:, :], in_=ps["m1"][:, 0:SL]),
            reads=[A(ps["m1"])], writes=[A(inv_bc)])
        for oc in range(NCH):
            for rr in range(NCORES):
                dma("S", eph[rr // 4][:, (rr % 4) * 1024:(rr % 4 + 1) * 1024],
                    pjq_g.ap()[rr, oc], in_acc=[A(pjq_g)],
                    out_acc=[A(eph[rr // 4], 0, (rr % 4) * 1024, 128,
                               (rr % 4 + 1) * 1024)],
                    tag=f"pj{sl}_{oc}_{rr}")
            pp = ps["a0"] if oc % 2 == 0 else ps["b0"]
            for hc in range(64):
                st, sp = hc == 0, hc == 63
                hsrc = h_lo if hc < 32 else h_hi
                hc0 = (hc % 32) * SL
                eb = eph[hc // 32]
                ec0 = (hc % 32) * 128
                sc.add("T", lambda e, st=st, sp=sp, pp=pp, hsrc=hsrc,
                       hc0=hc0, eb=eb, ec0=ec0: e.matmul(
                           pp[:, :], eb[:, ec0:ec0 + 128],
                           hsrc[:, hc0:hc0 + SL], start=st, stop=sp),
                       reads=[A(eb, 0, ec0, 128, ec0 + 128),
                              A(hsrc, 0, hc0, 128, hc0 + SL)],
                       writes=[A(pp)], tag=f"pmm{sl}_{oc}_{hc}")
            sc.add("V", lambda e, pp=pp: e.tensor_mul(
                fo_t[:, :], pp[:, :], inv_bc[:, :]),
                reads=[A(pp), A(inv_bc)], writes=[A(fo_t)])
            for ts in range(4):
                tc = sl * 4 + ts
                pm = ps["m0"] if ts % 2 == 0 else ps["m1"]
                sc.add("T", lambda e, ts=ts, pm=pm: e.transpose(
                    pm[:, 0:128], fo_t[:, ts * 128:(ts + 1) * 128], id_f32[:, :]),
                    reads=[A(fo_t, 0, ts * 128, 128, (ts + 1) * 128), A(id_f32)],
                    writes=[A(pm, 0, 0, 128, 128)], tag=f"ftr{sl}_{oc}_{ts}")
                xc0 = tc * D + oc * 128
                sc.add("V", lambda e, pm=pm, xc0=xc0: e.tensor_add(
                    x2tok[:, xc0:xc0 + 128], x2tok[:, xc0:xc0 + 128],
                    pm[:, 0:128]),
                    reads=[A(pm, 0, 0, 128, 128), A(x2tok, 0, xc0, 128, xc0 + 128)],
                    writes=[A(x2tok, 0, xc0, 128, xc0 + 128)])
        for ts in range(4):
            tc = sl * 4 + ts
            emit_ln(tc, tc * D, False)
            dma("S", out_d[tc * 128:(tc + 1) * 128, :], x2tok[:, tc * D:(tc + 1) * D],
                in_acc=[A(x2tok, 0, tc * D, 128, (tc + 1) * D)],
                tag=f"out{tc}")

    with (
        nc.semaphore("semT") as smT, nc.semaphore("semV") as smV,
        nc.semaphore("semA") as smA, nc.semaphore("semS") as smS,
        nc.semaphore("semG") as smG,
        nc.Block() as blk,
    ):
        sems = {"T": smT, "V": smV, "A": smA, "S": smS, "G": smG}
        finals = sc.finalize(blk, sems, None)

        @blk.sync
        def _(sy):
            sy.wait_ge(smS, finals["S"])

    return nc, taps



def _layoutA(w):
    wr = w.reshape(NCH, 128, NCH, 128)
    return np.ascontiguousarray(np.transpose(wr, (0, 3, 2, 1)))


def _stage_core(ins, cid):
    b, q = cid // 4, cid % 4
    t0 = q * TL
    x = ins["L0_attn_x"]
    state = ins["L0_attn_state"].reshape(D)
    prev = state if q == 0 else x[b, t0 - 1]
    xe = np.concatenate([prev[None, :], x[b, t0:t0 + TL]], axis=0)
    xfT = np.ascontiguousarray(xe.T)
    x_blk = np.ascontiguousarray(xfT.reshape(NCH, 128, TL + 1))
    x_tok = np.ascontiguousarray(x[b, t0:t0 + TL])

    def pp(v):
        return np.ascontiguousarray(v.reshape(NCH, 128).T)

    wkv = ins["L0_attn_wkvstate"]
    wbd = np.zeros((128, HP * 128), np.float32)
    for hp in range(HP):
        wbd[0:64, hp * 128:hp * 128 + 64] = wkv[2 * hp]
        wbd[64:128, hp * 128 + 64:(hp + 1) * 128] = wkv[2 * hp + 1]

    t_loc = np.arange(TL, dtype=np.float32)
    iota_fwd = np.broadcast_to(t0 + t_loc, (128, TL)).astype(np.float32)
    iota_rev = np.broadcast_to(TG - 1 - t0 - t_loc, (128, TL)).astype(np.float32)

    m = (np.arange(128)[:, None] < np.arange(128)[None, :]).astype(np.float32)
    mask2 = np.concatenate([m, m], axis=1).astype(BF)

    obd2 = np.zeros((128, 2), BF)
    obd2[0:64, 0] = BF(1.0)
    obd2[64:128, 1] = BF(1.0)

    pm = np.zeros((8,), np.float32)
    for r in range(8):
        if r // 4 == b and r % 4 < q:
            pm[r] = 1.0
    pmask = np.broadcast_to(pm, (128, 8)).astype(np.float32)

    gu = ins["L0_ffn_w_gate_up"]
    pj = ins["L0_ffn_w_proj"]
    gu_sl = gu[2048 * cid: 2048 * (cid + 1)]
    pj_sl = pj[:, 1024 * cid: 1024 * (cid + 1)]

    d = {
        "x_blk": np.ascontiguousarray(x_blk),
        "x_tok": x_tok,
        "wr_blk": _layoutA(ins["L0_attn_w_r"]),
        "wk_blk": _layoutA(ins["L0_attn_w_k"]),
        "wv_blk": _layoutA(ins["L0_attn_w_v"]),
        "wg_blk": _layoutA(ins["L0_attn_w_g"]),
        "wo_blk": _layoutA(ins["L0_attn_w_o"]),
        "guq_in": np.ascontiguousarray(gu_sl.reshape(16, 128, D)),
        "pjq_in": np.ascontiguousarray(pj_sl.reshape(16, 128, 1024)),
        "tmk_pp": pp(ins["L0_attn_time_mix_k"].reshape(D)),
        "tmv_pp": pp(ins["L0_attn_time_mix_v"].reshape(D)),
        "tmr_pp": pp(ins["L0_attn_time_mix_r"].reshape(D)),
        "tmg_pp": pp(ins["L0_attn_time_mix_g"].reshape(D)),
        "decay_pp": pp(ins["L0_attn_time_decay"].reshape(D)),
        "u_pp": pp(ins["L0_attn_time_faaaa"].reshape(D)),
        "wkvbd": wbd,
        "gam_rep": np.broadcast_to(ins["L0_attn_ln_w"], (128, D)).astype(np.float32),
        "bet_rep": np.broadcast_to(ins["L0_attn_ln_b"], (128, D)).astype(np.float32),
        "iota_fwd": iota_fwd,
        "iota_rev": iota_rev,
        "mask2": mask2,
        "id_bf": np.eye(128, dtype=BF),
        "id_f32": np.eye(128, dtype=np.float32),
        "ones_bd2": obd2,
        "ones_col": np.ones((128, 1), BF),
        "ones_row": np.ones((1, 128), np.float32),
        "pmask_pp": pmask,
    }
    return {k: np.ascontiguousarray(v) for k, v in d.items()}


def kernel(**inputs):
    nc, _ = build_program()
    in_maps = [_stage_core(inputs, cid) for cid in range(NCORES)]
    res = run_bass_kernel_spmd(nc, in_maps, core_ids=list(range(NCORES)))
    out = np.empty((B, TG, D), np.float32)
    for cid in range(NCORES):
        b, q = cid // 4, cid % 4
        out[b, q * TL:(q + 1) * TL] = res.results[cid]["out"]
    return out


# version 05 (final)
# speedup vs baseline: 1.0728x; 1.0728x over previous
import numpy as np
import ml_dtypes
import concourse.bass as bass
import concourse.mybir as mybir
from concourse.bass_utils import run_bass_kernel_spmd

F32 = mybir.dt.float32
BF16 = mybir.dt.bfloat16
AF = mybir.ActivationFunctionType
ALU = mybir.AluOpType
AX = mybir.AxisListType
BF = ml_dtypes.bfloat16

NCORES = 8
B, TG, D = 2, 4096, 2048
TL = 1024
NCH = 16
HP = 16
CK = 128
NCK = TL // CK
SL = 512
NTC = 8
EPS_GN = 0.00064
EPS_LN = 1e-5
EPS_RMS = float(np.finfo(np.float32).eps)
KB = 1024
ENGS = ("T", "V", "A", "S", "G")



class Op:
    __slots__ = ("fn", "reads", "writes", "dma", "tag", "idx", "needed", "cnt",
                 "sync", "sync_before")

    def __init__(self, fn, reads, writes, dma, tag, sync=False, sync_before=False):
        self.fn = fn
        self.reads = reads
        self.writes = writes
        self.dma = dma
        self.tag = tag
        self.idx = -1
        self.needed = False
        self.cnt = 0
        self.sync = sync
        self.sync_before = sync_before


def _overlap(a, b):
    return a[0] < b[2] and b[0] < a[2] and a[1] < b[3] and b[1] < a[3]


class Sched:

    def __init__(self):
        self.ops = {e: [] for e in ENGS}
        self.writers = {}
        self.readers = {}
        self.deps = {}
        self.arena = {}
        self.psum = set()

    def reg_arena(self, handle, byte_lo, byte_hi, esz):
        self.arena[id(handle)] = (byte_lo, byte_hi, esz)

    def reg_psum(self, handle):
        self.psum.add(id(handle))

    def _norm(self, acc):
        h, p_lo, c_lo, p_hi, c_hi = acc
        hid = id(h)
        if hid in self.psum:
            return ("PSUM", hid, 0, 0, 128, 1 << 30)
        if hid in self.arena:
            lo, hi, esz = self.arena[hid]
            b_lo = min(lo + c_lo * esz, hi)
            b_hi = min(lo + (c_hi * esz if c_hi < (1 << 29) else hi), hi)
            return ("ARENA", "ARENA", p_lo, b_lo, p_hi, b_hi)
        return ("H", hid, p_lo, c_lo, p_hi, c_hi)

    def add(self, eng, fn, reads=(), writes=(), dma=False, tag="", sync=False,
            sync_before=False):
        op = Op(fn, [self._norm(a) for a in reads],
                [self._norm(a) for a in writes], dma, tag, sync, sync_before)
        op.idx = len(self.ops[eng])
        self.ops[eng].append(op)
        key = (eng, op.idx)
        dep = set()
        for acc in op.reads:
            for (a2, e2, i2) in self.writers.get(acc[1], ()):
                if _overlap(acc[2:], a2[2:]):
                    dep.add((e2, i2))
        for acc in op.writes:
            for (a2, e2, i2) in self.writers.get(acc[1], ()):
                if _overlap(acc[2:], a2[2:]):
                    dep.add((e2, i2))
            for (a2, e2, i2) in self.readers.get(acc[1], ()):
                if _overlap(acc[2:], a2[2:]):
                    dep.add((e2, i2))
        for acc in op.reads:
            self.readers.setdefault(acc[1], []).append((acc, eng, op.idx))
        for acc in op.writes:
            self.writers.setdefault(acc[1], []).append((acc, eng, op.idx))
        self.deps[key] = dep
        return key

    NSLOT = 4

    def finalize(self, blk, sems, dsems):
        for e in ENGS:
            for op in self.ops[e]:
                if op.dma:
                    op.needed = True
                for (e2, i2) in self.deps[(e, op.idx)]:
                    p = self.ops[e2][i2]
                    if e2 != e or p.dma or p.sync or op.sync_before:
                        p.needed = True
        for e in ENGS:
            c = 0
            nd = 0
            for op in self.ops[e]:
                if op.dma:
                    op.cnt = (nd % self.NSLOT, 16 * (nd // self.NSLOT + 1))
                    nd += 1
                else:
                    if op.needed:
                        c += 1
                    op.cnt = c

        def emitter(e):
            def run(eng):
                hw = {}
                for op in self.ops[e]:
                    waits = {}
                    for (e2, i2) in self.deps[(e, op.idx)]:
                        p = self.ops[e2][i2]
                        if e2 == e and not (p.dma or p.sync or op.sync_before):
                            continue
                        if p.dma:
                            key = (e2, p.cnt[0])
                            waits[key] = max(waits.get(key, 0), p.cnt[1])
                        else:
                            waits[(e2, -1)] = max(waits.get((e2, -1), 0), p.cnt)
                    if op.dma:
                        slot, v = op.cnt
                        if v > 16:
                            waits[(e, slot)] = max(waits.get((e, slot), 0), v - 16)
                    for (e2, slot), v in sorted(waits.items()):
                        if v > hw.get((e2, slot), 0):
                            eng.wait_ge(sems[e2] if slot < 0 else dsems[e2][slot], v)
                            hw[(e2, slot)] = v
                    instr = op.fn(eng)
                    if op.dma:
                        instr.then_inc(dsems[e][op.cnt[0]], 16)
                    elif op.needed:
                        instr.then_inc(sems[e], 1)
            return run

        blk.tensor(emitter("T"))
        blk.vector(emitter("V"))
        blk.scalar(emitter("A"))
        blk.sync(emitter("S"))
        blk.gpsimd(emitter("G"))
        fin = {}
        for e in ENGS:
            nd = sum(1 for op in self.ops[e] if op.dma)
            fin[e] = [(s, 16 * ((nd - 1 - s) // self.NSLOT + 1))
                      for s in range(min(nd, self.NSLOT))]
        return fin


def A(h, p_lo=0, c_lo=0, p_hi=128, c_hi=1 << 30):
    return (h, p_lo, c_lo, p_hi, c_hi)



def build_program(debug_taps=()):
    nc = bass.Bass("TRN2", target_bir_lowering=False)

    def din(name, shape, dt=F32):
        return nc.dram_tensor(name, list(shape), dt, kind="ExternalInput").ap()

    x_blk = din("x_blk", [NCH, 128, TL + 1])
    w_blk = {m: din(f"w{m}_blk", [NCH, 128, NCH, 128]) for m in "krvgo"}
    guq_in = din("guq_in", [16, 128, D])
    pjq_in = din("pjq_in", [16, 128, 1024])
    tm_d = {m: din(f"tm{m}_pp", [128, NCH]) for m in "kvrg"}
    decay_d = din("decay_pp", [128, NCH])
    u_d = din("u_pp", [128, NCH])
    wkvbd_d = din("wkvbd", [128, HP * 128])
    gam_d = din("gam_rep", [128, D])
    bet_d = din("bet_rep", [128, D])
    iota_f_d = din("iota_fwd", [128, TL])
    iota_r_d = din("iota_rev", [128, TL])
    mask2_d = din("mask2", [128, 256], BF16)
    id_bf_d = din("id_bf", [128, 128], BF16)
    id_f32_d = din("id_f32", [128, 128])
    onesbd2_d = din("ones_bd2", [128, 2], BF16)
    onescol_d = din("ones_col", [128, 1], BF16)
    onesrow_d = din("ones_row", [1, 128])
    pmask_d = din("pmask_pp", [128, 8])
    x_tok_d = din("x_tok", [TL, D])

    out_d = nc.dram_tensor("out", [TL, D], F32, kind="ExternalOutput").ap()

    guq_c = nc.dram_tensor("guq_c", [16, 128, D], BF16, kind="Internal")
    pjq_c = nc.dram_tensor("pjq_c", [16, 128, 1024], BF16, kind="Internal")
    guq_g = nc.dram_tensor("guq_g", [NCORES, 16, 128, D], BF16, kind="Internal")
    pjq_g = nc.dram_tensor("pjq_g", [NCORES, 16, 128, 1024], BF16, kind="Internal")
    sig_c = nc.dram_tensor("sig_c", [128, HP * 64], F32, kind="Internal")
    sig_g = nc.dram_tensor("sig_g", [NCORES, 128, HP * 64], F32, kind="Internal")

    taps = {}

    def tapd(name, shape, dt=F32):
        if name in debug_taps:
            taps[name] = nc.dram_tensor("tap_" + name, list(shape), dt,
                                        kind="ExternalOutput").ap()
        return taps.get(name)

    Sal = nc.alloc_sbuf_tensor
    iota = Sal("iota", [128, TL], F32)
    wkvbd = Sal("wkvbd_sb", [128, HP * 128], BF16)
    lnw = Sal("lnw", [128, NCH], F32)
    dectmp = Sal("dectmp", [128, NCH], F32)
    u_sb = Sal("u_sb", [128, NCH], F32)
    tm_sb = {m: Sal(f"tm_{m}", [128, NCH], F32) for m in "kvrg"}
    pmask = Sal("pmask_sb", [128, 8], F32)
    id_bf = Sal("id_bf_sb", [128, 128], BF16)
    id_f32 = Sal("id_f32_sb", [128, 128], F32)
    mask2 = Sal("mask2_sb", [128, 256], BF16)
    onesbd2 = Sal("onesbd2_sb", [128, 2], BF16)
    onescol = Sal("onescol_sb", [128, 1], BF16)
    onesrow = Sal("onesrow_sb", [1, 128], F32)
    S_f32 = Sal("S_f32", [128, HP * 128], F32)
    S_bf = [Sal(f"S_bf{i}", [128, 128], BF16) for i in range(2)]
    bon = Sal("bon", [128, 256], F32)
    qsc = Sal("qsc", [128, 32], F32)
    qscb = Sal("qscb", [128, 32], BF16)
    qrc = Sal("qrc", [128, 32], F32)
    gstat = Sal("gstat", [128, 96], F32)
    statln = Sal("statln", [128, 8], F32)
    rw_t = [Sal(f"rw_t{i}", [128, 128], BF16) for i in range(2)]
    d_t = Sal("d_t", [128, SL], F32)
    vgT_t = Sal("vgT_t", [128, TL], BF16)
    wfor_t = [Sal(f"wfor_t{i}", [128, 128], BF16) for i in range(2)]

    ARENA = 182 * KB
    arena = Sal("arena", [128, ARENA], mybir.dt.uint8)
    abase = nc.lookup_mloc(arena).addr
    sc = Sched()

    def at(name, off_kb, shape, dt):
        off = int(round(off_kb * KB))
        h = nc.alloc_sbuf_tensor_at(name, list(shape), dt, offset=abase + off)
        esz = mybir.dt.size(dt)
        sc.reg_arena(h, off, off + shape[1] * esz, esz)
        return h

    kwT = at("kwT", 0, [128, HP * 1024], BF16)
    rT = at("rT", 32, [128, HP * 1024], BF16)
    x2tok = at("x2tok", 0, [128, NTC * D], F32)
    lerpA = at("lerpA", 64, [128, NCH * 1024], BF16)
    v_tok = at("v_tok", 64, [128, NTC * D], BF16)
    ygT = at("ygT", 64, [128, NCH * 1024], BF16)
    sqd = at("sqd", 64, [128, D], F32)
    x2btok = at("x2btok", 72, [128, D], BF16)
    h_lo = at("h_lo", 64, [128, 32 * SL], BF16)
    g_tok = at("g_tok", 96, [128, NTC * D], BF16)
    qw = [at(f"qw{i}", 96 + 4 * i, [128, D], BF16) for i in range(2)]
    qx = at("qx", 104, [128, D], BF16)
    qa = at("qa", 108, [128, D], BF16)
    qs = at("qs", 112, [128, D], BF16)
    wback = [at(f"wback{i}", 116 + 4 * i, [128, TL], F32) for i in range(2)]
    ku_t = at("ku_t", 124, [128, TL], BF16)
    kr_t = at("kr_t", 126, [128, TL], BF16)
    h_hi = at("h_hi", 96, [128, 32 * SL], BF16)
    lerpB = at("lerpB", 128, [128, NCH * 1024], BF16)
    sig_sb = at("sig_sb", 128, [128, HP * 64], F32)
    wkv_t = [at(f"wkv_t{i}", 128 + 4 * i, [128, D], BF16) for i in range(2)]
    yc_t = at("yc_t", 136, [128, D], F32)
    slab_t = at("slab_t", 144, [128, HP * 64], F32)

    sqc_t = at("sqc_t", 152, [128, D], BF16)
    rbd = [at(f"rbd{i}", 156 + 0.5 * i, [128, 256], BF16) for i in range(2)]
    attT = [at(f"attT{i}", 157 + 0.5 * i, [128, 256], BF16) for i in range(2)]
    kwtok = [at(f"kwtok{i}", 158 + 0.25 * i, [128, 128], BF16) for i in range(2)]
    x2bf = at("x2bf", 128, [128, NCH * 1024], BF16)
    gamb = at("gamb", 160, [128, D], F32)
    betb = at("betb", 168, [128, D], F32)
    ao_tmp = [at(f"ao_tmp{i}", 160 + 2 * i, [128, SL], F32) for i in range(2)]
    sgl_t = at("sgl_t", 160, [128, SL], F32)
    h2_t = at("h2_t", 162, [128, SL], BF16)
    fo_t = at("fo_t", 160, [128, SL], F32)
    inv_row = at("inv_row", 162, [1, SL], F32)
    inv_bc = at("inv_bc", 164, [128, SL], F32)
    xst = [at(f"xst{i}", 160 + 2.25 * i, [128, 513], F32) for i in range(2)]
    wst = [at(f"wst{i}", 165 + 4 * i, [128, D], BF16) for i in range(2)]
    egu = [at(f"egu{i}", 166 + 4 * i, [128, D], BF16) for i in range(4)]
    eph = [at(f"eph{i}", 166 + 8 * i, [128, 4096], BF16) for i in range(2)]

    ps = {
        "a0": nc.alloc_psum_tensor("ps_a0", [128, SL], F32),
        "a1": nc.alloc_psum_tensor("ps_a1", [128, SL], F32),
        "b0": nc.alloc_psum_tensor("ps_b0", [128, SL], F32),
        "b1": nc.alloc_psum_tensor("ps_b1", [128, SL], F32),
        "t0": nc.alloc_psum_tensor("ps_t0", [128, 1024], BF16),
        "t1": nc.alloc_psum_tensor("ps_t1", [128, 1024], BF16),
        "m0": nc.alloc_psum_tensor("ps_m0", [128, SL], F32),
        "m1": nc.alloc_psum_tensor("ps_m1", [128, SL], F32),
    }
    for h in ps.values():
        sc.reg_psum(h)

    def dma(eng, out_ap, in_ap, out_acc=None, in_acc=None, tag=""):
        def f(e, out_ap=out_ap, in_ap=in_ap):
            return e.dma_start(out=out_ap, in_=in_ap)
        sc.add(eng, f, reads=(in_acc or []), writes=(out_acc or []),
               dma=True, tag=tag)

    dma("S", iota[:, :], iota_r_d, out_acc=[A(iota)], tag="ld_iota_r")
    dma("S", lnw[:, :], decay_d, out_acc=[A(lnw)], tag="ld_decay")
    dma("S", u_sb[:, :], u_d, out_acc=[A(u_sb)])
    for m in "kvrg":
        dma("S", tm_sb[m][:, :], tm_d[m], out_acc=[A(tm_sb[m])])
    dma("S", pmask[:, :], pmask_d, out_acc=[A(pmask)])
    dma("S", id_bf[:, :], id_bf_d, out_acc=[A(id_bf)])
    dma("S", id_f32[:, :], id_f32_d, out_acc=[A(id_f32)])
    dma("S", mask2[:, :], mask2_d, out_acc=[A(mask2)])
    dma("S", onesbd2[:, :], onesbd2_d, out_acc=[A(onesbd2)])
    dma("S", onescol[:, :], onescol_d, out_acc=[A(onescol)])
    dma("S", onesrow[:, :], onesrow_d, out_acc=[A(onesrow, 0, 0, 1, 128)])
    dma("G", wkvbd[:, :], wkvbd_d,
        out_acc=[A(wkvbd)], tag="ld_wkvbd")

    sc.add("A", lambda e: e.activation(out=dectmp[:, :], in_=lnw[:, :], func=AF.Exp),
           reads=[A(lnw)], writes=[A(dectmp)])
    sc.add("V", lambda e: e.tensor_scalar_mul(lnw[:, :], dectmp[:, :], -1.0),
           reads=[A(dectmp)], writes=[A(lnw)])

    def emit_lerp(mkey, dest):
        tm = tm_sb[mkey]
        for h2 in range(2):
            for ic in range(NCH):
                i = (h2 * NCH + ic) % 2
                xs = xst[i]
                dma("S", xs[:, :], x_blk[ic, :, h2 * SL:h2 * SL + SL + 1],
                    out_acc=[A(xs)], tag=f"x_{mkey}_{h2}_{ic}")
                sc.add("V", lambda e, xs=xs: e.tensor_sub(
                    d_t[:, :], xs[:, 1:SL + 1], xs[:, 0:SL]),
                    reads=[A(xs)], writes=[A(d_t)])
                c0 = ic * 1024 + h2 * SL
                sc.add("V", lambda e, xs=xs, ic=ic, c0=c0: e.scalar_tensor_tensor(
                    out=dest[:, c0:c0 + SL], in0=d_t[:, :],
                    scalar=tm[:, ic:ic + 1], in1=xs[:, 0:SL],
                    op0=ALU.mult, op1=ALU.add),
                    reads=[A(d_t), A(tm), A(xs)],
                    writes=[A(dest, 0, c0, 128, c0 + SL)])

    def emit_proj(mat, src, evac, accs=("a0", "a1", "b0", "b1"), per_oc=None):
        for oc in range(NCH):
            wb = wst[oc % 2]
            dma("G", wb[:, :], w_blk[mat][oc].rearrange("p a b -> p (a b)"),
                out_acc=[A(wb)], tag=f"w{mat}{oc}")
            pp = [ps[accs[2 * (oc % 2)]], ps[accs[2 * (oc % 2) + 1]]]
            for ic in range(NCH):
                st, sp = ic == 0, ic == NCH - 1
                for n in range(2):
                    sc.add("T", lambda e, wb=wb, ic=ic, n=n, st=st, sp=sp, pp=pp: e.matmul(
                        pp[n][:, :], wb[:, ic * 128:(ic + 1) * 128],
                        src[:, ic * 1024 + n * SL: ic * 1024 + (n + 1) * SL],
                        start=st, stop=sp),
                        reads=[A(wb, 0, ic * 128, 128, (ic + 1) * 128),
                               A(src, 0, ic * 1024 + n * SL, 128, ic * 1024 + (n + 1) * SL)],
                        writes=[A(pp[n])], tag=f"{mat}mm{oc}_{ic}_{n}")
            evac(oc, pp)
            if per_oc is not None:
                per_oc(oc)

    emit_lerp("r", lerpA)

    def r_evac(oc, pp):
        for n in range(2):
            c0 = oc * 1024 + n * SL
            sc.add("V", lambda e, pp=pp, n=n, c0=c0: e.tensor_copy(
                out=rT[:, c0:c0 + SL], in_=pp[n][:, :]),
                reads=[A(pp[n])], writes=[A(rT, 0, c0, 128, c0 + SL)])

    def quant_tile(t):
        isgu = t < 16
        W = D if isgu else 1024
        ngrp = W // 64
        ntr = W // 128
        qwb = qw[t % 2]
        dma("G", qwb[:, 0:W], (guq_in[t] if isgu else pjq_in[t - 16]),
            out_acc=[A(qwb, 0, 0, 128, W)], tag=f"q_in{t}")
        a3 = qwb.ap()[:, 0:W].rearrange("p (g e) -> p g e", e=64)
        sc.add("V", lambda e, a3=a3, ngrp=ngrp: e.tensor_reduce(
            qsc[:, 0:ngrp], a3, AX.X, ALU.add, apply_absolute_value=True),
            reads=[A(qwb, 0, 0, 128, W)], writes=[A(qsc)], sync=True, sync_before=True)
        sc.add("G", lambda e, ngrp=ngrp: e.tensor_scalar(
            out=qscb[:, 0:ngrp], in0=qsc[:, 0:ngrp], scalar1=1.0 / 64,
            scalar2=1e-8, op0=ALU.mult, op1=ALU.max),
            reads=[A(qsc)], writes=[A(qscb)])
        sc.add("V", lambda e, ngrp=ngrp: e.reciprocal(qrc[:, 0:ngrp], qscb[:, 0:ngrp]),
            reads=[A(qscb)], writes=[A(qrc)], sync=True, sync_before=True)
        rbc = qrc.ap()[:, 0:ngrp].unsqueeze(2).broadcast_to([128, ngrp, 64])
        sbc = qscb.ap()[:, 0:ngrp].unsqueeze(2).broadcast_to([128, ngrp, 64])
        qx3 = qx.ap()[:, 0:W].rearrange("p (g e) -> p g e", e=64)
        qa3 = qa.ap()[:, 0:W].rearrange("p (g e) -> p g e", e=64)
        qw3 = qwb.ap()[:, 0:W].rearrange("p (g e) -> p g e", e=64)
        sc.add("G", lambda e, qx3=qx3, qw3=qw3, rbc=rbc: e.tensor_tensor(
            out=qx3, in0=qw3, in1=rbc, op=ALU.mult),
            reads=[A(qwb, 0, 0, 128, W), A(qrc)], writes=[A(qx, 0, 0, 128, W)])
        sc.add("V", lambda e, qa3=qa3, qx3=qx3, sbc=sbc: e.scalar_tensor_tensor(
            out=qa3, in0=qx3, scalar=0.5, in1=sbc, op0=ALU.is_gt, op1=ALU.mult),
            reads=[A(qx, 0, 0, 128, W), A(qscb)], writes=[A(qa, 0, 0, 128, W)])
        sc.add("V", lambda e, qx3=qx3, sbc=sbc: e.scalar_tensor_tensor(
            out=qx3, in0=qx3, scalar=-0.5, in1=sbc, op0=ALU.is_lt, op1=ALU.mult),
            reads=[A(qx, 0, 0, 128, W), A(qscb)], writes=[A(qx, 0, 0, 128, W)])
        sc.add("G", lambda e, W=W: e.tensor_sub(
            qa.ap()[:, 0:W], qa.ap()[:, 0:W], qx.ap()[:, 0:W]),
            reads=[A(qa, 0, 0, 128, W), A(qx, 0, 0, 128, W)],
            writes=[A(qa, 0, 0, 128, W)])
        for k in range(ntr):
            pt = ps["t0"] if k % 2 == 0 else ps["t1"]
            sc.add("T", lambda e, k=k, pt=pt: e.transpose(
                pt[:, 0:128], qa[:, k * 128:(k + 1) * 128], id_bf[:, :]),
                reads=[A(qa, 0, k * 128, 128, (k + 1) * 128), A(id_bf)],
                writes=[A(pt, 0, 0, 128, 128)], tag=f"qtr{t}_{k}")
            sc.add("A", lambda e, k=k, pt=pt: e.activation(
                out=qs[:, k * 128:(k + 1) * 128], in_=pt[:, 0:128], func=AF.Copy),
                reads=[A(pt, 0, 0, 128, 128)],
                writes=[A(qs, 0, k * 128, 128, (k + 1) * 128)])
        dma("G", (guq_c.ap()[t] if isgu else pjq_c.ap()[t - 16]), qs[:, 0:W],
            in_acc=[A(qs, 0, 0, 128, W)],
            out_acc=[A(guq_c if isgu else pjq_c)], tag=f"q_st{t}")

    def coll1a(e):
        return e.collective_compute(
            "AllGather", ALU.bypass, replica_groups=[list(range(NCORES))],
            ins=[guq_c.ap().opt()], outs=[guq_g.ap().opt()])

    def coll1b(e):
        return e.collective_compute(
            "AllGather", ALU.bypass, replica_groups=[list(range(NCORES))],
            ins=[pjq_c.ap().opt()], outs=[pjq_g.ap().opt()])
    emit_proj("r", lerpA, r_evac,
              per_oc=lambda oc: (quant_tile(2 * oc), quant_tile(2 * oc + 1)))
    sc.add("G", coll1a, reads=[A(guq_c)], writes=[A(guq_g)], tag="coll1a")
    sc.add("G", coll1b, reads=[A(pjq_c)], writes=[A(pjq_g)], tag="coll1b")

    emit_lerp("k", lerpA)

    def k_evac(oc, pp):
        wbk = wback[oc % 2]
        sc.add("A", lambda e, wbk=wbk, oc=oc: e.activation(
            out=wbk[:, :], in_=iota[:, :], func=AF.Exp, scale=lnw[:, oc:oc + 1]),
            reads=[A(iota), A(lnw, 0, oc, 128, oc + 1)], writes=[A(wbk)])
        for n in range(2):
            c0 = oc * 1024 + n * SL
            sc.add("V", lambda e, pp=pp, n=n, c0=c0, wbk=wbk: e.tensor_tensor(
                out=kwT[:, c0:c0 + SL], in0=pp[n][:, :],
                in1=wbk[:, n * SL:(n + 1) * SL], op=ALU.mult),
                reads=[A(pp[n]), A(wbk, 0, n * SL, 128, (n + 1) * SL)],
                writes=[A(kwT, 0, c0, 128, c0 + SL)])
            sc.add("V", lambda e, pp=pp, n=n, oc=oc: e.tensor_scalar_mul(
                ku_t[:, n * SL:(n + 1) * SL], pp[n][:, :], u_sb[:, oc:oc + 1]),
                reads=[A(pp[n]), A(u_sb, 0, oc, 128, oc + 1)],
                writes=[A(ku_t, 0, n * SL, 128, (n + 1) * SL)])
        sc.add("V", lambda e, oc=oc: e.tensor_mul(
            kr_t[:, :], ku_t[:, :], rT[:, oc * 1024:(oc + 1) * 1024]),
            reads=[A(ku_t), A(rT, 0, oc * 1024, 128, (oc + 1) * 1024)],
            writes=[A(kr_t)])
        for ck in range(NCK):
            sc.add("T", lambda e, ck=ck: e.matmul(
                ps["m0"][:, ck * 2:(ck + 1) * 2], kr_t[:, ck * 128:(ck + 1) * 128],
                onesbd2[:, :], start=True, stop=True),
                reads=[A(kr_t, 0, ck * 128, 128, (ck + 1) * 128), A(onesbd2)],
                writes=[A(ps["m0"], 0, ck * 2, 128, (ck + 1) * 2)],
                tag=f"bon{oc}_{ck}")
        sc.add("V", lambda e, oc=oc: e.tensor_copy(
            out=bon[:, oc * 16:(oc + 1) * 16], in_=ps["m0"][:, 0:16]),
            reads=[A(ps["m0"], 0, 0, 128, 16)],
            writes=[A(bon, 0, oc * 16, 128, (oc + 1) * 16)])

    emit_proj("k", lerpA, k_evac)


    emit_lerp("v", lerpB)
    if tapd("xv", [128, NCH * 1024], BF16) is not None:
        dma("S", taps["xv"], lerpB[:, :], in_acc=[A(lerpB)])

    def vg_evac(kind):
        def evac(oc, pp):
            for n in range(2):
                if kind == "v":
                    sc.add("A", lambda e, pp=pp, n=n: e.activation(
                        out=vgT_t[:, n * SL:(n + 1) * SL], in_=pp[n][:, :],
                        func=AF.Copy),
                        reads=[A(pp[n])],
                        writes=[A(vgT_t, 0, n * SL, 128, (n + 1) * SL)])
                else:
                    sc.add("A", lambda e, pp=pp, n=n: e.activation(
                        out=vgT_t[:, n * SL:(n + 1) * SL], in_=pp[n][:, :],
                        func=AF.Silu),
                        reads=[A(pp[n])],
                        writes=[A(vgT_t, 0, n * SL, 128, (n + 1) * SL)])
            dst = v_tok if kind == "v" else g_tok
            for tc in range(NTC):
                pt = ps["t0"] if tc % 2 == 0 else ps["t1"]
                sc.add("T", lambda e, tc=tc, pt=pt: e.transpose(
                    pt[:, 0:128], vgT_t[:, tc * 128:(tc + 1) * 128], id_bf[:, :]),
                    reads=[A(vgT_t, 0, tc * 128, 128, (tc + 1) * 128), A(id_bf)],
                    writes=[A(pt, 0, 0, 128, 128)], tag=f"{kind}tr{oc}_{tc}")
                c0 = tc * D + oc * 128
                sc.add("V", lambda e, pt=pt, c0=c0: e.tensor_copy(
                    out=dst[:, c0:c0 + 128], in_=pt[:, 0:128]),
                    reads=[A(pt, 0, 0, 128, 128)],
                    writes=[A(dst, 0, c0, 128, c0 + 128)])
        return evac

    emit_proj("v", lerpB, vg_evac("v"))
    if tapd("vgT", [128, TL], BF16) is not None:
        dma("S", taps["vgT"], vgT_t[:, :], in_acc=[A(vgT_t)])
    emit_lerp("g", lerpB)
    emit_proj("g", lerpB, vg_evac("g"))

    dma("S", gamb[:, :], gam_d, out_acc=[A(gamb)], tag="ld_gam")
    dma("S", betb[:, :], bet_d, out_acc=[A(betb)], tag="ld_bet")

    for hp in range(HP):
        pA_ = ps["m0"]
        pB_ = ps["m1"]
        for ck in range(NCK):
            pt = ps["t0"] if ck % 2 == 0 else ps["t1"]
            kt = kwtok[ck % 2]
            c0 = hp * 1024 + ck * 128
            sc.add("T", lambda e, pt=pt, c0=c0: e.transpose(
                pt[:, 0:128], kwT[:, c0:c0 + 128], id_bf[:, :]),
                reads=[A(kwT, 0, c0, 128, c0 + 128), A(id_bf)],
                writes=[A(pt, 0, 0, 128, 128)], tag=f"sigtr{hp}_{ck}")
            sc.add("A", lambda e, pt=pt, kt=kt: e.activation(
                out=kt[:, :], in_=pt[:, 0:128], func=AF.Copy),
                reads=[A(pt, 0, 0, 128, 128)], writes=[A(kt)])
            vc0 = ck * D + hp * 128
            st, sp = ck == 0, ck == NCK - 1
            sc.add("T", lambda e, kt=kt, vc0=vc0, st=st, sp=sp, pA_=pA_: e.matmul(
                pA_[0:64, 0:64], kt[:, 0:64], v_tok[:, vc0:vc0 + 64],
                start=st, stop=sp),
                reads=[A(kt, 0, 0, 128, 64), A(v_tok, 0, vc0, 128, vc0 + 64)],
                writes=[A(pA_, 0, 0, 64, 64)], tag=f"sigA{hp}_{ck}")
            sc.add("T", lambda e, kt=kt, vc0=vc0, st=st, sp=sp, pB_=pB_: e.matmul(
                pB_[64:128, 64:128], kt[:, 64:128], v_tok[:, vc0 + 64:vc0 + 128],
                start=st, stop=sp),
                reads=[A(kt, 0, 64, 128, 128), A(v_tok, 0, vc0 + 64, 128, vc0 + 128)],
                writes=[A(pB_, 64, 64, 128, 128)], tag=f"sigB{hp}_{ck}")
        sc.add("V", lambda e, hp=hp, pA_=pA_: e.tensor_copy(
            out=sig_sb[0:64, hp * 64:(hp + 1) * 64], in_=pA_[0:64, 0:64]),
            reads=[A(pA_, 0, 0, 64, 64)],
            writes=[A(sig_sb, 0, hp * 64, 64, (hp + 1) * 64)])
        sc.add("V", lambda e, hp=hp, pB_=pB_: e.tensor_copy(
            out=sig_sb[64:128, hp * 64:(hp + 1) * 64], in_=pB_[64:128, 64:128]),
            reads=[A(pB_, 64, 64, 128, 128)],
            writes=[A(sig_sb, 64, hp * 64, 128, (hp + 1) * 64)])
    dma("S", sig_c.ap(), sig_sb[:, :], in_acc=[A(sig_sb)], out_acc=[A(sig_c)],
        tag="sig_store")

    def coll2(e):
        return e.collective_compute(
            "AllGather", ALU.bypass, replica_groups=[list(range(NCORES))],
            ins=[sig_c.ap().opt()], outs=[sig_g.ap().opt()])
    sc.add("G", coll2, reads=[A(sig_c)], writes=[A(sig_g)], tag="coll2")

    if tapd("rT", [128, HP * 1024], BF16) is not None:
        dma("S", taps["rT"], rT[:, :], in_acc=[A(rT)])
    if tapd("kwT", [128, HP * 1024], BF16) is not None:
        dma("S", taps["kwT"], kwT[:, :], in_acc=[A(kwT)])
    if tapd("v_tok", [128, NTC * D], BF16) is not None:
        dma("S", taps["v_tok"], v_tok[:, :], in_acc=[A(v_tok)])
    if tapd("g_tok", [128, NTC * D], BF16) is not None:
        dma("S", taps["g_tok"], g_tok[:, :], in_acc=[A(g_tok)])
    if tapd("bon", [128, 256]) is not None:
        dma("S", taps["bon"], bon[:, :], in_acc=[A(bon)])
    if tapd("sig", [128, HP * 64]) is not None:
        dma("S", taps["sig"], sig_sb[:, :], in_acc=[A(sig_sb)])

    dma("S", iota[:, :], iota_f_d, out_acc=[A(iota)], tag="ld_iota_f")

    sc.add("V", lambda e: e.memset(S_f32[:, :], 0.0), writes=[A(S_f32)])
    sc.add("V", lambda e: e.memset(sig_sb[:, :], 0.0), writes=[A(sig_sb)])
    for r in range(NCORES):
        dma("S", slab_t[:, :], sig_g.ap()[r], in_acc=[A(sig_g)],
            out_acc=[A(slab_t)], tag=f"slab{r}")
        sc.add("V", lambda e, r=r: e.scalar_tensor_tensor(
            out=sig_sb[:, :], in0=slab_t[:, :], scalar=pmask[:, r:r + 1],
            in1=sig_sb[:, :], op0=ALU.mult, op1=ALU.add),
            reads=[A(slab_t), A(pmask), A(sig_sb)], writes=[A(sig_sb)])
    for hp in range(HP):
        sc.add("V", lambda e, hp=hp: e.tensor_copy(
            out=S_f32[0:64, hp * 128:hp * 128 + 64],
            in_=sig_sb[0:64, hp * 64:(hp + 1) * 64]),
            reads=[A(sig_sb, 0, hp * 64, 64, (hp + 1) * 64)],
            writes=[A(S_f32, 0, hp * 128, 64, hp * 128 + 64)])
        sc.add("V", lambda e, hp=hp: e.tensor_copy(
            out=S_f32[64:128, hp * 128 + 64:(hp + 1) * 128],
            in_=sig_sb[64:128, hp * 64:(hp + 1) * 64]),
            reads=[A(sig_sb, 64, hp * 64, 128, (hp + 1) * 64)],
            writes=[A(S_f32, 64, hp * 128 + 64, 128, (hp + 1) * 128)])
    sc.add("V", lambda e: e.memset(rbd[0][:, :], 0.0), writes=[A(rbd[0])])
    sc.add("V", lambda e: e.memset(rbd[1][:, :], 0.0), writes=[A(rbd[1])])
    if tapd("cpref", [128, HP * 128]) is not None:
        dma("S", taps["cpref"], S_f32[:, :], in_acc=[A(S_f32)])

    def scan_front(it):
        ck, hp = it
        p = (ck * HP + hp) % 2
        c0 = hp * 1024 + ck * 128
        sc.add("A", lambda e, p=p, ck=ck, hp=hp: e.activation(
            out=wfor_t[p][:, :], in_=iota[:, ck * 128:(ck + 1) * 128],
            func=AF.Exp, scale=lnw[:, hp:hp + 1]),
            reads=[A(iota, 0, ck * 128, 128, (ck + 1) * 128),
                   A(lnw, 0, hp, 128, hp + 1)],
            writes=[A(wfor_t[p])])
        sc.add("V", lambda e, p=p, c0=c0: e.tensor_mul(
            rw_t[p][:, :], rT[:, c0:c0 + 128], wfor_t[p][:, :]),
            reads=[A(rT, 0, c0, 128, c0 + 128), A(wfor_t[p])],
            writes=[A(rw_t[p])])
        sc.add("V", lambda e, p=p, c0=c0: e.tensor_copy(
            out=rbd[p][0:64, 0:128], in_=rT[0:64, c0:c0 + 128]),
            reads=[A(rT, 0, c0, 64, c0 + 128)], writes=[A(rbd[p], 0, 0, 64, 128)])
        sc.add("V", lambda e, p=p, c0=c0: e.tensor_copy(
            out=rbd[p][64:128, 128:256], in_=rT[64:128, c0:c0 + 128]),
            reads=[A(rT, 64, c0, 128, c0 + 128)],
            writes=[A(rbd[p], 64, 128, 128, 256)])
        sc.add("V", lambda e, p=p, hp=hp: e.tensor_copy(
            out=S_bf[p][:, :], in_=S_f32[:, hp * 128:(hp + 1) * 128]),
            reads=[A(S_f32, 0, hp * 128, 128, (hp + 1) * 128)],
            writes=[A(S_bf[p])])
        pt = ps["t0"] if p == 0 else ps["t1"]
        sc.add("T", lambda e, pt=pt, c0=c0: e.transpose(
            pt[:, 0:128], kwT[:, c0:c0 + 128], id_bf[:, :]),
            reads=[A(kwT, 0, c0, 128, c0 + 128), A(id_bf)],
            writes=[A(pt, 0, 0, 128, 128)], tag=f"ctr{ck}_{hp}")
        pa = ps["a0"] if p == 0 else ps["a1"]
        sc.add("T", lambda e, pa=pa, c0=c0, p=p: e.matmul(
            pa[:, 0:256], kwT[:, c0:c0 + 128], rbd[p][:, :],
            start=True, stop=True),
            reads=[A(kwT, 0, c0, 128, c0 + 128), A(rbd[p])],
            writes=[A(pa, 0, 0, 128, 256)], tag=f"catt{ck}_{hp}")
        sc.add("V", lambda e, pa=pa, p=p: e.tensor_tensor(
            out=attT[p][:, :], in0=pa[:, 0:256], in1=mask2[:, :], op=ALU.mult),
            reads=[A(pa, 0, 0, 128, 256), A(mask2)], writes=[A(attT[p])])
        sc.add("V", lambda e, pt=pt, p=p: e.tensor_copy(
            out=kwtok[p][:, :], in_=pt[:, 0:128]),
            reads=[A(pt, 0, 0, 128, 128)], writes=[A(kwtok[p])])

    def scan_back(it):
        ck, hp = it
        p = (ck * HP + hp) % 2
        c0 = hp * 1024 + ck * 128
        vc0 = ck * D + hp * 128
        po = ps["b0"] if p == 0 else ps["b1"]
        sc.add("T", lambda e, po=po, p=p, hp=hp: e.matmul(
            po[:, 0:128], rw_t[p][:, :], wkvbd[:, hp * 128:(hp + 1) * 128],
            start=True, stop=False),
            reads=[A(rw_t[p]), A(wkvbd, 0, hp * 128, 128, (hp + 1) * 128)],
            writes=[A(po, 0, 0, 128, 128)], tag=f"cbase{ck}_{hp}")
        sc.add("T", lambda e, po=po, p=p, c0=c0: e.matmul(
            po[:, 0:128], rT[:, c0:c0 + 128], S_bf[p][:, :],
            start=False, stop=False),
            reads=[A(rT, 0, c0, 128, c0 + 128), A(S_bf[p])],
            writes=[A(po, 0, 0, 128, 128)])
        sc.add("T", lambda e, po=po, p=p, vc0=vc0: e.matmul(
            po[:, 0:64], attT[p][:, 0:128], v_tok[:, vc0:vc0 + 64],
            start=False, stop=False),
            reads=[A(attT[p], 0, 0, 128, 128), A(v_tok, 0, vc0, 128, vc0 + 64)],
            writes=[A(po, 0, 0, 128, 64)])
        sc.add("T", lambda e, po=po, p=p, vc0=vc0: e.matmul(
            po[:, 64:128], attT[p][:, 128:256], v_tok[:, vc0 + 64:vc0 + 128],
            start=False, stop=True),
            reads=[A(attT[p], 0, 128, 128, 256),
                   A(v_tok, 0, vc0 + 64, 128, vc0 + 128)],
            writes=[A(po, 0, 64, 128, 128)])
        pm = ps["m0"] if p == 0 else ps["m1"]
        sc.add("T", lambda e, pm=pm, p=p, vc0=vc0: e.matmul(
            pm[0:64, 0:64], kwtok[p][:, 0:64], v_tok[:, vc0:vc0 + 64],
            start=True, stop=True),
            reads=[A(kwtok[p], 0, 0, 128, 64), A(v_tok, 0, vc0, 128, vc0 + 64)],
            writes=[A(pm, 0, 0, 64, 64)], tag=f"csup{ck}_{hp}")
        sc.add("T", lambda e, pm=pm, p=p, vc0=vc0: e.matmul(
            pm[64:128, 64:128], kwtok[p][:, 64:128],
            v_tok[:, vc0 + 64:vc0 + 128], start=True, stop=True),
            reads=[A(kwtok[p], 0, 64, 128, 128),
                   A(v_tok, 0, vc0 + 64, 128, vc0 + 128)],
            writes=[A(pm, 64, 64, 128, 128)])
        wt = wkv_t[ck % 2]
        for ee in range(2):
            bc = hp * 16 + ck * 2 + ee
            wc0 = hp * 128 + ee * 64
            sc.add("V", lambda e, po=po, vc0=vc0, ee=ee, bc=bc, wt=wt, wc0=wc0:
                   e.scalar_tensor_tensor(
                       out=wt[:, wc0:wc0 + 64],
                       in0=v_tok[:, vc0 + ee * 64:vc0 + (ee + 1) * 64],
                       scalar=bon[:, bc:bc + 1], in1=po[:, ee * 64:(ee + 1) * 64],
                       op0=ALU.mult, op1=ALU.add),
                   reads=[A(v_tok, 0, vc0 + ee * 64, 128, vc0 + (ee + 1) * 64),
                          A(bon, 0, bc, 128, bc + 1),
                          A(po, 0, ee * 64, 128, (ee + 1) * 64)],
                   writes=[A(wt, 0, wc0, 128, wc0 + 64)])
        sc.add("V", lambda e, pm=pm, hp=hp: e.tensor_add(
            S_f32[0:64, hp * 128:hp * 128 + 64],
            S_f32[0:64, hp * 128:hp * 128 + 64], pm[0:64, 0:64]),
            reads=[A(pm, 0, 0, 64, 64),
                   A(S_f32, 0, hp * 128, 64, hp * 128 + 64)],
            writes=[A(S_f32, 0, hp * 128, 64, hp * 128 + 64)])
        sc.add("V", lambda e, pm=pm, hp=hp: e.tensor_add(
            S_f32[64:128, hp * 128 + 64:(hp + 1) * 128],
            S_f32[64:128, hp * 128 + 64:(hp + 1) * 128], pm[64:128, 64:128]),
            reads=[A(pm, 64, 64, 128, 128),
                   A(S_f32, 64, hp * 128 + 64, 128, (hp + 1) * 128)],
            writes=[A(S_f32, 64, hp * 128 + 64, 128, (hp + 1) * 128)])

    def groupnorm_gate(ck):
        wt = wkv_t[ck % 2]
        g3 = lambda x: x.ap().rearrange("p (g e) -> p g e", e=64)
        bc = lambda x, n: x.ap()[:, 0:n].unsqueeze(2).broadcast_to([128, n, 64])
        sc.add("V", lambda e, wt=wt: e.tensor_reduce(
            gstat[:, 0:32], g3(wt), AX.X, ALU.add),
            reads=[A(wt)], writes=[A(gstat, 0, 0, 128, 32)], sync=True, sync_before=True)
        sc.add("V", lambda e: e.tensor_scalar_mul(
            gstat[:, 32:64], gstat[:, 0:32], 1.0 / 64),
            reads=[A(gstat, 0, 0, 128, 32)], writes=[A(gstat, 0, 32, 128, 64)], sync=True, sync_before=True)
        sc.add("V", lambda e, wt=wt: e.tensor_tensor(
            out=yc_t.ap().rearrange("p (g e) -> p g e", e=64), in0=g3(wt),
            in1=gstat.ap()[:, 32:64].unsqueeze(2).broadcast_to([128, 32, 64]),
            op=ALU.subtract),
            reads=[A(wt), A(gstat, 0, 32, 128, 64)], writes=[A(yc_t)], sync=True, sync_before=True)
        sc.add("A", lambda e: e.activation(out=sqc_t[:, :], in_=yc_t[:, :],
                                           func=AF.Square),
            reads=[A(yc_t)], writes=[A(sqc_t)], sync=True, sync_before=True)
        sc.add("V", lambda e: e.tensor_reduce(
            gstat[:, 0:32], g3(sqc_t), AX.X, ALU.add),
            reads=[A(sqc_t)], writes=[A(gstat, 0, 0, 128, 32)], sync=True, sync_before=True)
        sc.add("V", lambda e: e.tensor_scalar(
            out=gstat[:, 0:32], in0=gstat[:, 0:32], scalar1=1.0 / 64,
            scalar2=EPS_GN, op0=ALU.mult, op1=ALU.add),
            reads=[A(gstat, 0, 0, 128, 32)], writes=[A(gstat, 0, 0, 128, 32)], sync=True, sync_before=True)
        sc.add("A", lambda e: e.activation(out=gstat[:, 64:96], in_=gstat[:, 0:32],
                                           func=AF.Ln),
            reads=[A(gstat, 0, 0, 128, 32)], writes=[A(gstat, 0, 64, 128, 96)],
            sync=True, sync_before=True)
        sc.add("A", lambda e: e.activation(out=gstat[:, 0:32], in_=gstat[:, 64:96],
                                           func=AF.Exp, scale=-0.5),
            reads=[A(gstat, 0, 64, 128, 96)], writes=[A(gstat, 0, 0, 128, 32)],
            sync=True, sync_before=True)
        sc.add("V", lambda e: e.tensor_tensor(
            out=yc_t.ap().rearrange("p (g e) -> p g e", e=64),
            in0=yc_t.ap().rearrange("p (g e) -> p g e", e=64),
            in1=gstat.ap()[:, 0:32].unsqueeze(2).broadcast_to([128, 32, 64]),
            op=ALU.mult),
            reads=[A(yc_t), A(gstat, 0, 0, 128, 32)], writes=[A(yc_t)], sync=True, sync_before=True)
        sc.add("V", lambda e: e.tensor_mul(yc_t[:, :], yc_t[:, :], gamb[:, :]),
            reads=[A(yc_t), A(gamb)], writes=[A(yc_t)], sync=True, sync_before=True)
        sc.add("V", lambda e: e.tensor_add(yc_t[:, :], yc_t[:, :], betb[:, :]),
            reads=[A(yc_t), A(betb)], writes=[A(yc_t)], sync=True, sync_before=True)
        c0 = ck * D
        sc.add("V", lambda e, c0=c0: e.tensor_mul(
            g_tok[:, c0:c0 + D], yc_t[:, :], g_tok[:, c0:c0 + D]),
            reads=[A(yc_t), A(g_tok, 0, c0, 128, c0 + D)],
            writes=[A(g_tok, 0, c0, 128, c0 + D)], sync=True, sync_before=True)

    tw = tapd("wkv8", [NCK, 128, D], BF16)
    def maybe_tap_wkv(ck):
        if tw is not None:
            dma("S", tw[ck], wkv_t[ck % 2][:, :], in_acc=[A(wkv_t[ck % 2])])

    its = [(ck, hp) for ck in range(NCK) for hp in range(HP)]
    prev = None
    for it in its:
        scan_front(it)
        if prev is not None:
            scan_back(prev)
            if prev[1] == HP - 1:
                maybe_tap_wkv(prev[0])
                groupnorm_gate(prev[0])
        prev = it
    scan_back(prev)
    maybe_tap_wkv(prev[0])
    groupnorm_gate(prev[0])
    if tapd("sfin", [128, HP * 128]) is not None:
        dma("S", taps["sfin"], S_f32[:, :], in_acc=[A(S_f32)])
    if tapd("gnyc", [128, D]) is not None:
        dma("S", taps["gnyc"], yc_t[:, :], in_acc=[A(yc_t)])
    if tapd("gnstat", [128, 96]) is not None:
        dma("S", taps["gnstat"], gstat[:, :], in_acc=[A(gstat)])

    if tapd("yg", [128, NTC * D], BF16) is not None:
        dma("S", taps["yg"], g_tok[:, :], in_acc=[A(g_tok)])

    for tc in range(NTC):
        for cc in range(NCH):
            pt = ps["t0"] if (tc * NCH + cc) % 2 == 0 else ps["t1"]
            c0 = tc * D + cc * 128
            d0 = cc * 1024 + tc * 128
            sc.add("T", lambda e, pt=pt, c0=c0: e.transpose(
                pt[:, 0:128], g_tok[:, c0:c0 + 128], id_bf[:, :]),
                reads=[A(g_tok, 0, c0, 128, c0 + 128), A(id_bf)],
                writes=[A(pt, 0, 0, 128, 128)], tag=f"ygtr{tc}_{cc}")
            sc.add("V", lambda e, pt=pt, d0=d0: e.tensor_copy(
                out=ygT[:, d0:d0 + 128], in_=pt[:, 0:128]),
                reads=[A(pt, 0, 0, 128, 128)],
                writes=[A(ygT, 0, d0, 128, d0 + 128)])

    for tc in range(NTC):
        dma("S", x2tok[:, tc * D:(tc + 1) * D],
            x_tok_d[tc * 128:(tc + 1) * 128, :],
            out_acc=[A(x2tok, 0, tc * D, 128, (tc + 1) * D)], tag=f"xtok{tc}")

    for oc in range(NCH):
        wb = wst[oc % 2]
        dma("G", wb[:, :], w_blk["o"][oc].rearrange("p a b -> p (a b)"),
            out_acc=[A(wb)], tag=f"wo{oc}")
        pp = [ps["a0"], ps["a1"]] if oc % 2 == 0 else [ps["b0"], ps["b1"]]
        for cc in range(NCH):
            st, sp = cc == 0, cc == NCH - 1
            for n in range(2):
                sc.add("T", lambda e, wb=wb, cc=cc, n=n, st=st, sp=sp, pp=pp: e.matmul(
                    pp[n][:, :], wb[:, cc * 128:(cc + 1) * 128],
                    ygT[:, cc * 1024 + n * SL: cc * 1024 + (n + 1) * SL],
                    start=st, stop=sp),
                    reads=[A(wb, 0, cc * 128, 128, (cc + 1) * 128),
                           A(ygT, 0, cc * 1024 + n * SL, 128, cc * 1024 + (n + 1) * SL)],
                    writes=[A(pp[n])], tag=f"womm{oc}_{cc}_{n}")
        for n in range(2):
            ao = ao_tmp[n]
            sc.add("V", lambda e, pp=pp, n=n, ao=ao: e.tensor_copy(
                out=ao[:, :], in_=pp[n][:, :]),
                reads=[A(pp[n])], writes=[A(ao)])
            for ts in range(4):
                tc = n * 4 + ts
                pm = ps["m0"] if tc % 2 == 0 else ps["m1"]
                sc.add("T", lambda e, ao=ao, ts=ts, pm=pm: e.transpose(
                    pm[:, 0:128], ao[:, ts * 128:(ts + 1) * 128], id_f32[:, :]),
                    reads=[A(ao, 0, ts * 128, 128, (ts + 1) * 128), A(id_f32)],
                    writes=[A(pm, 0, 0, 128, 128)], tag=f"aotr{oc}_{tc}")
                xc0 = tc * D + oc * 128
                sc.add("V", lambda e, pm=pm, xc0=xc0: e.tensor_add(
                    x2tok[:, xc0:xc0 + 128], x2tok[:, xc0:xc0 + 128],
                    pm[:, 0:128]),
                    reads=[A(pm, 0, 0, 128, 128),
                           A(x2tok, 0, xc0, 128, xc0 + 128)],
                    writes=[A(x2tok, 0, xc0, 128, xc0 + 128)])

    if tapd("x1", [128, NTC * D]) is not None:
        dma("S", taps["x1"], x2tok[:, :], in_acc=[A(x2tok)])

    def emit_ln(tc, src_col0, write_bf16):
        c0 = src_col0
        sc.add("V", lambda e, c0=c0: e.tensor_reduce(
            statln[:, 0:1], x2tok[:, c0:c0 + D], AX.X, ALU.add),
            reads=[A(x2tok, 0, c0, 128, c0 + D)],
            writes=[A(statln, 0, 0, 128, 1)], sync=True, sync_before=True)
        sc.add("V", lambda e: e.tensor_scalar_mul(
            statln[:, 1:2], statln[:, 0:1], 1.0 / D),
            reads=[A(statln, 0, 0, 128, 1)], writes=[A(statln, 0, 1, 128, 2)])
        sc.add("A", lambda e, c0=c0: e.activation(
            out=sqd[:, :], in_=x2tok[:, c0:c0 + D], func=AF.Square),
            reads=[A(x2tok, 0, c0, 128, c0 + D)], writes=[A(sqd)])
        sc.add("V", lambda e: e.tensor_reduce(
            statln[:, 2:3], sqd[:, :], AX.X, ALU.add),
            reads=[A(sqd)], writes=[A(statln, 0, 2, 128, 3)], sync=True, sync_before=True)
        sc.add("V", lambda e: e.tensor_scalar_mul(
            statln[:, 2:3], statln[:, 2:3], 1.0 / D),
            reads=[A(statln, 0, 2, 128, 3)], writes=[A(statln, 0, 2, 128, 3)])
        sc.add("V", lambda e: e.tensor_mul(
            statln[:, 3:4], statln[:, 1:2], statln[:, 1:2]),
            reads=[A(statln, 0, 1, 128, 2)], writes=[A(statln, 0, 3, 128, 4)])
        sc.add("V", lambda e: e.tensor_sub(
            statln[:, 2:3], statln[:, 2:3], statln[:, 3:4]),
            reads=[A(statln, 0, 2, 128, 4)], writes=[A(statln, 0, 2, 128, 3)])
        sc.add("V", lambda e: e.tensor_scalar_add(
            statln[:, 2:3], statln[:, 2:3], EPS_LN),
            reads=[A(statln, 0, 2, 128, 3)], writes=[A(statln, 0, 2, 128, 3)])
        sc.add("A", lambda e: e.activation(
            out=statln[:, 4:5], in_=statln[:, 2:3], func=AF.Ln),
            reads=[A(statln, 0, 2, 128, 3)], writes=[A(statln, 0, 4, 128, 5)],
            sync=True, sync_before=True)
        sc.add("A", lambda e: e.activation(
            out=statln[:, 5:6], in_=statln[:, 4:5], func=AF.Exp, scale=-0.5),
            reads=[A(statln, 0, 4, 128, 5)], writes=[A(statln, 0, 5, 128, 6)],
            sync=True, sync_before=True)
        sc.add("V", lambda e, c0=c0: e.tensor_scalar(
            out=x2tok[:, c0:c0 + D], in0=x2tok[:, c0:c0 + D],
            scalar1=statln[:, 1:2], scalar2=statln[:, 5:6],
            op0=ALU.subtract, op1=ALU.mult),
            reads=[A(x2tok, 0, c0, 128, c0 + D), A(statln, 0, 1, 128, 6)],
            writes=[A(x2tok, 0, c0, 128, c0 + D)])
        if write_bf16:
            sc.add("V", lambda e, c0=c0: e.tensor_copy(
                out=x2btok[:, :], in_=x2tok[:, c0:c0 + D]),
                reads=[A(x2tok, 0, c0, 128, c0 + D)], writes=[A(x2btok)])

    for tc in range(NTC):
        emit_ln(tc, tc * D, True)
        for ic in range(NCH):
            pt = ps["t0"] if (tc * NCH + ic) % 2 == 0 else ps["t1"]
            sc.add("T", lambda e, pt=pt, ic=ic: e.transpose(
                pt[:, 0:128], x2btok[:, ic * 128:(ic + 1) * 128], id_bf[:, :]),
                reads=[A(x2btok, 0, ic * 128, 128, (ic + 1) * 128), A(id_bf)],
                writes=[A(pt, 0, 0, 128, 128)], tag=f"x2tr{tc}_{ic}")
            d0 = ic * 1024 + tc * 128
            sc.add("V", lambda e, pt=pt, d0=d0: e.tensor_copy(
                out=x2bf[:, d0:d0 + 128], in_=pt[:, 0:128]),
                reads=[A(pt, 0, 0, 128, 128)],
                writes=[A(x2bf, 0, d0, 128, d0 + 128)])

    if tapd("x2bf", [128, NCH * 1024], BF16) is not None:
        dma("S", taps["x2bf"], x2bf[:, :], in_acc=[A(x2bf)])

    for sl in range(2):
        for hcp in range(64):
            gblk = egu[(2 * hcp) % 4]
            ublk = egu[(2 * hcp + 1) % 4]
            gh, uh = hcp, 64 + hcp
            dma("S", gblk[:, :], guq_g.ap()[gh // 16, gh % 16],
                in_acc=[A(guq_g)], out_acc=[A(gblk)], tag=f"gu_g{sl}_{hcp}")
            dma("S", ublk[:, :], guq_g.ap()[uh // 16, uh % 16],
                in_acc=[A(guq_g)], out_acc=[A(ublk)], tag=f"gu_u{sl}_{hcp}")
            pg = ps["a0"] if hcp % 2 == 0 else ps["a1"]
            pu = ps["b0"] if hcp % 2 == 0 else ps["b1"]
            for ic in range(NCH):
                st, sp = ic == 0, ic == NCH - 1
                xs0 = ic * 1024 + sl * SL
                sc.add("T", lambda e, gblk=gblk, ic=ic, st=st, sp=sp, pg=pg, xs0=xs0:
                       e.matmul(pg[:, :], gblk[:, ic * 128:(ic + 1) * 128],
                                x2bf[:, xs0:xs0 + SL], start=st, stop=sp),
                       reads=[A(gblk, 0, ic * 128, 128, (ic + 1) * 128),
                              A(x2bf, 0, xs0, 128, xs0 + SL)],
                       writes=[A(pg)], tag=f"gmm{sl}_{hcp}_{ic}")
                sc.add("T", lambda e, ublk=ublk, ic=ic, st=st, sp=sp, pu=pu, xs0=xs0:
                       e.matmul(pu[:, :], ublk[:, ic * 128:(ic + 1) * 128],
                                x2bf[:, xs0:xs0 + SL], start=st, stop=sp),
                       reads=[A(ublk, 0, ic * 128, 128, (ic + 1) * 128),
                              A(x2bf, 0, xs0, 128, xs0 + SL)],
                       writes=[A(pu)], tag=f"umm{sl}_{hcp}_{ic}")
            sc.add("A", lambda e, pg=pg: e.activation(
                out=sgl_t[:, :], in_=pg[:, :], func=AF.Silu),
                reads=[A(pg)], writes=[A(sgl_t)])
            hdst = h_lo if hcp < 32 else h_hi
            hc0 = (hcp % 32) * SL
            sc.add("V", lambda e, pu=pu, hdst=hdst, hc0=hc0: e.tensor_tensor(
                out=hdst[:, hc0:hc0 + SL], in0=sgl_t[:, :], in1=pu[:, :],
                op=ALU.mult),
                reads=[A(sgl_t), A(pu)],
                writes=[A(hdst, 0, hc0, 128, hc0 + SL)])
            sc.add("V", lambda e, hdst=hdst, hc0=hc0: e.tensor_mul(
                h2_t[:, :], hdst[:, hc0:hc0 + SL], hdst[:, hc0:hc0 + SL]),
                reads=[A(hdst, 0, hc0, 128, hc0 + SL)], writes=[A(h2_t)])
            st, sp = hcp == 0, hcp == 63
            sc.add("T", lambda e, st=st, sp=sp: e.matmul(
                ps["m0"][0:1, 0:SL], onescol[:, :], h2_t[:, :],
                start=st, stop=sp),
                reads=[A(onescol), A(h2_t)],
                writes=[A(ps["m0"], 0, 0, 1, SL)], tag=f"hsq{sl}_{hcp}")
        sc.add("V", lambda e: e.tensor_copy(out=inv_row[:, :], in_=ps["m0"][0:1, 0:SL]),
            reads=[A(ps["m0"], 0, 0, 1, SL)], writes=[A(inv_row, 0, 0, 1, SL)])
        sc.add("V", lambda e: e.tensor_scalar(
            out=inv_row[:, :], in0=inv_row[:, :], scalar1=1.0 / 8192,
            scalar2=EPS_RMS, op0=ALU.mult, op1=ALU.add),
            reads=[A(inv_row, 0, 0, 1, SL)], writes=[A(inv_row, 0, 0, 1, SL)])
        sc.add("A", lambda e: e.activation(
            out=inv_row[:, :], in_=inv_row[:, :], func=AF.Ln),
            reads=[A(inv_row, 0, 0, 1, SL)], writes=[A(inv_row, 0, 0, 1, SL)],
            sync=True, sync_before=True)
        sc.add("A", lambda e: e.activation(
            out=inv_row[:, :], in_=inv_row[:, :], func=AF.Exp, scale=-0.5),
            reads=[A(inv_row, 0, 0, 1, SL)], writes=[A(inv_row, 0, 0, 1, SL)],
            sync=True, sync_before=True)
        sc.add("T", lambda e: e.matmul(
            ps["m1"][:, 0:SL], onesrow[:, :], inv_row[:, :],
            start=True, stop=True),
            reads=[A(onesrow, 0, 0, 1, 128), A(inv_row, 0, 0, 1, SL)],
            writes=[A(ps["m1"])], tag=f"invb{sl}")
        sc.add("V", lambda e: e.tensor_copy(out=inv_bc[:, :], in_=ps["m1"][:, 0:SL]),
            reads=[A(ps["m1"])], writes=[A(inv_bc)])
        for oc in range(NCH):
            for rr in range(NCORES):
                dma("S", eph[rr // 4][:, (rr % 4) * 1024:(rr % 4 + 1) * 1024],
                    pjq_g.ap()[rr, oc], in_acc=[A(pjq_g)],
                    out_acc=[A(eph[rr // 4], 0, (rr % 4) * 1024, 128,
                               (rr % 4 + 1) * 1024)],
                    tag=f"pj{sl}_{oc}_{rr}")
            pp = ps["a0"] if oc % 2 == 0 else ps["b0"]
            for hc in range(64):
                st, sp = hc == 0, hc == 63
                hsrc = h_lo if hc < 32 else h_hi
                hc0 = (hc % 32) * SL
                eb = eph[hc // 32]
                ec0 = (hc % 32) * 128
                sc.add("T", lambda e, st=st, sp=sp, pp=pp, hsrc=hsrc,
                       hc0=hc0, eb=eb, ec0=ec0: e.matmul(
                           pp[:, :], eb[:, ec0:ec0 + 128],
                           hsrc[:, hc0:hc0 + SL], start=st, stop=sp),
                       reads=[A(eb, 0, ec0, 128, ec0 + 128),
                              A(hsrc, 0, hc0, 128, hc0 + SL)],
                       writes=[A(pp)], tag=f"pmm{sl}_{oc}_{hc}")
            sc.add("V", lambda e, pp=pp: e.tensor_mul(
                fo_t[:, :], pp[:, :], inv_bc[:, :]),
                reads=[A(pp), A(inv_bc)], writes=[A(fo_t)])
            for ts in range(4):
                tc = sl * 4 + ts
                pm = ps["m0"] if ts % 2 == 0 else ps["m1"]
                sc.add("T", lambda e, ts=ts, pm=pm: e.transpose(
                    pm[:, 0:128], fo_t[:, ts * 128:(ts + 1) * 128], id_f32[:, :]),
                    reads=[A(fo_t, 0, ts * 128, 128, (ts + 1) * 128), A(id_f32)],
                    writes=[A(pm, 0, 0, 128, 128)], tag=f"ftr{sl}_{oc}_{ts}")
                xc0 = tc * D + oc * 128
                sc.add("V", lambda e, pm=pm, xc0=xc0: e.tensor_add(
                    x2tok[:, xc0:xc0 + 128], x2tok[:, xc0:xc0 + 128],
                    pm[:, 0:128]),
                    reads=[A(pm, 0, 0, 128, 128), A(x2tok, 0, xc0, 128, xc0 + 128)],
                    writes=[A(x2tok, 0, xc0, 128, xc0 + 128)])
        for ts in range(4):
            tc = sl * 4 + ts
            emit_ln(tc, tc * D, False)
            dma("S", out_d[tc * 128:(tc + 1) * 128, :], x2tok[:, tc * D:(tc + 1) * D],
                in_acc=[A(x2tok, 0, tc * D, 128, (tc + 1) * D)],
                tag=f"out{tc}")

    import contextlib
    with contextlib.ExitStack() as stack:
        sems = {e: stack.enter_context(nc.semaphore(f"sem{e}")) for e in ENGS}
        dsems = {e: [stack.enter_context(nc.semaphore(f"dsem{e}{i}"))
                     for i in range(Sched.NSLOT)] for e in ENGS}
        blk = stack.enter_context(nc.Block())
        finals = sc.finalize(blk, sems, dsems)

        @blk.sync
        def _(sy):
            for s, v in finals["S"]:
                sy.wait_ge(dsems["S"][s], v)
            for s, v in finals["G"]:
                sy.wait_ge(dsems["G"][s], v)

    return nc, taps



def _layoutA(w):
    wr = w.reshape(NCH, 128, NCH, 128)
    return np.ascontiguousarray(np.transpose(wr, (0, 3, 2, 1)))


def _stage_core(ins, cid):
    b, q = cid // 4, cid % 4
    t0 = q * TL
    x = ins["L0_attn_x"]
    state = ins["L0_attn_state"].reshape(D)
    prev = state if q == 0 else x[b, t0 - 1]
    xe = np.concatenate([prev[None, :], x[b, t0:t0 + TL]], axis=0)
    xfT = np.ascontiguousarray(xe.T)
    x_blk = np.ascontiguousarray(xfT.reshape(NCH, 128, TL + 1))
    x_tok = np.ascontiguousarray(x[b, t0:t0 + TL])

    def pp(v):
        return np.ascontiguousarray(v.reshape(NCH, 128).T)

    wkv = ins["L0_attn_wkvstate"]
    wbd = np.zeros((128, HP * 128), np.float32)
    for hp in range(HP):
        wbd[0:64, hp * 128:hp * 128 + 64] = wkv[2 * hp]
        wbd[64:128, hp * 128 + 64:(hp + 1) * 128] = wkv[2 * hp + 1]

    t_loc = np.arange(TL, dtype=np.float32)
    iota_fwd = np.broadcast_to(t0 + t_loc, (128, TL)).astype(np.float32)
    iota_rev = np.broadcast_to(TG - 1 - t0 - t_loc, (128, TL)).astype(np.float32)

    m = (np.arange(128)[:, None] < np.arange(128)[None, :]).astype(np.float32)
    mask2 = np.concatenate([m, m], axis=1).astype(BF)

    obd2 = np.zeros((128, 2), BF)
    obd2[0:64, 0] = BF(1.0)
    obd2[64:128, 1] = BF(1.0)

    pm = np.zeros((8,), np.float32)
    for r in range(8):
        if r // 4 == b and r % 4 < q:
            pm[r] = 1.0
    pmask = np.broadcast_to(pm, (128, 8)).astype(np.float32)

    gu = ins["L0_ffn_w_gate_up"]
    pj = ins["L0_ffn_w_proj"]
    gu_sl = gu[2048 * cid: 2048 * (cid + 1)]
    pj_sl = pj[:, 1024 * cid: 1024 * (cid + 1)]

    d = {
        "x_blk": np.ascontiguousarray(x_blk),
        "x_tok": x_tok,
        "wr_blk": _layoutA(ins["L0_attn_w_r"]),
        "wk_blk": _layoutA(ins["L0_attn_w_k"]),
        "wv_blk": _layoutA(ins["L0_attn_w_v"]),
        "wg_blk": _layoutA(ins["L0_attn_w_g"]),
        "wo_blk": _layoutA(ins["L0_attn_w_o"]),
        "guq_in": np.ascontiguousarray(gu_sl.reshape(16, 128, D)),
        "pjq_in": np.ascontiguousarray(pj_sl.reshape(16, 128, 1024)),
        "tmk_pp": pp(ins["L0_attn_time_mix_k"].reshape(D)),
        "tmv_pp": pp(ins["L0_attn_time_mix_v"].reshape(D)),
        "tmr_pp": pp(ins["L0_attn_time_mix_r"].reshape(D)),
        "tmg_pp": pp(ins["L0_attn_time_mix_g"].reshape(D)),
        "decay_pp": pp(ins["L0_attn_time_decay"].reshape(D)),
        "u_pp": pp(ins["L0_attn_time_faaaa"].reshape(D)),
        "wkvbd": wbd,
        "gam_rep": np.broadcast_to(ins["L0_attn_ln_w"], (128, D)).astype(np.float32),
        "bet_rep": np.broadcast_to(ins["L0_attn_ln_b"], (128, D)).astype(np.float32),
        "iota_fwd": iota_fwd,
        "iota_rev": iota_rev,
        "mask2": mask2,
        "id_bf": np.eye(128, dtype=BF),
        "id_f32": np.eye(128, dtype=np.float32),
        "ones_bd2": obd2,
        "ones_col": np.ones((128, 1), BF),
        "ones_row": np.ones((1, 128), np.float32),
        "pmask_pp": pmask,
    }
    return {k: np.ascontiguousarray(v) for k, v in d.items()}


def kernel(**inputs):
    nc, _ = build_program()
    in_maps = [_stage_core(inputs, cid) for cid in range(NCORES)]
    res = run_bass_kernel_spmd(nc, in_maps, core_ids=list(range(NCORES)))
    out = np.empty((B, TG, D), np.float32)
    for cid in range(NCORES):
        b, q = cid // 4, cid % 4
        out[b, q * TL:(q + 1) * TL] = res.results[cid]["out"]
    return out


# version 07
# speedup vs baseline: 1.1501x; 1.0720x over previous
import numpy as np
import ml_dtypes
import concourse.bass as bass
import concourse.mybir as mybir
from concourse.bass_utils import run_bass_kernel_spmd

F32 = mybir.dt.float32
BF16 = mybir.dt.bfloat16
AF = mybir.ActivationFunctionType
ALU = mybir.AluOpType
AX = mybir.AxisListType
BF = ml_dtypes.bfloat16

NCORES = 8
B, TG, D = 2, 4096, 2048
TL = 1024
NCH = 16
HP = 16
CK = 128
NCK = TL // CK
SL = 512
NTC = 8
EPS_GN = 0.00064
EPS_LN = 1e-5
EPS_RMS = float(np.finfo(np.float32).eps)
KB = 1024
ENGS = ("T", "V", "A", "S", "G")



class Op:
    __slots__ = ("fn", "reads", "writes", "dma", "tag", "idx", "needed", "cnt",
                 "sync", "sync_before")

    def __init__(self, fn, reads, writes, dma, tag, sync=False, sync_before=False):
        self.fn = fn
        self.reads = reads
        self.writes = writes
        self.dma = dma
        self.tag = tag
        self.idx = -1
        self.needed = False
        self.cnt = 0
        self.sync = sync
        self.sync_before = sync_before


def _overlap(a, b):
    return a[0] < b[2] and b[0] < a[2] and a[1] < b[3] and b[1] < a[3]


class Sched:

    def __init__(self):
        self.ops = {e: [] for e in ENGS}
        self.writers = {}
        self.readers = {}
        self.deps = {}
        self.arena = {}
        self.psum = set()

    def reg_arena(self, handle, byte_lo, byte_hi, esz):
        self.arena[id(handle)] = (byte_lo, byte_hi, esz)

    def reg_psum(self, handle):
        self.psum.add(id(handle))

    def _norm(self, acc):
        h, p_lo, c_lo, p_hi, c_hi = acc
        hid = id(h)
        if hid in self.psum:
            return ("PSUM", hid, 0, 0, 128, 1 << 30)
        if hid in self.arena:
            lo, hi, esz = self.arena[hid]
            b_lo = min(lo + c_lo * esz, hi)
            b_hi = min(lo + (c_hi * esz if c_hi < (1 << 29) else hi), hi)
            return ("ARENA", "ARENA", p_lo, b_lo, p_hi, b_hi)
        return ("H", hid, p_lo, c_lo, p_hi, c_hi)

    def add(self, eng, fn, reads=(), writes=(), dma=False, tag="", sync=False,
            sync_before=False):
        op = Op(fn, [self._norm(a) for a in reads],
                [self._norm(a) for a in writes], dma, tag, sync, sync_before)
        op.idx = len(self.ops[eng])
        self.ops[eng].append(op)
        key = (eng, op.idx)
        dep = set()
        for acc in op.reads:
            for (a2, e2, i2) in self.writers.get(acc[1], ()):
                if _overlap(acc[2:], a2[2:]):
                    dep.add((e2, i2))
        for acc in op.writes:
            for (a2, e2, i2) in self.writers.get(acc[1], ()):
                if _overlap(acc[2:], a2[2:]):
                    dep.add((e2, i2))
            for (a2, e2, i2) in self.readers.get(acc[1], ()):
                if _overlap(acc[2:], a2[2:]):
                    dep.add((e2, i2))
        for acc in op.reads:
            self.readers.setdefault(acc[1], []).append((acc, eng, op.idx))
        for acc in op.writes:
            self.writers.setdefault(acc[1], []).append((acc, eng, op.idx))
        self.deps[key] = dep
        return key

    NSLOT = 4

    def finalize(self, blk, sems, dsems):
        for e in ENGS:
            for op in self.ops[e]:
                if op.dma:
                    op.needed = True
                for (e2, i2) in self.deps[(e, op.idx)]:
                    p = self.ops[e2][i2]
                    if e2 != e or p.dma or p.sync or op.sync_before:
                        p.needed = True
        for e in ENGS:
            c = 0
            nd = 0
            for op in self.ops[e]:
                if op.dma:
                    op.cnt = (nd % self.NSLOT, 16 * (nd // self.NSLOT + 1))
                    nd += 1
                else:
                    if op.needed:
                        c += 1
                    op.cnt = c

        def emitter(e):
            def run(eng):
                hw = {}
                for op in self.ops[e]:
                    waits = {}
                    for (e2, i2) in self.deps[(e, op.idx)]:
                        p = self.ops[e2][i2]
                        if e2 == e and not (p.dma or p.sync or op.sync_before):
                            continue
                        if p.dma:
                            key = (e2, p.cnt[0])
                            waits[key] = max(waits.get(key, 0), p.cnt[1])
                        else:
                            waits[(e2, -1)] = max(waits.get((e2, -1), 0), p.cnt)
                    if op.dma:
                        slot, v = op.cnt
                        if v > 16:
                            waits[(e, slot)] = max(waits.get((e, slot), 0), v - 16)
                    for (e2, slot), v in sorted(waits.items()):
                        if v > hw.get((e2, slot), 0):
                            eng.wait_ge(sems[e2] if slot < 0 else dsems[e2][slot], v)
                            hw[(e2, slot)] = v
                    instr = op.fn(eng)
                    if op.dma:
                        instr.then_inc(dsems[e][op.cnt[0]], 16)
                    elif op.needed:
                        instr.then_inc(sems[e], 1)
            return run

        blk.tensor(emitter("T"))
        blk.vector(emitter("V"))
        blk.scalar(emitter("A"))
        blk.sync(emitter("S"))
        blk.gpsimd(emitter("G"))
        fin = {}
        for e in ENGS:
            nd = sum(1 for op in self.ops[e] if op.dma)
            fin[e] = [(s, 16 * ((nd - 1 - s) // self.NSLOT + 1))
                      for s in range(min(nd, self.NSLOT))]
        return fin


def A(h, p_lo=0, c_lo=0, p_hi=128, c_hi=1 << 30):
    return (h, p_lo, c_lo, p_hi, c_hi)



def build_program(debug_taps=()):
    nc = bass.Bass("TRN2", target_bir_lowering=False)

    def din(name, shape, dt=F32):
        return nc.dram_tensor(name, list(shape), dt, kind="ExternalInput").ap()

    x_blk = din("x_blk", [NCH, 128, TL + 1])
    w_blk = {m: din(f"w{m}_blk", [NCH, 128, NCH, 128]) for m in "krvgo"}
    guq_in = din("guq_in", [16, 128, D])
    pjq_in = din("pjq_in", [16, 128, 1024])
    tm_d = {m: din(f"tm{m}_pp", [128, NCH]) for m in "kvrg"}
    decay_d = din("decay_pp", [128, NCH])
    u_d = din("u_pp", [128, NCH])
    wkvbd_d = din("wkvbd", [128, HP * 128])
    gam_d = din("gam_rep", [128, D])
    bet_d = din("bet_rep", [128, D])
    iota_f_d = din("iota_fwd", [128, TL])
    iota_r_d = din("iota_rev", [128, TL])
    mask2_d = din("mask2", [128, 256], BF16)
    id_bf_d = din("id_bf", [128, 128], BF16)
    id_f32_d = din("id_f32", [128, 128])
    onesbd2_d = din("ones_bd2", [128, 2], BF16)
    onescol_d = din("ones_col", [128, 1], BF16)
    onesrow_d = din("ones_row", [1, 128])
    pmask_d = din("pmask_pp", [128, 8])
    x_tok_d = din("x_tok", [TL, D])

    out_d = nc.dram_tensor("out", [TL, D], F32, kind="ExternalOutput").ap()

    guq_c = nc.dram_tensor("guq_c", [16, 128, D], BF16, kind="Internal")
    pjq_c = nc.dram_tensor("pjq_c", [16, 128, 1024], BF16, kind="Internal")
    guq_g = nc.dram_tensor("guq_g", [NCORES, 16, 128, D], BF16, kind="Internal")
    pjq_g = nc.dram_tensor("pjq_g", [NCORES, 16, 128, 1024], BF16, kind="Internal")
    sig_c = nc.dram_tensor("sig_c", [128, HP * 64], F32, kind="Internal")
    sig_g = nc.dram_tensor("sig_g", [NCORES, 128, HP * 64], F32, kind="Internal")

    taps = {}

    def tapd(name, shape, dt=F32):
        if name in debug_taps:
            taps[name] = nc.dram_tensor("tap_" + name, list(shape), dt,
                                        kind="ExternalOutput").ap()
        return taps.get(name)

    Sal = nc.alloc_sbuf_tensor
    wkvbd = Sal("wkvbd_sb", [128, HP * 128], BF16)
    lnw = Sal("lnw", [128, NCH], F32)
    dectmp = Sal("dectmp", [128, NCH], F32)
    u_sb = Sal("u_sb", [128, NCH], F32)
    tm_sb = {m: Sal(f"tm_{m}", [128, NCH], F32) for m in "kvrg"}
    pmask = Sal("pmask_sb", [128, 8], F32)
    id_bf = Sal("id_bf_sb", [128, 128], BF16)
    id_f32 = Sal("id_f32_sb", [128, 128], F32)
    mask2 = Sal("mask2_sb", [128, 256], BF16)
    onesbd2 = Sal("onesbd2_sb", [128, 2], BF16)
    onescol = Sal("onescol_sb", [128, 1], BF16)
    onesrow = Sal("onesrow_sb", [1, 128], F32)
    S_f32 = Sal("S_f32", [128, HP * 128], F32)
    S_bf = [Sal(f"S_bf{i}", [128, 128], BF16) for i in range(2)]
    bon = Sal("bon", [128, 256], F32)
    qsc = Sal("qsc", [128, 32], F32)
    qscb = Sal("qscb", [128, 32], BF16)
    qrc = Sal("qrc", [128, 32], F32)
    gstat = Sal("gstat", [128, 96], F32)
    statln = Sal("statln", [128, 8], F32)
    rw_t = [Sal(f"rw_t{i}", [128, 128], BF16) for i in range(2)]
    d_t = Sal("d_t", [128, SL], F32)
    kr_t1 = Sal("kr_t1", [128, TL], BF16)
    h2p = [Sal(f"h2p{i}", [128, SL], BF16) for i in range(2)]
    wfor_t = [Sal(f"wfor_t{i}", [128, 128], BF16) for i in range(2)]

    ARENA = 182 * KB
    arena = Sal("arena", [128, ARENA], mybir.dt.uint8)
    abase = nc.lookup_mloc(arena).addr
    sc = Sched()

    def at(name, off_kb, shape, dt):
        off = int(round(off_kb * KB))
        h = nc.alloc_sbuf_tensor_at(name, list(shape), dt, offset=abase + off)
        esz = mybir.dt.size(dt)
        sc.reg_arena(h, off, off + shape[1] * esz, esz)
        return h

    kwT = at("kwT", 0, [128, HP * 1024], BF16)
    rT = at("rT", 32, [128, HP * 1024], BF16)
    x2tok = at("x2tok", 0, [128, NTC * D], F32)
    lerpA = at("lerpA", 64, [128, NCH * 1024], BF16)
    v_tok = at("v_tok", 64, [128, NTC * D], BF16)
    ygT = at("ygT", 64, [128, NCH * 1024], BF16)
    sqd = at("sqd", 64, [128, D], F32)
    x2btok = at("x2btok", 72, [128, D], BF16)
    h_lo = at("h_lo", 64, [128, 32 * SL], BF16)
    g_tok = at("g_tok", 96, [128, NTC * D], BF16)
    qw = [at(f"qw{i}", 96 + 4 * i, [128, D], BF16) for i in range(2)]
    qx = at("qx", 104, [128, D], BF16)
    qa = at("qa", 108, [128, D], BF16)
    qs = at("qs", 112, [128, D], BF16)
    wback = [at(f"wback{i}", 116 + 4 * i, [128, TL], F32) for i in range(2)]
    ku_t = at("ku_t", 124, [128, TL], BF16)
    kr_t = [at("kr_t0", 126, [128, TL], BF16), kr_t1]
    vgT_t = [at("vgT_t0", 160, [128, TL], BF16),
             at("vgT_t1", 162.25, [128, TL], BF16)]
    h_hi = at("h_hi", 96, [128, 32 * SL], BF16)
    lerpB = at("lerpB", 128, [128, NCH * 1024], BF16)
    sig_sb = at("sig_sb", 128, [128, HP * 64], F32)
    wkv_t = [at(f"wkv_t{i}", 128 + 4 * i, [128, D], BF16) for i in range(2)]
    yc_t = at("yc_t", 136, [128, D], F32)
    slab_t = at("slab_t", 144, [128, HP * 64], F32)

    sqc_t = at("sqc_t", 152, [128, D], BF16)
    rbd = [at(f"rbd{i}", 156 + 0.5 * i, [128, 256], BF16) for i in range(2)]
    attT = [at(f"attT{i}", 157 + 0.5 * i, [128, 256], BF16) for i in range(2)]
    kwtok = [at(f"kwtok{i}", 158 + 0.25 * i, [128, 128], BF16) for i in range(2)]
    x2bf = at("x2bf", 128, [128, NCH * 1024], BF16)
    gamb = at("gamb", 160, [128, D], F32)
    betb = at("betb", 168, [128, D], F32)
    ao_tmp = [at(f"ao_tmp{i}", 160 + 2 * i, [128, SL], F32) for i in range(2)]
    sgl_t = [at(f"sgl_t{i}", 160 + 2 * i, [128, SL], F32) for i in range(2)]
    fo_t = at("fo_t", 160, [128, SL], F32)
    inv_row = at("inv_row", 162, [1, SL], F32)
    inv_bc = at("inv_bc", 164, [128, SL], F32)
    xst = [at(f"xst{i}", 160 + 2.25 * i, [128, 513], F32) for i in range(2)]
    iota_rev = at("iota_rev_sb", 128, [128, TL], F32)
    iota_fwd = at("iota_fwd_sb", 176, [128, TL], F32)
    egu = [at(f"egu{i}", 166 + 4 * i, [128, D], BF16) for i in range(4)]
    wstf32 = [at(f"wstf32{i}", 166 + 8 * i, [128, D], F32) for i in range(2)]
    wstb = [at(f"wstb{i}", 166 + 8 * i, [128, D], BF16) for i in range(2)]
    eph = [at(f"eph{i}", 166 + 8 * i, [128, 4096], BF16) for i in range(2)]

    ps = {
        "a0": nc.alloc_psum_tensor("ps_a0", [128, SL], F32),
        "a1": nc.alloc_psum_tensor("ps_a1", [128, SL], F32),
        "b0": nc.alloc_psum_tensor("ps_b0", [128, SL], F32),
        "b1": nc.alloc_psum_tensor("ps_b1", [128, SL], F32),
        "t0": nc.alloc_psum_tensor("ps_t0", [128, 1024], BF16),
        "t1": nc.alloc_psum_tensor("ps_t1", [128, 1024], BF16),
        "m0": nc.alloc_psum_tensor("ps_m0", [128, SL], F32),
        "m1": nc.alloc_psum_tensor("ps_m1", [128, SL], F32),
    }
    for h in ps.values():
        sc.reg_psum(h)

    def dma(eng, out_ap, in_ap, out_acc=None, in_acc=None, tag=""):
        def f(e, out_ap=out_ap, in_ap=in_ap):
            return e.dma_start(out=out_ap, in_=in_ap)
        sc.add(eng, f, reads=(in_acc or []), writes=(out_acc or []),
               dma=True, tag=tag)

    dma("S", iota_rev[:, :], iota_r_d, out_acc=[A(iota_rev)], tag="ld_iota_r")
    dma("S", lnw[:, :], decay_d, out_acc=[A(lnw)], tag="ld_decay")
    dma("S", u_sb[:, :], u_d, out_acc=[A(u_sb)])
    for m in "kvrg":
        dma("S", tm_sb[m][:, :], tm_d[m], out_acc=[A(tm_sb[m])])
    dma("S", pmask[:, :], pmask_d, out_acc=[A(pmask)])
    dma("S", id_bf[:, :], id_bf_d, out_acc=[A(id_bf)])
    dma("S", id_f32[:, :], id_f32_d, out_acc=[A(id_f32)])
    dma("S", mask2[:, :], mask2_d, out_acc=[A(mask2)])
    dma("S", onesbd2[:, :], onesbd2_d, out_acc=[A(onesbd2)])
    dma("S", onescol[:, :], onescol_d, out_acc=[A(onescol)])
    dma("S", onesrow[:, :], onesrow_d, out_acc=[A(onesrow, 0, 0, 1, 128)])
    dma("G", wkvbd[:, :], wkvbd_d,
        out_acc=[A(wkvbd)], tag="ld_wkvbd")

    sc.add("A", lambda e: e.activation(out=dectmp[:, :], in_=lnw[:, :], func=AF.Exp),
           reads=[A(lnw)], writes=[A(dectmp)])
    sc.add("V", lambda e: e.tensor_scalar_mul(lnw[:, :], dectmp[:, :], -1.0),
           reads=[A(dectmp)], writes=[A(lnw)])

    def emit_lerp(mkey, dest):
        tm = tm_sb[mkey]
        for h2 in range(2):
            for ic in range(NCH):
                i = (h2 * NCH + ic) % 2
                xs = xst[i]
                dma("S", xs[:, :], x_blk[ic, :, h2 * SL:h2 * SL + SL + 1],
                    out_acc=[A(xs)], tag=f"x_{mkey}_{h2}_{ic}")
                sc.add("V", lambda e, xs=xs: e.tensor_sub(
                    d_t[:, :], xs[:, 1:SL + 1], xs[:, 0:SL]),
                    reads=[A(xs)], writes=[A(d_t)])
                c0 = ic * 1024 + h2 * SL
                sc.add("V", lambda e, xs=xs, ic=ic, c0=c0: e.scalar_tensor_tensor(
                    out=dest[:, c0:c0 + SL], in0=d_t[:, :],
                    scalar=tm[:, ic:ic + 1], in1=xs[:, 0:SL],
                    op0=ALU.mult, op1=ALU.add),
                    reads=[A(d_t), A(tm), A(xs)],
                    writes=[A(dest, 0, c0, 128, c0 + SL)])

    def emit_proj(mat, src, evac, accs=("a0", "a1", "b0", "b1"), per_oc=None):
        for oc in range(NCH):
            wf = wstf32[oc % 2]
            wb = wstb[oc % 2]
            dma("S", wf[:, :], w_blk[mat][oc].rearrange("p a b -> p (a b)"),
                out_acc=[A(wf)], tag=f"w{mat}{oc}")
            sc.add("A", lambda e, wf=wf, wb=wb: e.activation(
                out=wb[:, :], in_=wf[:, :], func=AF.Copy),
                reads=[A(wf)], writes=[A(wb)], tag=f"c{mat}{oc}")
            pp = [ps[accs[2 * (oc % 2)]], ps[accs[2 * (oc % 2) + 1]]]
            for ic in range(NCH):
                st, sp = ic == 0, ic == NCH - 1
                for n in range(2):
                    sc.add("T", lambda e, wb=wb, ic=ic, n=n, st=st, sp=sp, pp=pp: e.matmul(
                        pp[n][:, :], wb[:, ic * 128:(ic + 1) * 128],
                        src[:, ic * 1024 + n * SL: ic * 1024 + (n + 1) * SL],
                        start=st, stop=sp),
                        reads=[A(wb, 0, ic * 128, 128, (ic + 1) * 128),
                               A(src, 0, ic * 1024 + n * SL, 128, ic * 1024 + (n + 1) * SL)],
                        writes=[A(pp[n])], tag=f"{mat}mm{oc}_{ic}_{n}")
            evac(oc, pp)
            if per_oc is not None:
                per_oc(oc)

    emit_lerp("r", lerpA)

    def r_evac(oc, pp):
        for n in range(2):
            c0 = oc * 1024 + n * SL
            sc.add("V", lambda e, pp=pp, n=n, c0=c0: e.tensor_copy(
                out=rT[:, c0:c0 + SL], in_=pp[n][:, :]),
                reads=[A(pp[n])], writes=[A(rT, 0, c0, 128, c0 + SL)])

    def quant_tile(t):
        isgu = t < 16
        W = D if isgu else 1024
        ngrp = W // 64
        ntr = W // 128
        qwb = qw[t % 2]
        dma("G", qwb[:, 0:W], (guq_in[t] if isgu else pjq_in[t - 16]),
            out_acc=[A(qwb, 0, 0, 128, W)], tag=f"q_in{t}")
        a3 = qwb.ap()[:, 0:W].rearrange("p (g e) -> p g e", e=64)
        sc.add("V", lambda e, a3=a3, ngrp=ngrp: e.tensor_reduce(
            qsc[:, 0:ngrp], a3, AX.X, ALU.add, apply_absolute_value=True),
            reads=[A(qwb, 0, 0, 128, W)], writes=[A(qsc)], sync=True, sync_before=True)
        sc.add("G", lambda e, ngrp=ngrp: e.tensor_scalar(
            out=qscb[:, 0:ngrp], in0=qsc[:, 0:ngrp], scalar1=1.0 / 64,
            scalar2=1e-8, op0=ALU.mult, op1=ALU.max),
            reads=[A(qsc)], writes=[A(qscb)])
        sc.add("V", lambda e, ngrp=ngrp: e.reciprocal(qrc[:, 0:ngrp], qscb[:, 0:ngrp]),
            reads=[A(qscb)], writes=[A(qrc)], sync=True, sync_before=True)
        rbc = qrc.ap()[:, 0:ngrp].unsqueeze(2).broadcast_to([128, ngrp, 64])
        sbc = qscb.ap()[:, 0:ngrp].unsqueeze(2).broadcast_to([128, ngrp, 64])
        qx3 = qx.ap()[:, 0:W].rearrange("p (g e) -> p g e", e=64)
        qa3 = qa.ap()[:, 0:W].rearrange("p (g e) -> p g e", e=64)
        qw3 = qwb.ap()[:, 0:W].rearrange("p (g e) -> p g e", e=64)
        sc.add("G", lambda e, qx3=qx3, qw3=qw3, rbc=rbc: e.tensor_tensor(
            out=qx3, in0=qw3, in1=rbc, op=ALU.mult),
            reads=[A(qwb, 0, 0, 128, W), A(qrc)], writes=[A(qx, 0, 0, 128, W)])
        sc.add("V", lambda e, qa3=qa3, qx3=qx3, sbc=sbc: e.scalar_tensor_tensor(
            out=qa3, in0=qx3, scalar=0.5, in1=sbc, op0=ALU.is_gt, op1=ALU.mult),
            reads=[A(qx, 0, 0, 128, W), A(qscb)], writes=[A(qa, 0, 0, 128, W)])
        sc.add("V", lambda e, qx3=qx3, sbc=sbc: e.scalar_tensor_tensor(
            out=qx3, in0=qx3, scalar=-0.5, in1=sbc, op0=ALU.is_lt, op1=ALU.mult),
            reads=[A(qx, 0, 0, 128, W), A(qscb)], writes=[A(qx, 0, 0, 128, W)])
        sc.add("G", lambda e, W=W: e.tensor_sub(
            qa.ap()[:, 0:W], qa.ap()[:, 0:W], qx.ap()[:, 0:W]),
            reads=[A(qa, 0, 0, 128, W), A(qx, 0, 0, 128, W)],
            writes=[A(qa, 0, 0, 128, W)])
        for k in range(ntr):
            pt = ps["t0"] if k % 2 == 0 else ps["t1"]
            sc.add("T", lambda e, k=k, pt=pt: e.transpose(
                pt[:, 0:128], qa[:, k * 128:(k + 1) * 128], id_bf[:, :]),
                reads=[A(qa, 0, k * 128, 128, (k + 1) * 128), A(id_bf)],
                writes=[A(pt, 0, 0, 128, 128)], tag=f"qtr{t}_{k}")
            sc.add("A", lambda e, k=k, pt=pt: e.activation(
                out=qs[:, k * 128:(k + 1) * 128], in_=pt[:, 0:128], func=AF.Copy),
                reads=[A(pt, 0, 0, 128, 128)],
                writes=[A(qs, 0, k * 128, 128, (k + 1) * 128)])
        dma("G", (guq_c.ap()[t] if isgu else pjq_c.ap()[t - 16]), qs[:, 0:W],
            in_acc=[A(qs, 0, 0, 128, W)],
            out_acc=[A(guq_c if isgu else pjq_c)], tag=f"q_st{t}")

    def coll1a(e):
        return e.collective_compute(
            "AllGather", ALU.bypass, replica_groups=[list(range(NCORES))],
            ins=[guq_c.ap().opt()], outs=[guq_g.ap().opt()])

    def coll1b(e):
        return e.collective_compute(
            "AllGather", ALU.bypass, replica_groups=[list(range(NCORES))],
            ins=[pjq_c.ap().opt()], outs=[pjq_g.ap().opt()])
    emit_proj("r", lerpA, r_evac, per_oc=lambda oc: quant_tile(oc))

    emit_lerp("k", lerpA)

    def k_evac(oc, pp):
        wbk = wback[oc % 2]
        sc.add("A", lambda e, wbk=wbk, oc=oc: e.activation(
            out=wbk[:, :], in_=iota_rev[:, :], func=AF.Exp, scale=lnw[:, oc:oc + 1]),
            reads=[A(iota_rev), A(lnw, 0, oc, 128, oc + 1)], writes=[A(wbk)])
        for n in range(2):
            c0 = oc * 1024 + n * SL
            sc.add("V", lambda e, pp=pp, n=n, c0=c0, wbk=wbk: e.tensor_tensor(
                out=kwT[:, c0:c0 + SL], in0=pp[n][:, :],
                in1=wbk[:, n * SL:(n + 1) * SL], op=ALU.mult),
                reads=[A(pp[n]), A(wbk, 0, n * SL, 128, (n + 1) * SL)],
                writes=[A(kwT, 0, c0, 128, c0 + SL)])
            sc.add("V", lambda e, pp=pp, n=n, oc=oc: e.tensor_scalar_mul(
                ku_t[:, n * SL:(n + 1) * SL], pp[n][:, :], u_sb[:, oc:oc + 1]),
                reads=[A(pp[n]), A(u_sb, 0, oc, 128, oc + 1)],
                writes=[A(ku_t, 0, n * SL, 128, (n + 1) * SL)])
        kr = kr_t[oc % 2]
        sc.add("V", lambda e, oc=oc, kr=kr: e.tensor_mul(
            kr[:, :], ku_t[:, :], rT[:, oc * 1024:(oc + 1) * 1024]),
            reads=[A(ku_t), A(rT, 0, oc * 1024, 128, (oc + 1) * 1024)],
            writes=[A(kr)])
        if oc > 0:
            emit_bonus(oc - 1)

    def emit_bonus(oc):
        kr = kr_t[oc % 2]
        for ck in range(NCK):
            sc.add("T", lambda e, ck=ck, kr=kr: e.matmul(
                ps["m0"][:, ck * 2:(ck + 1) * 2], kr[:, ck * 128:(ck + 1) * 128],
                onesbd2[:, :], start=True, stop=True),
                reads=[A(kr, 0, ck * 128, 128, (ck + 1) * 128), A(onesbd2)],
                writes=[A(ps["m0"], 0, ck * 2, 128, (ck + 1) * 2)],
                tag=f"bon{oc}_{ck}")
        sc.add("V", lambda e, oc=oc: e.tensor_copy(
            out=bon[:, oc * 16:(oc + 1) * 16], in_=ps["m0"][:, 0:16]),
            reads=[A(ps["m0"], 0, 0, 128, 16)],
            writes=[A(bon, 0, oc * 16, 128, (oc + 1) * 16)])

    emit_proj("k", lerpA, k_evac,
              per_oc=lambda oc: quant_tile(16 + oc))
    emit_bonus(NCH - 1)
    sc.add("G", coll1a, reads=[A(guq_c)], writes=[A(guq_g)], tag="coll1a")
    sc.add("G", coll1b, reads=[A(pjq_c)], writes=[A(pjq_g)], tag="coll1b")


    emit_lerp("v", lerpB)
    if tapd("xv", [128, NCH * 1024], BF16) is not None:
        dma("S", taps["xv"], lerpB[:, :], in_acc=[A(lerpB)])

    def vg_evac(kind):
        dst = v_tok if kind == "v" else g_tok
        fn = AF.Copy if kind == "v" else AF.Silu

        def transposes(oc):
            vt = vgT_t[oc % 2]
            for tc in range(NTC):
                pt = ps["t0"] if tc % 2 == 0 else ps["t1"]
                sc.add("T", lambda e, tc=tc, pt=pt, vt=vt: e.transpose(
                    pt[:, 0:128], vt[:, tc * 128:(tc + 1) * 128], id_bf[:, :]),
                    reads=[A(vt, 0, tc * 128, 128, (tc + 1) * 128), A(id_bf)],
                    writes=[A(pt, 0, 0, 128, 128)], tag=f"{kind}tr{oc}_{tc}")
                c0 = tc * D + oc * 128
                sc.add("V", lambda e, pt=pt, c0=c0: e.tensor_copy(
                    out=dst[:, c0:c0 + 128], in_=pt[:, 0:128]),
                    reads=[A(pt, 0, 0, 128, 128)],
                    writes=[A(dst, 0, c0, 128, c0 + 128)])

        def evac(oc, pp):
            vt = vgT_t[oc % 2]
            for n in range(2):
                sc.add("A", lambda e, pp=pp, n=n, vt=vt: e.activation(
                    out=vt[:, n * SL:(n + 1) * SL], in_=pp[n][:, :], func=fn),
                    reads=[A(pp[n])],
                    writes=[A(vt, 0, n * SL, 128, (n + 1) * SL)])
            if oc > 0:
                transposes(oc - 1)

        return evac, transposes

    v_ev, v_fl = vg_evac("v")
    emit_proj("v", lerpB, v_ev)
    v_fl(NCH - 1)
    emit_lerp("g", lerpB)
    g_ev, g_fl = vg_evac("g")
    emit_proj("g", lerpB, g_ev)
    g_fl(NCH - 1)

    dma("S", gamb[:, :], gam_d, out_acc=[A(gamb)], tag="ld_gam")
    dma("S", betb[:, :], bet_d, out_acc=[A(betb)], tag="ld_bet")

    for hp in range(HP):
        pA_ = ps["m0"]
        pB_ = ps["m1"]
        for ck in range(NCK):
            pt = ps["t0"] if ck % 2 == 0 else ps["t1"]
            kt = kwtok[ck % 2]
            c0 = hp * 1024 + ck * 128
            sc.add("T", lambda e, pt=pt, c0=c0: e.transpose(
                pt[:, 0:128], kwT[:, c0:c0 + 128], id_bf[:, :]),
                reads=[A(kwT, 0, c0, 128, c0 + 128), A(id_bf)],
                writes=[A(pt, 0, 0, 128, 128)], tag=f"sigtr{hp}_{ck}")
            sc.add("A", lambda e, pt=pt, kt=kt: e.activation(
                out=kt[:, :], in_=pt[:, 0:128], func=AF.Copy),
                reads=[A(pt, 0, 0, 128, 128)], writes=[A(kt)])
            vc0 = ck * D + hp * 128
            st, sp = ck == 0, ck == NCK - 1
            sc.add("T", lambda e, kt=kt, vc0=vc0, st=st, sp=sp, pA_=pA_: e.matmul(
                pA_[0:64, 0:64], kt[:, 0:64], v_tok[:, vc0:vc0 + 64],
                start=st, stop=sp),
                reads=[A(kt, 0, 0, 128, 64), A(v_tok, 0, vc0, 128, vc0 + 64)],
                writes=[A(pA_, 0, 0, 64, 64)], tag=f"sigA{hp}_{ck}")
            sc.add("T", lambda e, kt=kt, vc0=vc0, st=st, sp=sp, pB_=pB_: e.matmul(
                pB_[64:128, 64:128], kt[:, 64:128], v_tok[:, vc0 + 64:vc0 + 128],
                start=st, stop=sp),
                reads=[A(kt, 0, 64, 128, 128), A(v_tok, 0, vc0 + 64, 128, vc0 + 128)],
                writes=[A(pB_, 64, 64, 128, 128)], tag=f"sigB{hp}_{ck}")
        sc.add("V", lambda e, hp=hp, pA_=pA_: e.tensor_copy(
            out=sig_sb[0:64, hp * 64:(hp + 1) * 64], in_=pA_[0:64, 0:64]),
            reads=[A(pA_, 0, 0, 64, 64)],
            writes=[A(sig_sb, 0, hp * 64, 64, (hp + 1) * 64)])
        sc.add("V", lambda e, hp=hp, pB_=pB_: e.tensor_copy(
            out=sig_sb[64:128, hp * 64:(hp + 1) * 64], in_=pB_[64:128, 64:128]),
            reads=[A(pB_, 64, 64, 128, 128)],
            writes=[A(sig_sb, 64, hp * 64, 128, (hp + 1) * 64)])
    dma("S", sig_c.ap(), sig_sb[:, :], in_acc=[A(sig_sb)], out_acc=[A(sig_c)],
        tag="sig_store")

    def coll2(e):
        return e.collective_compute(
            "AllGather", ALU.bypass, replica_groups=[list(range(NCORES))],
            ins=[sig_c.ap().opt()], outs=[sig_g.ap().opt()])
    sc.add("G", coll2, reads=[A(sig_c)], writes=[A(sig_g)], tag="coll2")

    if tapd("rT", [128, HP * 1024], BF16) is not None:
        dma("S", taps["rT"], rT[:, :], in_acc=[A(rT)])
    if tapd("kwT", [128, HP * 1024], BF16) is not None:
        dma("S", taps["kwT"], kwT[:, :], in_acc=[A(kwT)])
    if tapd("v_tok", [128, NTC * D], BF16) is not None:
        dma("S", taps["v_tok"], v_tok[:, :], in_acc=[A(v_tok)])
    if tapd("g_tok", [128, NTC * D], BF16) is not None:
        dma("S", taps["g_tok"], g_tok[:, :], in_acc=[A(g_tok)])
    if tapd("bon", [128, 256]) is not None:
        dma("S", taps["bon"], bon[:, :], in_acc=[A(bon)])
    if tapd("sig", [128, HP * 64]) is not None:
        dma("S", taps["sig"], sig_sb[:, :], in_acc=[A(sig_sb)])

    dma("S", iota_fwd[:, :], iota_f_d, out_acc=[A(iota_fwd)], tag="ld_iota_f")

    sc.add("V", lambda e: e.memset(S_f32[:, :], 0.0), writes=[A(S_f32)])
    sc.add("V", lambda e: e.memset(sig_sb[:, :], 0.0), writes=[A(sig_sb)])
    for r in range(NCORES):
        dma("S", slab_t[:, :], sig_g.ap()[r], in_acc=[A(sig_g)],
            out_acc=[A(slab_t)], tag=f"slab{r}")
        sc.add("V", lambda e, r=r: e.scalar_tensor_tensor(
            out=sig_sb[:, :], in0=slab_t[:, :], scalar=pmask[:, r:r + 1],
            in1=sig_sb[:, :], op0=ALU.mult, op1=ALU.add),
            reads=[A(slab_t), A(pmask), A(sig_sb)], writes=[A(sig_sb)])
    for hp in range(HP):
        sc.add("V", lambda e, hp=hp: e.tensor_copy(
            out=S_f32[0:64, hp * 128:hp * 128 + 64],
            in_=sig_sb[0:64, hp * 64:(hp + 1) * 64]),
            reads=[A(sig_sb, 0, hp * 64, 64, (hp + 1) * 64)],
            writes=[A(S_f32, 0, hp * 128, 64, hp * 128 + 64)])
        sc.add("V", lambda e, hp=hp: e.tensor_copy(
            out=S_f32[64:128, hp * 128 + 64:(hp + 1) * 128],
            in_=sig_sb[64:128, hp * 64:(hp + 1) * 64]),
            reads=[A(sig_sb, 64, hp * 64, 128, (hp + 1) * 64)],
            writes=[A(S_f32, 64, hp * 128 + 64, 128, (hp + 1) * 128)])
    sc.add("V", lambda e: e.memset(rbd[0][:, :], 0.0), writes=[A(rbd[0])])
    sc.add("V", lambda e: e.memset(rbd[1][:, :], 0.0), writes=[A(rbd[1])])
    if tapd("cpref", [128, HP * 128]) is not None:
        dma("S", taps["cpref"], S_f32[:, :], in_acc=[A(S_f32)])

    def scan_front(it):
        ck, hp = it
        p = (ck * HP + hp) % 2
        c0 = hp * 1024 + ck * 128
        sc.add("A", lambda e, p=p, ck=ck, hp=hp: e.activation(
            out=wfor_t[p][:, :], in_=iota_fwd[:, ck * 128:(ck + 1) * 128],
            func=AF.Exp, scale=lnw[:, hp:hp + 1]),
            reads=[A(iota_fwd, 0, ck * 128, 128, (ck + 1) * 128),
                   A(lnw, 0, hp, 128, hp + 1)],
            writes=[A(wfor_t[p])])
        sc.add("V", lambda e, p=p, c0=c0: e.tensor_mul(
            rw_t[p][:, :], rT[:, c0:c0 + 128], wfor_t[p][:, :]),
            reads=[A(rT, 0, c0, 128, c0 + 128), A(wfor_t[p])],
            writes=[A(rw_t[p])])
        sc.add("V", lambda e, p=p, c0=c0: e.tensor_copy(
            out=rbd[p][0:64, 0:128], in_=rT[0:64, c0:c0 + 128]),
            reads=[A(rT, 0, c0, 64, c0 + 128)], writes=[A(rbd[p], 0, 0, 64, 128)])
        sc.add("V", lambda e, p=p, c0=c0: e.tensor_copy(
            out=rbd[p][64:128, 128:256], in_=rT[64:128, c0:c0 + 128]),
            reads=[A(rT, 64, c0, 128, c0 + 128)],
            writes=[A(rbd[p], 64, 128, 128, 256)])
        sc.add("V", lambda e, p=p, hp=hp: e.tensor_copy(
            out=S_bf[p][:, :], in_=S_f32[:, hp * 128:(hp + 1) * 128]),
            reads=[A(S_f32, 0, hp * 128, 128, (hp + 1) * 128)],
            writes=[A(S_bf[p])])
        pt = ps["t0"] if p == 0 else ps["t1"]
        sc.add("T", lambda e, pt=pt, c0=c0: e.transpose(
            pt[:, 0:128], kwT[:, c0:c0 + 128], id_bf[:, :]),
            reads=[A(kwT, 0, c0, 128, c0 + 128), A(id_bf)],
            writes=[A(pt, 0, 0, 128, 128)], tag=f"ctr{ck}_{hp}")
        pa = ps["a0"] if p == 0 else ps["a1"]
        sc.add("T", lambda e, pa=pa, c0=c0, p=p: e.matmul(
            pa[:, 0:256], kwT[:, c0:c0 + 128], rbd[p][:, :],
            start=True, stop=True),
            reads=[A(kwT, 0, c0, 128, c0 + 128), A(rbd[p])],
            writes=[A(pa, 0, 0, 128, 256)], tag=f"catt{ck}_{hp}")
        sc.add("V", lambda e, pa=pa, p=p: e.tensor_tensor(
            out=attT[p][:, :], in0=pa[:, 0:256], in1=mask2[:, :], op=ALU.mult),
            reads=[A(pa, 0, 0, 128, 256), A(mask2)], writes=[A(attT[p])])
        sc.add("V", lambda e, pt=pt, p=p: e.tensor_copy(
            out=kwtok[p][:, :], in_=pt[:, 0:128]),
            reads=[A(pt, 0, 0, 128, 128)], writes=[A(kwtok[p])])

    def scan_back(it):
        ck, hp = it
        p = (ck * HP + hp) % 2
        c0 = hp * 1024 + ck * 128
        vc0 = ck * D + hp * 128
        po = ps["b0"] if p == 0 else ps["b1"]
        sc.add("T", lambda e, po=po, p=p, hp=hp: e.matmul(
            po[:, 0:128], rw_t[p][:, :], wkvbd[:, hp * 128:(hp + 1) * 128],
            start=True, stop=False),
            reads=[A(rw_t[p]), A(wkvbd, 0, hp * 128, 128, (hp + 1) * 128)],
            writes=[A(po, 0, 0, 128, 128)], tag=f"cbase{ck}_{hp}")
        sc.add("T", lambda e, po=po, p=p, c0=c0: e.matmul(
            po[:, 0:128], rT[:, c0:c0 + 128], S_bf[p][:, :],
            start=False, stop=False),
            reads=[A(rT, 0, c0, 128, c0 + 128), A(S_bf[p])],
            writes=[A(po, 0, 0, 128, 128)])
        sc.add("T", lambda e, po=po, p=p, vc0=vc0: e.matmul(
            po[:, 0:64], attT[p][:, 0:128], v_tok[:, vc0:vc0 + 64],
            start=False, stop=False),
            reads=[A(attT[p], 0, 0, 128, 128), A(v_tok, 0, vc0, 128, vc0 + 64)],
            writes=[A(po, 0, 0, 128, 64)])
        sc.add("T", lambda e, po=po, p=p, vc0=vc0: e.matmul(
            po[:, 64:128], attT[p][:, 128:256], v_tok[:, vc0 + 64:vc0 + 128],
            start=False, stop=True),
            reads=[A(attT[p], 0, 128, 128, 256),
                   A(v_tok, 0, vc0 + 64, 128, vc0 + 128)],
            writes=[A(po, 0, 64, 128, 128)])
        pm = ps["m0"] if p == 0 else ps["m1"]
        sc.add("T", lambda e, pm=pm, p=p, vc0=vc0: e.matmul(
            pm[0:64, 0:64], kwtok[p][:, 0:64], v_tok[:, vc0:vc0 + 64],
            start=True, stop=True),
            reads=[A(kwtok[p], 0, 0, 128, 64), A(v_tok, 0, vc0, 128, vc0 + 64)],
            writes=[A(pm, 0, 0, 64, 64)], tag=f"csup{ck}_{hp}")
        sc.add("T", lambda e, pm=pm, p=p, vc0=vc0: e.matmul(
            pm[64:128, 64:128], kwtok[p][:, 64:128],
            v_tok[:, vc0 + 64:vc0 + 128], start=True, stop=True),
            reads=[A(kwtok[p], 0, 64, 128, 128),
                   A(v_tok, 0, vc0 + 64, 128, vc0 + 128)],
            writes=[A(pm, 64, 64, 128, 128)])
        wt = wkv_t[ck % 2]
        for ee in range(2):
            bc = hp * 16 + ck * 2 + ee
            wc0 = hp * 128 + ee * 64
            sc.add("V", lambda e, po=po, vc0=vc0, ee=ee, bc=bc, wt=wt, wc0=wc0:
                   e.scalar_tensor_tensor(
                       out=wt[:, wc0:wc0 + 64],
                       in0=v_tok[:, vc0 + ee * 64:vc0 + (ee + 1) * 64],
                       scalar=bon[:, bc:bc + 1], in1=po[:, ee * 64:(ee + 1) * 64],
                       op0=ALU.mult, op1=ALU.add),
                   reads=[A(v_tok, 0, vc0 + ee * 64, 128, vc0 + (ee + 1) * 64),
                          A(bon, 0, bc, 128, bc + 1),
                          A(po, 0, ee * 64, 128, (ee + 1) * 64)],
                   writes=[A(wt, 0, wc0, 128, wc0 + 64)])
        sc.add("V", lambda e, pm=pm, hp=hp: e.tensor_add(
            S_f32[0:64, hp * 128:hp * 128 + 64],
            S_f32[0:64, hp * 128:hp * 128 + 64], pm[0:64, 0:64]),
            reads=[A(pm, 0, 0, 64, 64),
                   A(S_f32, 0, hp * 128, 64, hp * 128 + 64)],
            writes=[A(S_f32, 0, hp * 128, 64, hp * 128 + 64)])
        sc.add("V", lambda e, pm=pm, hp=hp: e.tensor_add(
            S_f32[64:128, hp * 128 + 64:(hp + 1) * 128],
            S_f32[64:128, hp * 128 + 64:(hp + 1) * 128], pm[64:128, 64:128]),
            reads=[A(pm, 64, 64, 128, 128),
                   A(S_f32, 64, hp * 128 + 64, 128, (hp + 1) * 128)],
            writes=[A(S_f32, 64, hp * 128 + 64, 128, (hp + 1) * 128)])

    def groupnorm_gate(ck):
        wt = wkv_t[ck % 2]
        g3 = lambda x: x.ap().rearrange("p (g e) -> p g e", e=64)
        bc = lambda x, n: x.ap()[:, 0:n].unsqueeze(2).broadcast_to([128, n, 64])
        sc.add("V", lambda e, wt=wt: e.tensor_reduce(
            gstat[:, 0:32], g3(wt), AX.X, ALU.add),
            reads=[A(wt)], writes=[A(gstat, 0, 0, 128, 32)], sync=True, sync_before=True)
        sc.add("V", lambda e: e.tensor_scalar_mul(
            gstat[:, 32:64], gstat[:, 0:32], 1.0 / 64),
            reads=[A(gstat, 0, 0, 128, 32)], writes=[A(gstat, 0, 32, 128, 64)], sync=True, sync_before=True)
        sc.add("V", lambda e, wt=wt: e.tensor_tensor(
            out=yc_t.ap().rearrange("p (g e) -> p g e", e=64), in0=g3(wt),
            in1=gstat.ap()[:, 32:64].unsqueeze(2).broadcast_to([128, 32, 64]),
            op=ALU.subtract),
            reads=[A(wt), A(gstat, 0, 32, 128, 64)], writes=[A(yc_t)], sync=True, sync_before=True)
        sc.add("A", lambda e: e.activation(out=sqc_t[:, :], in_=yc_t[:, :],
                                           func=AF.Square),
            reads=[A(yc_t)], writes=[A(sqc_t)], sync=True, sync_before=True)
        sc.add("V", lambda e: e.tensor_reduce(
            gstat[:, 0:32], g3(sqc_t), AX.X, ALU.add),
            reads=[A(sqc_t)], writes=[A(gstat, 0, 0, 128, 32)], sync=True, sync_before=True)
        sc.add("V", lambda e: e.tensor_scalar(
            out=gstat[:, 0:32], in0=gstat[:, 0:32], scalar1=1.0 / 64,
            scalar2=EPS_GN, op0=ALU.mult, op1=ALU.add),
            reads=[A(gstat, 0, 0, 128, 32)], writes=[A(gstat, 0, 0, 128, 32)], sync=True, sync_before=True)
        sc.add("A", lambda e: e.activation(out=gstat[:, 64:96], in_=gstat[:, 0:32],
                                           func=AF.Ln),
            reads=[A(gstat, 0, 0, 128, 32)], writes=[A(gstat, 0, 64, 128, 96)],
            sync=True, sync_before=True)
        sc.add("A", lambda e: e.activation(out=gstat[:, 0:32], in_=gstat[:, 64:96],
                                           func=AF.Exp, scale=-0.5),
            reads=[A(gstat, 0, 64, 128, 96)], writes=[A(gstat, 0, 0, 128, 32)],
            sync=True, sync_before=True)
        sc.add("V", lambda e: e.tensor_tensor(
            out=yc_t.ap().rearrange("p (g e) -> p g e", e=64),
            in0=yc_t.ap().rearrange("p (g e) -> p g e", e=64),
            in1=gstat.ap()[:, 0:32].unsqueeze(2).broadcast_to([128, 32, 64]),
            op=ALU.mult),
            reads=[A(yc_t), A(gstat, 0, 0, 128, 32)], writes=[A(yc_t)], sync=True, sync_before=True)
        sc.add("V", lambda e: e.tensor_mul(yc_t[:, :], yc_t[:, :], gamb[:, :]),
            reads=[A(yc_t), A(gamb)], writes=[A(yc_t)], sync=True, sync_before=True)
        sc.add("V", lambda e: e.tensor_add(yc_t[:, :], yc_t[:, :], betb[:, :]),
            reads=[A(yc_t), A(betb)], writes=[A(yc_t)], sync=True, sync_before=True)
        c0 = ck * D
        sc.add("V", lambda e, c0=c0: e.tensor_mul(
            g_tok[:, c0:c0 + D], yc_t[:, :], g_tok[:, c0:c0 + D]),
            reads=[A(yc_t), A(g_tok, 0, c0, 128, c0 + D)],
            writes=[A(g_tok, 0, c0, 128, c0 + D)], sync=True, sync_before=True)

    tw = tapd("wkv8", [NCK, 128, D], BF16)
    def maybe_tap_wkv(ck):
        if tw is not None:
            dma("S", tw[ck], wkv_t[ck % 2][:, :], in_acc=[A(wkv_t[ck % 2])])

    its = [(ck, hp) for ck in range(NCK) for hp in range(HP)]
    prev = None
    for it in its:
        scan_front(it)
        if prev is not None:
            scan_back(prev)
            if prev[1] == HP - 1:
                maybe_tap_wkv(prev[0])
                groupnorm_gate(prev[0])
        prev = it
    scan_back(prev)
    maybe_tap_wkv(prev[0])
    groupnorm_gate(prev[0])
    if tapd("sfin", [128, HP * 128]) is not None:
        dma("S", taps["sfin"], S_f32[:, :], in_acc=[A(S_f32)])
    if tapd("gnyc", [128, D]) is not None:
        dma("S", taps["gnyc"], yc_t[:, :], in_acc=[A(yc_t)])
    if tapd("gnstat", [128, 96]) is not None:
        dma("S", taps["gnstat"], gstat[:, :], in_acc=[A(gstat)])

    if tapd("yg", [128, NTC * D], BF16) is not None:
        dma("S", taps["yg"], g_tok[:, :], in_acc=[A(g_tok)])

    for tc in range(NTC):
        for cc in range(NCH):
            pt = ps["t0"] if (tc * NCH + cc) % 2 == 0 else ps["t1"]
            c0 = tc * D + cc * 128
            d0 = cc * 1024 + tc * 128
            sc.add("T", lambda e, pt=pt, c0=c0: e.transpose(
                pt[:, 0:128], g_tok[:, c0:c0 + 128], id_bf[:, :]),
                reads=[A(g_tok, 0, c0, 128, c0 + 128), A(id_bf)],
                writes=[A(pt, 0, 0, 128, 128)], tag=f"ygtr{tc}_{cc}")
            sc.add("V", lambda e, pt=pt, d0=d0: e.tensor_copy(
                out=ygT[:, d0:d0 + 128], in_=pt[:, 0:128]),
                reads=[A(pt, 0, 0, 128, 128)],
                writes=[A(ygT, 0, d0, 128, d0 + 128)])

    for tc in range(NTC):
        dma("S", x2tok[:, tc * D:(tc + 1) * D],
            x_tok_d[tc * 128:(tc + 1) * 128, :],
            out_acc=[A(x2tok, 0, tc * D, 128, (tc + 1) * D)], tag=f"xtok{tc}")

    for oc in range(NCH):
        wf = wstf32[oc % 2]
        wb = wstb[oc % 2]
        dma("S", wf[:, :], w_blk["o"][oc].rearrange("p a b -> p (a b)"),
            out_acc=[A(wf)], tag=f"wo{oc}")
        sc.add("A", lambda e, wf=wf, wb=wb: e.activation(
            out=wb[:, :], in_=wf[:, :], func=AF.Copy),
            reads=[A(wf)], writes=[A(wb)], tag=f"co{oc}")
        pp = [ps["a0"], ps["a1"]] if oc % 2 == 0 else [ps["b0"], ps["b1"]]
        for cc in range(NCH):
            st, sp = cc == 0, cc == NCH - 1
            for n in range(2):
                sc.add("T", lambda e, wb=wb, cc=cc, n=n, st=st, sp=sp, pp=pp: e.matmul(
                    pp[n][:, :], wb[:, cc * 128:(cc + 1) * 128],
                    ygT[:, cc * 1024 + n * SL: cc * 1024 + (n + 1) * SL],
                    start=st, stop=sp),
                    reads=[A(wb, 0, cc * 128, 128, (cc + 1) * 128),
                           A(ygT, 0, cc * 1024 + n * SL, 128, cc * 1024 + (n + 1) * SL)],
                    writes=[A(pp[n])], tag=f"womm{oc}_{cc}_{n}")
        for n in range(2):
            ao = ao_tmp[n]
            sc.add("V", lambda e, pp=pp, n=n, ao=ao: e.tensor_copy(
                out=ao[:, :], in_=pp[n][:, :]),
                reads=[A(pp[n])], writes=[A(ao)])
            for ts in range(4):
                tc = n * 4 + ts
                pm = ps["m0"] if tc % 2 == 0 else ps["m1"]
                sc.add("T", lambda e, ao=ao, ts=ts, pm=pm: e.transpose(
                    pm[:, 0:128], ao[:, ts * 128:(ts + 1) * 128], id_f32[:, :]),
                    reads=[A(ao, 0, ts * 128, 128, (ts + 1) * 128), A(id_f32)],
                    writes=[A(pm, 0, 0, 128, 128)], tag=f"aotr{oc}_{tc}")
                xc0 = tc * D + oc * 128
                sc.add("V", lambda e, pm=pm, xc0=xc0: e.tensor_add(
                    x2tok[:, xc0:xc0 + 128], x2tok[:, xc0:xc0 + 128],
                    pm[:, 0:128]),
                    reads=[A(pm, 0, 0, 128, 128),
                           A(x2tok, 0, xc0, 128, xc0 + 128)],
                    writes=[A(x2tok, 0, xc0, 128, xc0 + 128)])

    if tapd("x1", [128, NTC * D]) is not None:
        dma("S", taps["x1"], x2tok[:, :], in_acc=[A(x2tok)])

    def emit_ln(tc, src_col0, write_bf16):
        c0 = src_col0
        sc.add("V", lambda e, c0=c0: e.tensor_reduce(
            statln[:, 0:1], x2tok[:, c0:c0 + D], AX.X, ALU.add),
            reads=[A(x2tok, 0, c0, 128, c0 + D)],
            writes=[A(statln, 0, 0, 128, 1)], sync=True, sync_before=True)
        sc.add("V", lambda e: e.tensor_scalar_mul(
            statln[:, 1:2], statln[:, 0:1], 1.0 / D),
            reads=[A(statln, 0, 0, 128, 1)], writes=[A(statln, 0, 1, 128, 2)])
        sc.add("A", lambda e, c0=c0: e.activation(
            out=sqd[:, :], in_=x2tok[:, c0:c0 + D], func=AF.Square),
            reads=[A(x2tok, 0, c0, 128, c0 + D)], writes=[A(sqd)])
        sc.add("V", lambda e: e.tensor_reduce(
            statln[:, 2:3], sqd[:, :], AX.X, ALU.add),
            reads=[A(sqd)], writes=[A(statln, 0, 2, 128, 3)], sync=True, sync_before=True)
        sc.add("V", lambda e: e.tensor_scalar_mul(
            statln[:, 2:3], statln[:, 2:3], 1.0 / D),
            reads=[A(statln, 0, 2, 128, 3)], writes=[A(statln, 0, 2, 128, 3)])
        sc.add("V", lambda e: e.tensor_mul(
            statln[:, 3:4], statln[:, 1:2], statln[:, 1:2]),
            reads=[A(statln, 0, 1, 128, 2)], writes=[A(statln, 0, 3, 128, 4)])
        sc.add("V", lambda e: e.tensor_sub(
            statln[:, 2:3], statln[:, 2:3], statln[:, 3:4]),
            reads=[A(statln, 0, 2, 128, 4)], writes=[A(statln, 0, 2, 128, 3)])
        sc.add("V", lambda e: e.tensor_scalar_add(
            statln[:, 2:3], statln[:, 2:3], EPS_LN),
            reads=[A(statln, 0, 2, 128, 3)], writes=[A(statln, 0, 2, 128, 3)])
        sc.add("A", lambda e: e.activation(
            out=statln[:, 4:5], in_=statln[:, 2:3], func=AF.Ln),
            reads=[A(statln, 0, 2, 128, 3)], writes=[A(statln, 0, 4, 128, 5)],
            sync=True, sync_before=True)
        sc.add("A", lambda e: e.activation(
            out=statln[:, 5:6], in_=statln[:, 4:5], func=AF.Exp, scale=-0.5),
            reads=[A(statln, 0, 4, 128, 5)], writes=[A(statln, 0, 5, 128, 6)],
            sync=True, sync_before=True)
        sc.add("V", lambda e, c0=c0: e.tensor_scalar(
            out=x2tok[:, c0:c0 + D], in0=x2tok[:, c0:c0 + D],
            scalar1=statln[:, 1:2], scalar2=statln[:, 5:6],
            op0=ALU.subtract, op1=ALU.mult),
            reads=[A(x2tok, 0, c0, 128, c0 + D), A(statln, 0, 1, 128, 6)],
            writes=[A(x2tok, 0, c0, 128, c0 + D)])
        if write_bf16:
            sc.add("V", lambda e, c0=c0: e.tensor_copy(
                out=x2btok[:, :], in_=x2tok[:, c0:c0 + D]),
                reads=[A(x2tok, 0, c0, 128, c0 + D)], writes=[A(x2btok)])

    for tc in range(NTC):
        emit_ln(tc, tc * D, True)
        for ic in range(NCH):
            pt = ps["t0"] if (tc * NCH + ic) % 2 == 0 else ps["t1"]
            sc.add("T", lambda e, pt=pt, ic=ic: e.transpose(
                pt[:, 0:128], x2btok[:, ic * 128:(ic + 1) * 128], id_bf[:, :]),
                reads=[A(x2btok, 0, ic * 128, 128, (ic + 1) * 128), A(id_bf)],
                writes=[A(pt, 0, 0, 128, 128)], tag=f"x2tr{tc}_{ic}")
            d0 = ic * 1024 + tc * 128
            sc.add("V", lambda e, pt=pt, d0=d0: e.tensor_copy(
                out=x2bf[:, d0:d0 + 128], in_=pt[:, 0:128]),
                reads=[A(pt, 0, 0, 128, 128)],
                writes=[A(x2bf, 0, d0, 128, d0 + 128)])

    if tapd("x2bf", [128, NCH * 1024], BF16) is not None:
        dma("S", taps["x2bf"], x2bf[:, :], in_acc=[A(x2bf)])

    def stat_mm(sl, hcp):
        h2 = h2p[hcp % 2]
        st, sp = hcp == 0, hcp == 63
        sc.add("T", lambda e, st=st, sp=sp, h2=h2: e.matmul(
            ps["m0"][0:1, 0:SL], onescol[:, :], h2[:, :], start=st, stop=sp),
            reads=[A(onescol), A(h2)],
            writes=[A(ps["m0"], 0, 0, 1, SL)], tag=f"hsq{sl}_{hcp}")

    for sl in range(2):
        for hcp in range(64):
            gblk = egu[(2 * hcp) % 4]
            ublk = egu[(2 * hcp + 1) % 4]
            gh, uh = hcp, 64 + hcp
            dma("S", gblk[:, :], guq_g.ap()[gh // 16, gh % 16],
                in_acc=[A(guq_g)], out_acc=[A(gblk)], tag=f"gu_g{sl}_{hcp}")
            dma("S", ublk[:, :], guq_g.ap()[uh // 16, uh % 16],
                in_acc=[A(guq_g)], out_acc=[A(ublk)], tag=f"gu_u{sl}_{hcp}")
            pg = ps["a0"] if hcp % 2 == 0 else ps["a1"]
            pu = ps["b0"] if hcp % 2 == 0 else ps["b1"]
            for ic in range(NCH):
                st, sp = ic == 0, ic == NCH - 1
                xs0 = ic * 1024 + sl * SL
                sc.add("T", lambda e, gblk=gblk, ic=ic, st=st, sp=sp, pg=pg, xs0=xs0:
                       e.matmul(pg[:, :], gblk[:, ic * 128:(ic + 1) * 128],
                                x2bf[:, xs0:xs0 + SL], start=st, stop=sp),
                       reads=[A(gblk, 0, ic * 128, 128, (ic + 1) * 128),
                              A(x2bf, 0, xs0, 128, xs0 + SL)],
                       writes=[A(pg)], tag=f"gmm{sl}_{hcp}_{ic}")
                sc.add("T", lambda e, ublk=ublk, ic=ic, st=st, sp=sp, pu=pu, xs0=xs0:
                       e.matmul(pu[:, :], ublk[:, ic * 128:(ic + 1) * 128],
                                x2bf[:, xs0:xs0 + SL], start=st, stop=sp),
                       reads=[A(ublk, 0, ic * 128, 128, (ic + 1) * 128),
                              A(x2bf, 0, xs0, 128, xs0 + SL)],
                       writes=[A(pu)], tag=f"umm{sl}_{hcp}_{ic}")
            sg = sgl_t[hcp % 2]
            h2 = h2p[hcp % 2]
            sc.add("A", lambda e, pg=pg, sg=sg: e.activation(
                out=sg[:, :], in_=pg[:, :], func=AF.Silu),
                reads=[A(pg)], writes=[A(sg)])
            hdst = h_lo if hcp < 32 else h_hi
            hc0 = (hcp % 32) * SL
            sc.add("V", lambda e, pu=pu, hdst=hdst, hc0=hc0, sg=sg: e.tensor_tensor(
                out=hdst[:, hc0:hc0 + SL], in0=sg[:, :], in1=pu[:, :],
                op=ALU.mult),
                reads=[A(sg), A(pu)],
                writes=[A(hdst, 0, hc0, 128, hc0 + SL)])
            sc.add("V", lambda e, hdst=hdst, hc0=hc0, h2=h2: e.tensor_mul(
                h2[:, :], hdst[:, hc0:hc0 + SL], hdst[:, hc0:hc0 + SL]),
                reads=[A(hdst, 0, hc0, 128, hc0 + SL)], writes=[A(h2)])
            if hcp > 0:
                stat_mm(sl, hcp - 1)
        stat_mm(sl, 63)
        sc.add("V", lambda e: e.tensor_copy(out=inv_row[:, :], in_=ps["m0"][0:1, 0:SL]),
            reads=[A(ps["m0"], 0, 0, 1, SL)], writes=[A(inv_row, 0, 0, 1, SL)])
        sc.add("V", lambda e: e.tensor_scalar(
            out=inv_row[:, :], in0=inv_row[:, :], scalar1=1.0 / 8192,
            scalar2=EPS_RMS, op0=ALU.mult, op1=ALU.add),
            reads=[A(inv_row, 0, 0, 1, SL)], writes=[A(inv_row, 0, 0, 1, SL)])
        sc.add("A", lambda e: e.activation(
            out=inv_row[:, :], in_=inv_row[:, :], func=AF.Ln),
            reads=[A(inv_row, 0, 0, 1, SL)], writes=[A(inv_row, 0, 0, 1, SL)],
            sync=True, sync_before=True)
        sc.add("A", lambda e: e.activation(
            out=inv_row[:, :], in_=inv_row[:, :], func=AF.Exp, scale=-0.5),
            reads=[A(inv_row, 0, 0, 1, SL)], writes=[A(inv_row, 0, 0, 1, SL)],
            sync=True, sync_before=True)
        sc.add("T", lambda e: e.matmul(
            ps["m1"][:, 0:SL], onesrow[:, :], inv_row[:, :],
            start=True, stop=True),
            reads=[A(onesrow, 0, 0, 1, 128), A(inv_row, 0, 0, 1, SL)],
            writes=[A(ps["m1"])], tag=f"invb{sl}")
        sc.add("V", lambda e: e.tensor_copy(out=inv_bc[:, :], in_=ps["m1"][:, 0:SL]),
            reads=[A(ps["m1"])], writes=[A(inv_bc)])
        for oc in range(NCH):
            for rr in range(NCORES):
                dma("S", eph[rr // 4][:, (rr % 4) * 1024:(rr % 4 + 1) * 1024],
                    pjq_g.ap()[rr, oc], in_acc=[A(pjq_g)],
                    out_acc=[A(eph[rr // 4], 0, (rr % 4) * 1024, 128,
                               (rr % 4 + 1) * 1024)],
                    tag=f"pj{sl}_{oc}_{rr}")
            pp = ps["a0"] if oc % 2 == 0 else ps["b0"]
            for hc in range(64):
                st, sp = hc == 0, hc == 63
                hsrc = h_lo if hc < 32 else h_hi
                hc0 = (hc % 32) * SL
                eb = eph[hc // 32]
                ec0 = (hc % 32) * 128
                sc.add("T", lambda e, st=st, sp=sp, pp=pp, hsrc=hsrc,
                       hc0=hc0, eb=eb, ec0=ec0: e.matmul(
                           pp[:, :], eb[:, ec0:ec0 + 128],
                           hsrc[:, hc0:hc0 + SL], start=st, stop=sp),
                       reads=[A(eb, 0, ec0, 128, ec0 + 128),
                              A(hsrc, 0, hc0, 128, hc0 + SL)],
                       writes=[A(pp)], tag=f"pmm{sl}_{oc}_{hc}")
            sc.add("V", lambda e, pp=pp: e.tensor_mul(
                fo_t[:, :], pp[:, :], inv_bc[:, :]),
                reads=[A(pp), A(inv_bc)], writes=[A(fo_t)])
            for ts in range(4):
                tc = sl * 4 + ts
                pm = ps["m0"] if ts % 2 == 0 else ps["m1"]
                sc.add("T", lambda e, ts=ts, pm=pm: e.transpose(
                    pm[:, 0:128], fo_t[:, ts * 128:(ts + 1) * 128], id_f32[:, :]),
                    reads=[A(fo_t, 0, ts * 128, 128, (ts + 1) * 128), A(id_f32)],
                    writes=[A(pm, 0, 0, 128, 128)], tag=f"ftr{sl}_{oc}_{ts}")
                xc0 = tc * D + oc * 128
                sc.add("V", lambda e, pm=pm, xc0=xc0: e.tensor_add(
                    x2tok[:, xc0:xc0 + 128], x2tok[:, xc0:xc0 + 128],
                    pm[:, 0:128]),
                    reads=[A(pm, 0, 0, 128, 128), A(x2tok, 0, xc0, 128, xc0 + 128)],
                    writes=[A(x2tok, 0, xc0, 128, xc0 + 128)])
        for ts in range(4):
            tc = sl * 4 + ts
            emit_ln(tc, tc * D, False)
            dma("S", out_d[tc * 128:(tc + 1) * 128, :], x2tok[:, tc * D:(tc + 1) * D],
                in_acc=[A(x2tok, 0, tc * D, 128, (tc + 1) * D)],
                tag=f"out{tc}")

    import contextlib
    with contextlib.ExitStack() as stack:
        sems = {e: stack.enter_context(nc.semaphore(f"sem{e}")) for e in ENGS}
        dsems = {e: [stack.enter_context(nc.semaphore(f"dsem{e}{i}"))
                     for i in range(Sched.NSLOT)] for e in ENGS}
        blk = stack.enter_context(nc.Block())
        finals = sc.finalize(blk, sems, dsems)

        @blk.sync
        def _(sy):
            for s, v in finals["S"]:
                sy.wait_ge(dsems["S"][s], v)
            for s, v in finals["G"]:
                sy.wait_ge(dsems["G"][s], v)

    return nc, taps



def _layoutA(w):
    wr = w.reshape(NCH, 128, NCH, 128)
    return np.ascontiguousarray(np.transpose(wr, (0, 3, 2, 1)))


def _stage_core(ins, cid):
    b, q = cid // 4, cid % 4
    t0 = q * TL
    x = ins["L0_attn_x"]
    state = ins["L0_attn_state"].reshape(D)
    prev = state if q == 0 else x[b, t0 - 1]
    xe = np.concatenate([prev[None, :], x[b, t0:t0 + TL]], axis=0)
    xfT = np.ascontiguousarray(xe.T)
    x_blk = np.ascontiguousarray(xfT.reshape(NCH, 128, TL + 1))
    x_tok = np.ascontiguousarray(x[b, t0:t0 + TL])

    def pp(v):
        return np.ascontiguousarray(v.reshape(NCH, 128).T)

    wkv = ins["L0_attn_wkvstate"]
    wbd = np.zeros((128, HP * 128), np.float32)
    for hp in range(HP):
        wbd[0:64, hp * 128:hp * 128 + 64] = wkv[2 * hp]
        wbd[64:128, hp * 128 + 64:(hp + 1) * 128] = wkv[2 * hp + 1]

    t_loc = np.arange(TL, dtype=np.float32)
    iota_fwd = np.broadcast_to(t0 + t_loc, (128, TL)).astype(np.float32)
    iota_rev = np.broadcast_to(TG - 1 - t0 - t_loc, (128, TL)).astype(np.float32)

    m = (np.arange(128)[:, None] < np.arange(128)[None, :]).astype(np.float32)
    mask2 = np.concatenate([m, m], axis=1).astype(BF)

    obd2 = np.zeros((128, 2), BF)
    obd2[0:64, 0] = BF(1.0)
    obd2[64:128, 1] = BF(1.0)

    pm = np.zeros((8,), np.float32)
    for r in range(8):
        if r // 4 == b and r % 4 < q:
            pm[r] = 1.0
    pmask = np.broadcast_to(pm, (128, 8)).astype(np.float32)

    gu = ins["L0_ffn_w_gate_up"]
    pj = ins["L0_ffn_w_proj"]
    gu_sl = gu[2048 * cid: 2048 * (cid + 1)]
    pj_sl = pj[:, 1024 * cid: 1024 * (cid + 1)]

    d = {
        "x_blk": np.ascontiguousarray(x_blk),
        "x_tok": x_tok,
        "wr_blk": _layoutA(ins["L0_attn_w_r"]),
        "wk_blk": _layoutA(ins["L0_attn_w_k"]),
        "wv_blk": _layoutA(ins["L0_attn_w_v"]),
        "wg_blk": _layoutA(ins["L0_attn_w_g"]),
        "wo_blk": _layoutA(ins["L0_attn_w_o"]),
        "guq_in": np.ascontiguousarray(gu_sl.reshape(16, 128, D)),
        "pjq_in": np.ascontiguousarray(pj_sl.reshape(16, 128, 1024)),
        "tmk_pp": pp(ins["L0_attn_time_mix_k"].reshape(D)),
        "tmv_pp": pp(ins["L0_attn_time_mix_v"].reshape(D)),
        "tmr_pp": pp(ins["L0_attn_time_mix_r"].reshape(D)),
        "tmg_pp": pp(ins["L0_attn_time_mix_g"].reshape(D)),
        "decay_pp": pp(ins["L0_attn_time_decay"].reshape(D)),
        "u_pp": pp(ins["L0_attn_time_faaaa"].reshape(D)),
        "wkvbd": wbd,
        "gam_rep": np.broadcast_to(ins["L0_attn_ln_w"], (128, D)).astype(np.float32),
        "bet_rep": np.broadcast_to(ins["L0_attn_ln_b"], (128, D)).astype(np.float32),
        "iota_fwd": iota_fwd,
        "iota_rev": iota_rev,
        "mask2": mask2,
        "id_bf": np.eye(128, dtype=BF),
        "id_f32": np.eye(128, dtype=np.float32),
        "ones_bd2": obd2,
        "ones_col": np.ones((128, 1), BF),
        "ones_row": np.ones((1, 128), np.float32),
        "pmask_pp": pmask,
    }
    return {k: np.ascontiguousarray(v) for k, v in d.items()}


def kernel(**inputs):
    nc, _ = build_program()
    in_maps = [_stage_core(inputs, cid) for cid in range(NCORES)]
    res = run_bass_kernel_spmd(nc, in_maps, core_ids=list(range(NCORES)))
    out = np.empty((B, TG, D), np.float32)
    for cid in range(NCORES):
        b, q = cid // 4, cid % 4
        out[b, q * TL:(q + 1) * TL] = res.results[cid]["out"]
    return out
